# version 27
# baseline (speedup 1.0000x reference)
import numpy as np
import concourse.bass as bass
import concourse.mybir as mybir
from concourse.bass_utils import run_bass_kernel_spmd

F32 = mybir.dt.float32
BF16 = mybir.dt.bfloat16
I8 = mybir.dt.int8
AF = mybir.ActivationFunctionType
ALU = mybir.AluOpType

EPS = 1e-5
S127 = 1.0 / 127.0
EPS2 = EPS * 127.0 * 127.0

D = 1024
H1 = 4096
H2 = 3072
KC = 8
M1 = 32
M2 = 24
T = 256
NT = 8
TOK = T * NT
NROW = 6


class _Buf:
    __slots__ = ("name", "arena", "lo", "hi", "writer", "readers")

    def __init__(self, name, arena=None, lo=0, hi=0):
        self.name, self.arena, self.lo, self.hi = name, arena, lo, hi
        self.writer = None
        self.readers = []


class _Op:
    __slots__ = ("eng", "fn", "needs_inc", "sem_val", "deps", "waits",
                 "is_dma", "is_drain", "tag", "seq", "sem_key")

    def __init__(self, eng, fn, is_dma, tag, seq):
        self.eng, self.fn, self.is_dma, self.tag, self.seq = eng, fn, is_dma, tag, seq
        self.needs_inc = is_dma
        self.sem_val = None
        self.deps = []
        self.waits = []
        self.is_drain = False
        self.sem_key = eng


class _Sched:
    ENGS = ("pe", "v", "s", "g", "q")

    def __init__(self, nc):
        self.nc = nc
        self.sems = {e: nc.alloc_semaphore(f"sem_{e}") for e in self.ENGS}
        self.ops = {e: [] for e in self.ENGS}
        self.bufs = []
        self.last_barrier = {e: -1 for e in self.ENGS}
        self.last_dma = {}

    def _dma_sem(self, key):
        if key not in self.sems:
            self.sems[key] = self.nc.alloc_semaphore(f"sem_{key}")
        return key

    def buf(self, name, arena=None, lo=0, hi=0):
        b = _Buf(name, arena, lo, hi)
        self.bufs.append(b)
        return b

    def _partners(self, b):
        if b.arena is None:
            return ()
        return [o for o in self.bufs
                if o is not b and o.arena == b.arena and o.lo < b.hi and o.hi > b.lo]

    def op(self, eng, fn, reads=(), writes=(), tag="", inc=False):
        deps = []

        def add(d):
            if d is not None and d not in deps:
                deps.append(d)

        for b in reads:
            add(b.writer)
            for p in self._partners(b):
                add(p.writer)
        for b in writes:
            add(b.writer)
            for r in b.readers:
                add(r)
            for p in self._partners(b):
                add(p.writer)
                for r in p.readers:
                    add(r)
        o = _Op(eng, fn, eng in ("g", "q"), tag, len(self.ops[eng]))
        if o.is_dma:
            key = "dma_" + (writes[0].name if writes else "out")
            o.sem_key = self._dma_sem(key)
            prev = self.last_dma.get(key)
            if prev is not None:
                add(prev)
            self.last_dma[key] = o
        if inc:
            o.needs_inc = True
        need_drain = False
        for d in deps:
            if d is o:
                continue
            if d.is_dma:
                d.needs_inc = True
                o.deps.append(d)
            elif d.eng == eng:
                if eng == "pe":
                    continue
                if d.seq > self.last_barrier[eng]:
                    need_drain = True
            else:
                d.needs_inc = True
                o.deps.append(d)
        if need_drain:
            dr = _Op(eng, None, False, "drain", len(self.ops[eng]))
            dr.is_drain = True
            self.ops[eng].append(dr)
            self.last_barrier[eng] = dr.seq - 1
            o.seq = len(self.ops[eng])
        for b in reads:
            if o not in b.readers:
                b.readers.append(o)
        for b in writes:
            b.writer = o
            b.readers = []
        self.ops[eng].append(o)
        return o

    def finalize(self):
        cnt = {}
        for e in self.ENGS:
            for o in self.ops[e]:
                if o.needs_inc:
                    cnt[o.sem_key] = cnt.get(o.sem_key, 0) + (16 if o.is_dma else 1)
                    o.sem_val = cnt[o.sem_key]
        for e in self.ENGS:
            high = {}
            for o in self.ops[e]:
                ws = []
                for d in o.deps:
                    assert d.sem_val is not None, (e, o.tag, d.eng, d.tag)
                    if d.sem_val > high.get(d.sem_key, 0):
                        ws.append((d.sem_key, d.sem_val))
                        high[d.sem_key] = d.sem_val
                o.waits = ws

    def emit(self, eng_obj, ename):
        for o in self.ops[ename]:
            for (se, v) in o.waits:
                eng_obj.wait_ge(self.sems[se], v)
            if o.is_drain:
                eng_obj.drain()
                continue
            if o.fn is None:
                continue
            inst = o.fn(eng_obj)
            if o.needs_inc:
                inst.then_inc(self.sems[o.sem_key], 16 if o.is_dma else 1)


def build_nc():
    nc = bass.Bass("TRN2", target_bir_lowering=False)

    x_h = nc.dram_tensor("x", [D, TOK], F32, kind="ExternalInput")
    y_h = nc.dram_tensor("y", [D, TOK], F32, kind="ExternalOutput")
    pw_h = nc.dram_tensor("pwT", [D, D], F32, kind="ExternalInput").ap()
    f1_h = nc.dram_tensor("f1T", [D, H1], F32, kind="ExternalInput").ap()
    f2_h = nc.dram_tensor("f2m", [KC, 128, M1, 128], F32, kind="ExternalInput")
    g_h = nc.dram_tensor("gT", [D, H2], I8, kind="ExternalInput").ap()
    u_h = nc.dram_tensor("uT", [D, H2], I8, kind="ExternalInput").ap()
    d_h = nc.dram_tensor("dm", [KC, 128, M2, 128], I8, kind="ExternalInput")
    f2s_h = nc.dram_tensor("f2s", [KC, 128, M1, 128], BF16)
    ds_h = nc.dram_tensor("ds", [KC, 128, M2, 128], BF16)
    vf_h = nc.dram_tensor("vecf", [128, 112], F32, kind="ExternalInput").ap()
    vb_h = nc.dram_tensor("vecb", [128, 80], F32, kind="ExternalInput").ap()

    vecf = nc.alloc_sbuf_tensor("vecf_s", [128, 112], F32)
    vecb = nc.alloc_sbuf_tensor("vecb_s", [128, 80], BF16)
    cvec = nc.alloc_sbuf_tensor("cvec", [128, 24], F32)
    c1f = nc.alloc_sbuf_tensor("c1f", [128, 32], F32)
    ctmp = nc.alloc_sbuf_tensor("ctmp", [128, 32], F32)
    epsc = nc.alloc_sbuf_tensor("epsc", [128, 3], F32)
    warm = nc.alloc_sbuf_tensor("warm", [128, 1], F32)
    onec = nc.alloc_sbuf_tensor("onec", [128, 1], F32)
    avg1 = nc.alloc_sbuf_tensor("avg1", [128, 128], BF16)
    avg4 = nc.alloc_sbuf_tensor("avg4", [128, 128], BF16)
    x1r = nc.alloc_sbuf_tensor("x1r", [128, KC * TOK], BF16)
    rows = [nc.alloc_sbuf_tensor(f"rows{p}", [128, NROW * T], F32) for p in range(2)]
    oWp = 0
    oW1 = oWp + 8 * 1024
    oW2 = oW1 + 8 * 4096
    oX0 = oW2 + 3 * 4096
    oY = oX0 + 2 * 2048
    oXA = oY + 2048
    oZ = oXA + 2 * 2048
    oH = oZ + 2048
    oAO = oH + 8192
    oSC = oAO + 2048
    aA = oSC + 1024
    oWg = 0
    oWu = oWg + 8 * 3072
    oWd = oWu + 8 * 3072
    oZ2 = oWd + 4 * 3072
    oHF = oZ2 + 2 * 2048
    oSG = oHF + 6144
    oF = oSG + 2 * 512
    aB = oF + 2048
    WB = max(aA, aB)
    wb = nc.alloc_sbuf_tensor("wb", [128, WB], BF16)
    oT = 0
    oE = 2048
    oTR = oE + 512
    FB = 3072
    fb = nc.alloc_sbuf_tensor("fb", [128, FB], F32)

    P = [nc.alloc_psum_tensor(f"ps{i}", [128, 512], F32) for i in range(8)]

    sch = _Sched(nc)
    bf = sch.buf
    S_ = sch.op

    b_vecf = bf("vecf")
    b_vecb = bf("vecb")
    b_cvec = bf("cvec")
    b_c1f = [bf(f"c1f{g}") for g in range(4)]
    b_ctmp = [bf("ctmp_e"), bf("ctmp_o")]
    b_ctmp2 = [bf("ctmp2_e"), bf("ctmp2_o")]
    b_eps = bf("eps")
    b_avg1 = bf("avg1")
    b_avg4 = bf("avg4")
    b_x1 = [bf(f"x1_{i}") for i in range(NT)]
    b_rows = [[bf(f"rows{p}_{s}") for s in range(NROW)] for p in range(2)]

    W = "w"
    b_Wp = bf("Wp", W, oWp * 2, (oWp + 8 * 1024) * 2)
    b_W1 = [[bf(f"W1_{k}_{g}", W, (oW1 + k * 4096 + g * 1024) * 2,
                (oW1 + k * 4096 + (g + 1) * 1024) * 2) for g in range(4)]
            for k in range(KC)]
    b_W2 = [bf(f"W2_{j}", W, (oW2 + j * 4096) * 2, (oW2 + (j + 1) * 4096) * 2)
            for j in range(3)]
    b_X0 = [bf(f"X0_{p}", W, (oX0 + p * 2048) * 2, (oX0 + (p + 1) * 2048) * 2)
            for p in range(2)]
    b_Y = bf("Y", W, oY * 2, (oY + 2048) * 2)
    b_XA = [[bf(f"XA{p}_{c}", W, (oXA + p * 2048 + c * T) * 2,
                (oXA + p * 2048 + (c + 1) * T) * 2) for c in range(KC)]
            for p in range(2)]
    b_Z = bf("Z", W, oZ * 2, (oZ + 2048) * 2)
    b_H = [bf(f"H{gq}", W, (oH + gq * 2048) * 2, (oH + (gq + 1) * 2048) * 2)
           for gq in range(4)]
    b_AO = [bf(f"AO{c}", W, (oAO + c * T) * 2, (oAO + (c + 1) * T) * 2)
            for c in range(KC)]
    b_SC = bf("SC", W, oSC * 2, (oSC + 1024) * 2)
    b_CB = [bf(f"CB{s}", W, (oSC + s * 256) * 2, (oSC + (s + 1) * 256) * 2)
            for s in range(3)]
    b_Wg = [bf(f"Wg{k}", W, (oWg + k * 3072) * 2, (oWg + (k + 1) * 3072) * 2)
            for k in range(KC)]
    b_Wu = [bf(f"Wu{k}", W, (oWu + k * 3072) * 2, (oWu + (k + 1) * 3072) * 2)
            for k in range(KC)]
    b_Wd = [bf(f"Wd_{j}", W, (oWd + j * 3072) * 2, (oWd + (j + 1) * 3072) * 2)
            for j in range(4)]
    b_Z2 = [bf(f"Z2_{p}", W, (oZ2 + p * 2048) * 2, (oZ2 + (p + 1) * 2048) * 2)
            for p in range(2)]
    b_HF = bf("HF", W, oHF * 2, (oHF + 6144) * 2)
    b_SG = [bf(f"SG_{p}", W, (oSG + p * 512) * 2, (oSG + (p + 1) * 512) * 2)
            for p in range(2)]
    b_F = bf("F", W, oF * 2, (oF + 2048) * 2)
    Fa = "f"
    b_T = bf("Tsh", Fa, oT * 4, (oT + 2048) * 4)
    b_E = bf("Ero", Fa, oE * 4, (oE + 1024) * 4)
    b_trow = [bf(f"trow{t}", Fa, (oTR + t * 128) * 4, (oTR + (t + 1) * 128) * 4)
              for t in range(2)]
    b_trow8 = [bf(f"trow8{t}", Fa, (oE + t * 512) * 4, (oE + (t + 1) * 512) * 4)
               for t in range(2)]
    b_P = [bf(f"P{i}") for i in range(8)]
    b_f2s = [bf(f"f2s{j}") for j in range(KC)]
    b_ds = [bf(f"ds{j}") for j in range(KC)]
    b_warm = bf("warm")
    b_onec = bf("onec")

    def wsl(off, n):
        return bass.AP(wb, off, [[WB, 128], [1, n]])

    def wsl3(off, d1, n1, n2):
        return bass.AP(wb, off, [[WB, 128], [d1, n1], [1, n2]])

    def fsl(off, n):
        return bass.AP(fb, off, [[FB, 128], [1, n]])

    def fsl3(off, d1, n1, n2):
        return bass.AP(fb, off, [[FB, 128], [d1, n1], [1, n2]])

    def rsl(p, s):
        return bass.AP(rows[p], s * T, [[NROW * T, 128], [1, T]])

    def rrep(p, s, reps):
        return bass.AP(rows[p], s * T, [[NROW * T, 128], [0, reps], [1, T]])

    def wrep(off, reps):
        return bass.AP(wb, off, [[WB, 128], [0, reps], [1, T]])

    def x1sl(i):
        return bass.AP(x1r, i * T, [[KC * TOK, 128], [TOK, KC], [1, T]])

    def x1c(c, i):
        return bass.AP(x1r, c * TOK + i * T, [[KC * TOK, 128], [1, T]])

    def pap(b, n=T, off=0):
        return P[b][:, off:off + n]

    def col(t, c):
        return t[:, c:c + 1]

    S_("v", lambda e: e.memset(epsc.ap()[:, 0:1], EPS), writes=[b_eps])
    S_("v", lambda e: e.memset(epsc.ap()[:, 1:2], EPS2), writes=[b_eps])
    S_("v", lambda e: e.memset(epsc.ap()[:, 2:3], EPS * EPS), writes=[b_eps])
    S_("v", lambda e: e.memset(onec.ap(), 1.0), writes=[b_onec])
    S_("v", lambda e: e.memset(avg1.ap(), 1.0 / D), writes=[b_avg1])
    S_("v", lambda e: e.memset(avg4.ap(), 1.0 / H1), writes=[b_avg4])
    S_("q", lambda e: e.dma_start(out=vecf.ap(), in_=vf_h), writes=[b_vecf], tag="vecf")

    def x0dma(i):
        p = i % 2
        dst = wsl3(oX0 + p * 2048, T, KC, T)
        src = bass.AP(x_h, i * T, [[TOK, 128], [128 * TOK, KC], [1, T]])
        S_("g", lambda e: e.dma_start(out=dst, in_=src), writes=[b_X0[p]],
           tag=f"x0_{i}")

    x0dma(0)
    S_("g", lambda e: e.dma_start(out=vecb.ap(), in_=vb_h), writes=[b_vecb], tag="vecb")
    S_("g", lambda e: e.dma_start(out=wsl(oWp, 8 * 1024), in_=pw_h), writes=[b_Wp],
       tag="Wp")
    for gq in range(4):
        for k in range(KC):
            S_("g", lambda e, k=k, gq=gq: e.dma_start(
                out=wsl(oW1 + k * 4096 + gq * 1024, 1024),
                in_=f1_h[k * 128:(k + 1) * 128, gq * 1024:(gq + 1) * 1024]),
                writes=[b_W1[k][gq]], tag=f"W1k{k}g{gq}")
        if gq == 0:
            x0dma(1)

    def emit_staging():
        for j in range(KC):
            n = 128 * M1 * 128
            S_("g", lambda e, j=j, n=n: e.dma_start(
                out=bass.AP(f2s_h, j * n, [[0, 1], [1, n]]),
                in_=bass.AP(f2_h, j * n, [[0, 1], [1, n]])),
                writes=[b_f2s[j]], tag=f"stg2.{j}")
        for j in range(KC):
            n = 128 * M2 * 128
            S_("g", lambda e, j=j, n=n: e.dma_start(
                out=bass.AP(ds_h, j * n, [[0, 1], [1, n]]),
                in_=bass.AP(d_h, j * n, [[0, 1], [1, n]])),
                writes=[b_ds[j]], tag=f"stgd.{j}")

    def bias_const(woff, wklen, bcol0, nk, banks2, whalf, wbuf, dst_col0, tagp):
        for h in whalf:
            bank = banks2[h % 2]
            sl = h % 2
            for k in range(nk):
                S_("pe", lambda e, h=h, k=k, bank=bank: e.matmul(
                    P[bank][0:1, 0:512], col(vecb, bcol0 + k),
                    wsl(woff + k * wklen + h * 512, 512),
                    start=(k == 0), stop=(k == nk - 1)),
                    reads=[b_vecb, wbuf(k, h)], writes=[b_P[bank]],
                    tag=f"{tagp}m{h}.{k}")
            S_("v", lambda e, sl=sl, bank=bank: e.tensor_copy(
                bass.AP(fb, oE + sl * 512, [[FB, 1], [1, 512]]),
                P[bank][0:1, 0:512]),
                reads=[b_P[bank]], writes=[b_trow8[sl]], tag=f"{tagp}cp{h}")
            for j in range(4):
                cc = (h % 2) * 4 + j
                S_("pe", lambda e, sl=sl, j=j, cc=cc: e.matmul(
                    P[5][:, cc:cc + 1],
                    bass.AP(fb, oE + sl * 512 + j * 128, [[FB, 1], [1, 128]]),
                    onec.ap()[0:1, 0:1], start=True, stop=True),
                    reads=[b_trow8[sl], b_onec], writes=[b_P[5]],
                    tag=f"{tagp}tp{h}.{j}")
        S_("v", lambda e: e.tensor_copy(
            bass.AP(ctmp, dst_col0, [[32, 128], [1, 8]]), P[5][:, 0:8]),
            reads=[b_P[5]], writes=[b_ctmp[0], b_ctmp[1]], tag=f"{tagp}out")

    def startup_consts():
        bias_const(oWp, 1024, 0, KC, (0, 2), [0, 1], lambda k, h: b_Wp, 0,
                   "c2p.")
        S_("v", lambda e: e.tensor_add(cvec.ap()[:, 0:8],
                                       bass.AP(ctmp, 0, [[32, 128], [1, 8]]),
                                       vecf.ap()[:, 64:72]),
           reads=[b_ctmp[0], b_ctmp[1], b_vecf], writes=[b_cvec], tag="c2p.add")
        S_("v", lambda e: e.tensor_mul(cvec.ap()[:, 0:8], cvec.ap()[:, 0:8],
                                       vecf.ap()[:, 8:16]),
           reads=[b_cvec, b_vecf], writes=[b_cvec], tag="c3p.mul")
        for k in range(KC):
            S_("v", lambda e, k=k: e.tensor_scalar_mul(
                wsl(oWp + k * 1024, 1024), wsl(oWp + k * 1024, 1024), col(vecf, k)),
                reads=[b_Wp, b_vecf], writes=[b_Wp], tag=f"fold.p{k}")
        for gq in range(4):
            bias_const(oW1, 4096, 8, KC, (1, 3), [2 * gq, 2 * gq + 1],
                       lambda k, h: b_W1[k][h // 2], 8 * gq, "c1f.")
            S_("v", lambda e, gq=gq: e.tensor_add(
                c1f.ap()[:, gq * 8:(gq + 1) * 8],
                bass.AP(ctmp, gq * 8, [[32, 128], [1, 8]]),
                vecf.ap()[:, 72 + gq * 8:80 + gq * 8]),
                reads=[b_ctmp[0], b_ctmp[1], b_vecf], writes=[b_c1f[gq]],
                tag=f"c1f.add{gq}")
            for k in range(KC):
                S_("v", lambda e, k=k, gq=gq: e.tensor_scalar_mul(
                    wsl(oW1 + k * 4096 + gq * 1024, 1024),
                    wsl(oW1 + k * 4096 + gq * 1024, 1024),
                    col(vecf, 16 + k)),
                    reads=[b_W1[k][gq], b_vecf], writes=[b_W1[k][gq]],
                    tag=f"fold.1{k}.{gq}")

    def stats_sqv(src_chunk_ap, src_buf, tagp, sq_off, sq_buf):
        srcb = src_buf if isinstance(src_buf, list) else [src_buf] * KC
        for b0 in range(2):
            S_("v", lambda e, b0=b0: e.tensor_mul(
                wsl3(sq_off + b0 * 4 * T, T, 4, T), src_chunk_ap((b0 * 4, 4)),
                src_chunk_ap((b0 * 4, 4))),
                reads=srcb[b0 * 4:b0 * 4 + 4], writes=[sq_buf],
                tag=f"{tagp}sq{b0}")

    def stats_mms(src_chunk_ap, src_buf, banks, tagp, sq_off, sq_buf):
        bs, bq = banks
        bsb, bso = bs if isinstance(bs, tuple) else (bs, 0)
        bqb, bqo = bq if isinstance(bq, tuple) else (bq, 0)
        srcb = src_buf if isinstance(src_buf, list) else [src_buf] * KC
        for c in range(KC):
            S_("pe", lambda e, c=c: e.matmul(
                pap(bsb, T, bso), avg1.ap(), src_chunk_ap(c), start=(c == 0),
                stop=(c == KC - 1)),
                reads=[srcb[c], b_avg1], writes=[b_P[bsb]], tag=f"{tagp}s{c}")
        for cc in range(KC):
            S_("pe", lambda e, cc=cc: e.matmul(
                pap(bqb, T, bqo), avg1.ap(), wsl(sq_off + cc * T, T),
                start=(cc == 0), stop=(cc == KC - 1)),
                reads=[sq_buf, b_avg1], writes=[b_P[bqb]], tag=f"{tagp}q{cc}")

    def stats_1k(p, src_chunk_ap, src_buf, banks, tagp, sq_off=oSC, sq_buf=None):
        if sq_buf is None:
            sq_buf = b_SC
        bs, bq = banks
        bsb, bso = bs if isinstance(bs, tuple) else (bs, 0)
        bqb, bqo = bq if isinstance(bq, tuple) else (bq, 0)
        srcb = src_buf if isinstance(src_buf, list) else [src_buf] * KC
        for c in range(KC):
            S_("pe", lambda e, c=c: e.matmul(
                pap(bsb, T, bso), avg1.ap(), src_chunk_ap(c), start=(c == 0),
                stop=(c == KC - 1)),
                reads=[srcb[c], b_avg1], writes=[b_P[bsb]], tag=f"{tagp}s{c}")
        for b0 in range(2):
            S_("v", lambda e, b0=b0: e.tensor_mul(
                wsl3(sq_off, T, 4, T), src_chunk_ap((b0 * 4, 4)),
                src_chunk_ap((b0 * 4, 4))),
                reads=srcb[b0 * 4:b0 * 4 + 4], writes=[sq_buf],
                tag=f"{tagp}sq{b0}")
            for c in range(4):
                cc = b0 * 4 + c
                S_("pe", lambda e, c=c, cc=cc: e.matmul(
                    pap(bqb, T, bqo), avg1.ap(), wsl(sq_off + c * T, T),
                    start=(cc == 0), stop=(cc == KC - 1)),
                    reads=[sq_buf, b_avg1], writes=[b_P[bqb]], tag=f"{tagp}q{cc}")
        return bs, bq

    def ln_rows(p, mu_s, r_s, tmp_s, banks, eps_col=0, double=False, s_slot=None,
                tagp="", bcast=False, head_v=False):
        bs, bq = banks
        bsb, bso = bs if isinstance(bs, tuple) else (bs, 0)
        bqb, bqo = bq if isinstance(bq, tuple) else (bq, 0)
        v_s = tmp_s + 1
        if head_v:
            S_("v", lambda e: e.tensor_copy(rsl(p, mu_s), pap(bsb, T, bso)),
               reads=[b_P[bsb]], writes=[b_rows[p][mu_s]], tag=f"{tagp}mu")
            S_("v", lambda e: e.tensor_mul(rsl(p, tmp_s), rsl(p, mu_s),
                                           rsl(p, mu_s)),
               reads=[b_rows[p][mu_s]], writes=[b_rows[p][tmp_s]],
               tag=f"{tagp}musq")
        else:
            S_("s", lambda e: e.activation(out=rsl(p, mu_s), in_=pap(bsb, T, bso),
                                           func=AF.Identity),
               reads=[b_P[bsb]], writes=[b_rows[p][mu_s]], tag=f"{tagp}mu")
            S_("s", lambda e: e.activation(out=rsl(p, tmp_s), in_=rsl(p, mu_s),
                                           func=AF.Square),
               reads=[b_rows[p][mu_s]], writes=[b_rows[p][tmp_s]],
               tag=f"{tagp}musq")
        if bcast:
            eng = "v" if head_v else "s"
            if head_v:
                S_("v", lambda e: e.tensor_copy(wsl(oSC, T), rsl(p, mu_s)),
                   reads=[b_rows[p][mu_s]], writes=[b_CB[0]], tag=f"{tagp}mub")
            else:
                S_("s", lambda e: e.activation(out=wsl(oSC, T), in_=rsl(p, mu_s),
                                               func=AF.Identity),
                   reads=[b_rows[p][mu_s]], writes=[b_CB[0]], tag=f"{tagp}mub")
        S_("v", lambda e: e.scalar_tensor_tensor(
            rsl(p, v_s), rsl(p, tmp_s), -1.0, pap(bqb, T, bqo), ALU.mult, ALU.add),
           reads=[b_rows[p][tmp_s], b_P[bqb]], writes=[b_rows[p][v_s]],
           tag=f"{tagp}v")
        if double:
            S_("s", lambda e: e.activation(out=rsl(p, tmp_s), in_=rsl(p, v_s),
                                           func=AF.Ln, bias=col(epsc, 2),
                                           scale=1.0 + EPS),
               reads=[b_rows[p][v_s], b_eps], writes=[b_rows[p][tmp_s]],
               tag=f"{tagp}lns")
            if bcast:
                S_("s", lambda e: e.activation(out=wsl(oSC + 256, T),
                                               in_=rsl(p, tmp_s),
                                               func=AF.Exp, scale=-0.5),
                   reads=[b_rows[p][tmp_s]], writes=[b_CB[1]], tag=f"{tagp}exps")
            else:
                S_("s", lambda e: e.activation(out=rsl(p, s_slot), in_=rsl(p, tmp_s),
                                               func=AF.Exp, scale=-0.5),
                   reads=[b_rows[p][tmp_s]], writes=[b_rows[p][s_slot]],
                   tag=f"{tagp}exps")
        S_("s", lambda e: e.activation(out=rsl(p, tmp_s), in_=rsl(p, v_s), func=AF.Ln,
                                       bias=col(epsc, eps_col), scale=1.0),
           reads=[b_rows[p][v_s], b_eps], writes=[b_rows[p][tmp_s]], tag=f"{tagp}ln")
        if bcast:
            S_("s", lambda e: e.activation(out=wsl(oSC + 512, T), in_=rsl(p, tmp_s),
                                           func=AF.Exp, scale=-0.5),
               reads=[b_rows[p][tmp_s]], writes=[b_CB[2]], tag=f"{tagp}exp")
        else:
            S_("s", lambda e: e.activation(out=rsl(p, r_s), in_=rsl(p, tmp_s),
                                           func=AF.Exp, scale=-0.5),
               reads=[b_rows[p][tmp_s]], writes=[b_rows[p][r_s]], tag=f"{tagp}exp")

    def prewarm_ln(tag):
        S_("s", lambda e: e.activation(out=warm.ap(), in_=col(epsc, 0), func=AF.Ln,
                                       bias=col(epsc, 0), scale=1.0),
           reads=[b_eps], writes=[b_warm], tag=tag)

    def a_src(off):
        def f(c):
            if isinstance(c, tuple):
                return wsl3(off + c[0] * T, T, c[1], T)
            return wsl(off + c * T, T)
        return f

    def headA_stats(i):
        p = i % 2
        b0 = 4 if p == 0 else 6
        stats_1k(p, a_src(oX0 + p * 2048), b_X0[p], ((b0, 0), (b0, 256)),
                 f"A{i}.l0.")

    def headA_rows(i):
        p = i % 2
        b0 = 4 if p == 0 else 6
        ln_rows(p, 0, 1, 3, ((b0, 0), (b0, 256)), double=True, s_slot=2,
                tagp=f"A{i}.l0.", bcast=True, head_v=True)
        S_("v", lambda e: e.tensor_sub(wsl3(oZ, T, KC, T),
                                       wsl3(oX0 + p * 2048, T, KC, T),
                                       wrep(oSC, KC)),
           reads=[b_X0[p], b_CB[0]], writes=[b_Z], tag=f"A{i}.t")
        S_("v", lambda e: e.tensor_mul(wsl3(oY, T, KC, T), wsl3(oZ, T, KC, T),
                                       wrep(oSC + 256, KC)),
           reads=[b_Z, b_CB[1]], writes=[b_Y], tag=f"A{i}.y")
        S_("v", lambda e: e.tensor_mul(wsl3(oZ, T, KC, T), wsl3(oZ, T, KC, T),
                                       wrep(oSC + 512, KC)),
           reads=[b_Z, b_CB[2]], writes=[b_Z], tag=f"A{i}.yin")
        for c in range(KC):
            S_("v", lambda e, c=c: e.tensor_scalar(
                wsl(oZ + c * T, T), wsl(oZ + c * T, T), 1.0, col(cvec, c),
                ALU.mult, ALU.add),
                reads=[b_Z, b_cvec], writes=[b_Z], tag=f"A{i}.yc{c}")
        S_("v", lambda e: e.tensor_copy(
            bass.AP(x1r, i * T, [[KC * TOK, 128], [TOK, KC], [1, T]]),
            wsl3(oX0 + p * 2048, T, KC, T)),
           reads=[b_X0[p]], writes=[b_x1[i]], tag=f"A{i}.x1p")

    def lnh_tail(j):
        pj = j % 2
        bkj = (4, 5) if pj == 0 else (6, 7)
        off, buf = oX0 + pj * 2048, b_X0[pj]
        for c in range(24, 32):
            S_("pe", lambda e, c=c: e.matmul(
                pap(bkj[0]), avg4.ap(), wsl(oH + c * T, T), start=False,
                stop=(c == M1 - 1)),
                reads=[b_H[3], b_avg4], writes=[b_P[bkj[0]]], tag=f"A{j}.lh.s{c}")
        for b0 in range(2):
            cc0 = 24 + b0 * 4
            S_("v", lambda e, cc0=cc0, off=off, b0=b0: e.tensor_mul(
                wsl3(off + b0 * 4 * T, T, 4, T), wsl3(oH + cc0 * T, T, 4, T),
                wsl3(oH + cc0 * T, T, 4, T)),
                reads=[b_H[3]], writes=[buf], tag=f"A{j}.hsq{cc0}")
        for c in range(24, 32):
            S_("v", lambda e, c=c: e.tensor_scalar_mul(
                wsl(oH + c * T, T), wsl(oH + c * T, T), col(vecf, 32 + c)),
                reads=[b_H[3], b_vecf], writes=[b_H[3]], tag=f"A{j}.hg{c}")
        for c in range(8):
            cc = 24 + c
            S_("pe", lambda e, c=c, cc=cc, off=off, bkj=bkj: e.matmul(
                pap(bkj[1]), avg4.ap(), wsl(off + c * T, T),
                start=False, stop=(cc == M1 - 1)),
                reads=[buf, b_avg4], writes=[b_P[bkj[1]]], tag=f"A{j}.lh.q{cc}")
        ln_rows(pj, 2, 4, 3, bkj, tagp=f"A{j}.lh.")
        S_("v", lambda e, pj=pj: e.tensor_mul(rsl(pj, 3), rsl(pj, 2), rsl(pj, 4)),
           reads=[b_rows[pj][2], b_rows[pj][4]], writes=[b_rows[pj][3]],
           tag=f"A{j}.mrh")
        S_("v", lambda e, pj=pj: e.tensor_scalar_mul(rsl(pj, 4), rsl(pj, 4), -1.0),
           reads=[b_rows[pj][4]], writes=[b_rows[pj][4]], tag=f"A{j}.nrh")

    def chainA_stats(i):
        p = i % 2
        b1 = 5 if p == 0 else 7
        stats_1k(p, a_src(oAO), b_AO, ((b1, 0), (b1, 256)), f"A{i}.l3.")

    def chainA_lnrows(i):
        p = i % 2
        b1 = 5 if p == 0 else 7
        ln_rows(p, 0, 1, 3, ((b1, 0), (b1, 256)), tagp=f"A{i}.l3.")

    def chainA_sheets(i):
        p = i % 2
        for hh in range(2):
            xsl = bass.AP(x1r, (hh * 4) * TOK + i * T,
                          [[KC * TOK, 128], [TOK, 4], [1, T]])
            S_("v", lambda e, hh=hh: e.tensor_sub(
                wsl3(oSC, T, 4, T), wsl3(oAO + hh * 4 * T, T, 4, T), rrep(p, 0, 4)),
                reads=b_AO[hh * 4:hh * 4 + 4] + [b_rows[p][0]], writes=[b_SC],
                tag=f"A{i}.t3{hh}")
            S_("v", lambda e, hh=hh: e.tensor_mul(
                wsl3(oSC, T, 4, T), wsl3(oSC, T, 4, T), rrep(p, 1, 4)),
                reads=[b_SC, b_rows[p][1]], writes=[b_SC], tag=f"A{i}.t3r{hh}")
            S_("v", lambda e, hh=hh, xsl=xsl: e.tensor_add(
                xsl, wsl3(oSC, T, 4, T), xsl),
                reads=[b_SC, b_x1[i]], writes=[b_x1[i]], tag=f"A{i}.x1{hh}")

    w2_cnt = [0]

    def w2dma(g):
        sb = g % 3
        m = g % KC
        dst = wsl3(oW2 + sb * 4096, 128, M1, 128)
        src = bass.AP(f2s_h, m * 128 * M1 * 128,
                      [[M1 * 128, 128], [128, M1], [1, 128]])
        S_("q", lambda e: e.dma_start(out=dst, in_=src), reads=[b_f2s[m]],
           writes=[b_W2[sb]], tag=f"W2g{g}")

    def ensure_w2(g):
        while w2_cnt[0] <= g:
            w2dma(w2_cnt[0])
            w2_cnt[0] += 1

    def fc2A(j):
        pj = j % 2
        bkj = (4, 5) if pj == 0 else (6, 7)
        for mp in range(KC // 2):
            for half in range(2):
                m = 2 * mp + half
                gidx = j * KC + m
                ensure_w2(gidx)
                sb = gidx % 3
                mb = m % 4
                for k in range(M1):
                    S_("pe", lambda e, m=m, k=k, mb=mb, sb=sb: e.matmul(
                        pap(mb), wsl(oW2 + sb * 4096 + k * 128, 128),
                        wsl(oH + k * T, T),
                        start=(k == 0), stop=(k == M1 - 1)),
                        reads=[b_W2[sb], b_H[k // 8]], writes=[b_P[mb]],
                        tag=f"A{j}.f2.{m}.{k}")
                if j == 0:
                    cb = bkj[0]
                    for k in range(M1):
                        S_("pe", lambda e, k=k, sb=sb: e.matmul(
                            P[cb][:, 256:258],
                            wsl(oW2 + sb * 4096 + k * 128, 128),
                            vecb.ap()[:, 16 + k:49 + k:32],
                            start=(k == 0), stop=(k == M1 - 1)),
                            reads=[b_vecb, b_W2[sb]], writes=[b_P[cb]],
                            tag=f"c2f.{m}.{k}")
                    S_("v", lambda e, m=m: e.tensor_add(
                        col(cvec, 8 + m), P[cb][:, 256:257], col(vecf, 104 + m)),
                        reads=[b_P[cb], b_vecf], writes=[b_cvec],
                        tag=f"c2f.add{m}")
                    S_("v", lambda e, m=m: e.tensor_mul(
                        col(cvec, 8 + m), col(cvec, 8 + m), col(vecf, 24 + m)),
                        reads=[b_cvec, b_vecf], writes=[b_cvec], tag=f"c3f.mul{m}")
                    S_("v", lambda e, m=m: e.tensor_copy(
                        bass.AP(cvec, 16 + m, [[24, 128], [1, 1]]), P[cb][:, 257:258]),
                        reads=[b_P[cb]], writes=[b_cvec], tag=f"wgh.sc{m}")
                if gidx + 3 < NT * KC:
                    ensure_w2(gidx + 3)
            for half in range(2):
                m = 2 * mp + half
                S_("v", lambda e, m=m, mb=m % 4: e.scalar_tensor_tensor(
                    fsl(oE + (m % 2) * 256, T), rsl(pj, 3), col(cvec, 16 + m),
                    pap(mb), ALU.mult, ALU.subtract),
                    reads=[b_rows[pj][3], b_cvec, b_P[m % 4]], writes=[b_E],
                    tag=f"A{j}.f2v{m}")
            for half in range(2):
                m = 2 * mp + half
                S_("v", lambda e, m=m: e.tensor_mul(
                    fsl(oE + (2 + m % 2) * 256, T), fsl(oE + (m % 2) * 256, T),
                    rsl(pj, 4)),
                    reads=[b_E, b_rows[pj][4]], writes=[b_E], tag=f"A{j}.f2t{m}")
            for half in range(2):
                m = 2 * mp + half
                S_("s", lambda e, m=m: e.activation(
                    out=fsl(oE + (m % 2) * 256, T), in_=fsl(oE + (2 + m % 2) * 256, T),
                    func=AF.Identity, bias=col(cvec, 8 + m), scale=col(vecf, 24 + m)),
                    reads=[b_E, b_cvec, b_vecf], writes=[b_E], tag=f"A{j}.f2e{m}")
            for half in range(2):
                m = 2 * mp + half
                S_("v", lambda e, m=m: e.tensor_add(
                    wsl(oAO + m * T, T), fsl(oE + (m % 2) * 256, T),
                    wsl(oXA + pj * 2048 + m * T, T)),
                    reads=[b_E, b_XA[pj][m]], writes=[b_AO[m]], tag=f"A{j}.ao{m}")

    def bodyA(i):
        p = i % 2
        bk = (4, 5) if p == 0 else (6, 7)
        for m in range(KC):
            mb = m % 4
            for k in range(KC):
                S_("pe", lambda e, m=m, k=k, mb=mb: e.matmul(
                    pap(mb), wsl(oWp + k * 1024 + m * 128, 128),
                    wsl(oY + k * T, T),
                    start=(k == 0), stop=(k == KC - 1)),
                    reads=[b_Wp, b_Y], writes=[b_P[mb]], tag=f"A{i}.pj{m}.{k}")
            S_("v", lambda e, m=m, mb=mb: e.scalar_tensor_tensor(
                wsl(oXA + p * 2048 + m * T, T), pap(mb), col(vecf, 8 + m),
                wsl(oZ + m * T, T), ALU.mult, ALU.add),
                reads=[b_P[mb], b_vecf, b_Z], writes=[b_XA[p][m]], tag=f"A{i}.xa{m}")
        stats_sqv(a_src(oXA + p * 2048), b_XA[p], f"A{i}.l2.", oZ, b_Z)
        if i > 0:
            lnh_tail(i - 1)
        if i > 1:
            chainA_sheets(i - 2)
        if i > 0:
            fc2A(i - 1)
        stats_mms(a_src(oXA + p * 2048), b_XA[p], bk, f"A{i}.l2.", oZ, b_Z)
        ln_rows(p, 5, 2, 3, bk, tagp=f"A{i}.l2.")
        S_("v", lambda e: e.tensor_sub(fsl3(oT, T, KC, T),
                                       wsl3(oXA + p * 2048, T, KC, T), rrep(p, 5, KC)),
           reads=b_XA[p] + [b_rows[p][5]], writes=[b_T], tag=f"A{i}.t2")
        if i > 0:
            stats_sqv(a_src(oAO), b_AO, f"A{i}.l3.", oZ, b_Z)
            p1 = (i - 1) % 2
            b1 = 5 if p1 == 0 else 7
            stats_mms(a_src(oAO), b_AO, ((b1, 0), (b1, 256)), f"A{i - 1}.l3.",
                      oZ, b_Z)
        S_("v", lambda e: e.tensor_mul(wsl3(oZ, T, KC, T), fsl3(oT, T, KC, T),
                                       rrep(p, 2, KC)),
           reads=[b_T, b_rows[p][2]], writes=[b_Z], tag=f"A{i}.z")
        if i > 0 and i + 1 < NT:
            x0dma(i + 1)
        def sqreg(g):
            if g % 2 == 0:
                return oY, b_Y
            return oX0 + p * 2048, b_X0[p]

        def lnh_sum(g):
            c0 = g * 8
            for c in range(c0, c0 + 8):
                S_("pe", lambda e, c=c: e.matmul(
                    pap(bk[0]), avg4.ap(), wsl(oH + c * T, T), start=(c == 0),
                    stop=(c == M1 - 1)),
                    reads=[b_H[c // 8], b_avg4], writes=[b_P[bk[0]]],
                    tag=f"A{i}.lh.s{c}")

        def lnh_sqv(g, hg=True):
            off, buf = sqreg(g)
            c0 = g * 8
            for b0 in range(2):
                cc0 = c0 + b0 * 4
                S_("v", lambda e, cc0=cc0, off=off, b0=b0: e.tensor_mul(
                    wsl3(off + b0 * 4 * T, T, 4, T), wsl3(oH + cc0 * T, T, 4, T),
                    wsl3(oH + cc0 * T, T, 4, T)),
                    reads=[b_H[cc0 // 8]], writes=[buf], tag=f"A{i}.hsq{cc0}")
            if hg:
                lnh_hg(g)

        def lnh_hg(g):
            for c in range(g * 8, g * 8 + 8):
                S_("v", lambda e, c=c: e.tensor_scalar_mul(
                    wsl(oH + c * T, T), wsl(oH + c * T, T), col(vecf, 32 + c)),
                    reads=[b_H[c // 8], b_vecf], writes=[b_H[c // 8]],
                    tag=f"A{i}.hg{c}")

        def lnh_sqmm(g):
            off, buf = sqreg(g)
            for c in range(8):
                cc = g * 8 + c
                S_("pe", lambda e, c=c, cc=cc, off=off: e.matmul(
                    pap(bk[1]), avg4.ap(), wsl(off + c * T, T),
                    start=(cc == 0), stop=(cc == M1 - 1)),
                    reads=[buf, b_avg4], writes=[b_P[bk[1]]],
                    tag=f"A{i}.lh.q{cc}")

        for g in range(4):
            for m in range(g * 8, g * 8 + 8):
                mb = m % 4
                for k in range(KC):
                    S_("pe", lambda e, m=m, k=k, mb=mb: e.matmul(
                        pap(mb), wsl(oW1 + k * 4096 + m * 128, 128),
                        wsl(oZ + k * T, T),
                        start=(k == 0), stop=(k == KC - 1)),
                        reads=[b_W1[k][m // 8], b_Z], writes=[b_P[mb]],
                        tag=f"A{i}.f1.{m}.{k}")
                S_("s", lambda e, m=m, mb=mb: e.activation(
                    out=wsl(oH + m * T, T), in_=pap(mb), func=AF.Gelu,
                    bias=col(c1f, m), scale=1.0),
                    reads=[b_P[mb], b_c1f[m // 8]], writes=[b_H[m // 8]],
                    tag=f"A{i}.gel{m}")
            if g > 0:
                lnh_sum(g - 1)
                lnh_sqv(g - 1)
            if g > 1:
                lnh_sqmm(g - 2)
            if g == 0:
                ensure_w2(i * KC + 2)
            if g == 2 and i + 1 < NT:
                headA_stats(i + 1)
        lnh_sqmm(2)
        prewarm_ln(f"A{i}.warm")
        if i + 1 < NT:
            headA_rows(i + 1)
        if i > 0:
            chainA_lnrows(i - 1)

    def headB_sq(i):
        for b0 in range(2):
            S_("v", lambda e, b0=b0, i=i: e.tensor_mul(
                wsl3(oF + b0 * 4 * T, T, 4, T),
                bass.AP(x1r, b0 * 4 * TOK + i * T, [[KC * TOK, 128], [TOK, 4], [1, T]]),
                bass.AP(x1r, b0 * 4 * TOK + i * T, [[KC * TOK, 128], [TOK, 4], [1, T]])),
                reads=[b_x1[i]], writes=[b_F], tag=f"B{i}.z2sq{b0}")

    def headB(i):
        p = i % 2
        bs, bq = 6, 7
        for c in range(KC):
            S_("pe", lambda e, c=c: e.matmul(
                pap(bs), avg1.ap(), x1c(c, i), start=(c == 0), stop=(c == KC - 1)),
                reads=[b_x1[i], b_avg1], writes=[b_P[bs]], tag=f"B{i}.z2s{c}")
        for cc in range(KC):
            S_("pe", lambda e, cc=cc: e.matmul(
                pap(bq), avg1.ap(), wsl(oF + cc * T, T), start=(cc == 0),
                stop=(cc == KC - 1)),
                reads=[b_F, b_avg1], writes=[b_P[bq]], tag=f"B{i}.z2q{cc}")
        ln_rows(p, 0, 1, 3, (bs, bq), tagp=f"B{i}.z2.")
        S_("v", lambda e: e.tensor_sub(fsl3(oT, T, KC, T), x1sl(i), rrep(p, 0, KC)),
           reads=[b_x1[i], b_rows[p][0]], writes=[b_T], tag=f"B{i}.tz")
        S_("v", lambda e: e.tensor_mul(wsl3(oZ2 + p * 2048, T, KC, T),
                                       fsl3(oT, T, KC, T), rrep(p, 1, KC)),
           reads=[b_T, b_rows[p][1]], writes=[b_Z2[p]], tag=f"B{i}.z2n")

    wd_cnt = [0]

    def wddma(g):
        sb = g % 4
        m = g % KC
        dst = wsl3(oWd + sb * 3072, 128, M2, 128)
        src = bass.AP(ds_h, m * 128 * M2 * 128,
                      [[M2 * 128, 128], [128, M2], [1, 128]])
        S_("q", lambda e: e.dma_start(out=dst, in_=src), reads=[b_ds[m]],
           writes=[b_Wd[sb]], tag=f"Wdg{g}")

    def ensure_wd(g):
        while wd_cnt[0] <= g:
            wddma(wd_cnt[0])
            wd_cnt[0] += 1

    def bodyB(i):
        p = i % 2

        def lf_pair(mp):
            for half in range(2):
                c = 2 * mp + half
                S_("pe", lambda e, c=c: e.matmul(
                    pap(3), avg1.ap(), wsl(oF + c * T, T), start=(c == 0),
                    stop=(c == KC - 1)),
                    reads=[b_F, b_avg1], writes=[b_P[3]], tag=f"B{i}.lf.s{c}")
            for half in range(2):
                c = 2 * mp + half
                S_("pe", lambda e, c=c: e.matmul(
                    pap(4), avg1.ap(), wsl(oZ2 + p * 2048 + c * T, T),
                    start=(c == 0), stop=(c == KC - 1)),
                    reads=[b_Z2[p], b_avg1], writes=[b_P[4]], tag=f"B{i}.lf.q{c}")

        if i + 1 < NT:
            headB_sq(i + 1)
        for mp in range(M2 // 2):
            m0 = 2 * mp
            gb = mp % 3
            ub = 3 + mp % 3
            for half in range(2):
                m = m0 + half
                for k in range(KC):
                    S_("pe", lambda e, m=m, k=k, gb=gb, half=half: e.matmul(
                        pap(gb, T, half * 256),
                        wsl(oWg + k * 3072 + m * 128, 128),
                        wsl(oZ2 + p * 2048 + k * T, T), start=(k == 0),
                        stop=(k == KC - 1)),
                        reads=[b_Wg[k], b_Z2[p]], writes=[b_P[gb]],
                        tag=f"B{i}.g{m}.{k}")
            for half in range(2):
                m = m0 + half
                for k in range(KC):
                    S_("pe", lambda e, m=m, k=k, ub=ub, half=half: e.matmul(
                        pap(ub, T, half * 256),
                        wsl(oWu + k * 3072 + m * 128, 128),
                        wsl(oZ2 + p * 2048 + k * T, T), start=(k == 0),
                        stop=(k == KC - 1)),
                        reads=[b_Wu[k], b_Z2[p]], writes=[b_P[ub]],
                        tag=f"B{i}.u{m}.{k}")
            sg = mp % 2
            S_("s", lambda e, gb=gb, sg=sg: e.activation(
                out=wsl(oSG + sg * 512, 512), in_=pap(gb, 512), func=AF.Silu,
                scale=S127),
                reads=[b_P[gb]], writes=[b_SG[sg]], tag=f"B{i}.si{mp}")
            S_("v", lambda e, m0=m0, ub=ub, sg=sg: e.scalar_tensor_tensor(
                wsl(oHF + m0 * T, 512), pap(ub, 512), S127, wsl(oSG + sg * 512, 512),
                ALU.mult, ALU.mult),
                reads=[b_P[ub], b_SG[sg]], writes=[b_HF], tag=f"B{i}.hf{mp}")
        prewarm_ln(f"B{i}.warm")
        if i + 1 < NT:
            headB(i + 1)
        ensure_wd(i * KC + 3)
        for mp in range(KC // 2):
            m0 = 2 * mp
            dbk = mp % 3
            for half in range(2):
                m = m0 + half
                gidx = i * KC + m
                ensure_wd(gidx)
                sb = gidx % 4
                for k in range(M2):
                    S_("pe", lambda e, m=m, k=k, dbk=dbk, sb=sb, half=half: e.matmul(
                        pap(dbk, T, half * 256),
                        wsl(oWd + sb * 3072 + k * 128, 128), wsl(oHF + k * T, T),
                        start=(k == 0), stop=(k == M2 - 1)),
                        reads=[b_Wd[sb], b_HF], writes=[b_P[dbk]], tag=f"B{i}.d{m}.{k}")
                if gidx + 3 < NT * KC:
                    ensure_wd(gidx + 3)
            S_("s", lambda e, dbk=dbk, m0=m0: e.activation(
                out=wsl(oF + m0 * T, 512), in_=pap(dbk, 512), func=AF.Identity),
                reads=[b_P[dbk]], writes=[b_F], tag=f"B{i}.fe{mp}")
            S_("v", lambda e, m0=m0: e.tensor_mul(
                wsl(oZ2 + p * 2048 + m0 * T, 512), wsl(oF + m0 * T, 512),
                wsl(oF + m0 * T, 512)),
                reads=[b_F], writes=[b_Z2[p]], tag=f"B{i}.fsq{mp}")
            if mp > 0:
                lf_pair(mp - 1)
        lf_pair(KC // 2 - 1)
        ln_rows(p, 5, 2, 3, (3, 4), eps_col=1, tagp=f"B{i}.lf.")
        S_("v", lambda e: e.tensor_sub(fsl3(oT, T, KC, T), wsl3(oF, T, KC, T),
                                       rrep(p, 5, KC)),
           reads=[b_F, b_rows[p][5]], writes=[b_T], tag=f"B{i}.tf")
        S_("v", lambda e: e.tensor_mul(fsl3(oT, T, KC, T), fsl3(oT, T, KC, T),
                                       rrep(p, 2, KC)),
           reads=[b_T, b_rows[p][2]], writes=[b_T], tag=f"B{i}.tfr")
        for hh in range(2):
            S_("v", lambda e, hh=hh: e.tensor_add(
                fsl3(oE, T, 4, T), fsl3(oT + hh * 4 * T, T, 4, T),
                bass.AP(x1r, (hh * 4) * TOK + i * T,
                        [[KC * TOK, 128], [TOK, 4], [1, T]])),
                reads=[b_T, b_x1[i]], writes=[b_E], tag=f"B{i}.out{hh}")
            dst = bass.AP(y_h, (hh * 4 * 128) * TOK + i * T,
                          [[TOK, 128], [128 * TOK, 4], [1, T]])
            S_("q", lambda e, hh=hh, dst=dst: e.dma_start(out=dst,
                                                          in_=fsl3(oE, T, 4, T)),
               reads=[b_E], tag=f"B{i}.st{hh}")

    headA_stats(0)
    startup_consts()
    emit_staging()
    headA_rows(0)
    for i in range(NT):
        bodyA(i)
    for k in range(KC):
        S_("g", lambda e, k=k: e.dma_start(
            out=wsl(oWg + k * 3072, 3072), in_=g_h[k * 128:(k + 1) * 128, :]),
            writes=[b_Wg[k]], tag=f"Wgk{k}")
    chainA_sheets(NT - 2)
    lnh_tail(NT - 1)
    fc2A(NT - 1)
    for k in range(KC):
        S_("g", lambda e, k=k: e.dma_start(
            out=wsl(oWu + k * 3072, 3072), in_=u_h[k * 128:(k + 1) * 128, :]),
            writes=[b_Wu[k]], tag=f"Wuk{k}")
    stats_sqv(a_src(oAO), b_AO, f"A{NT - 1}.l3.", oZ, b_Z)
    stats_mms(a_src(oAO), b_AO, ((7, 0), (7, 256)), f"A{NT - 1}.l3.", oZ, b_Z)
    chainA_lnrows(NT - 1)
    chainA_sheets(NT - 1)
    ensure_wd(3)
    headB_sq(0)
    headB(0)
    for i in range(NT):
        bodyB(i)

    last_store = [o for o in sch.ops["q"]
                  if o.tag.startswith("B") and ".st" in o.tag][-1]
    last_store.needs_inc = True
    fin = _Op("q", None, False, "final", len(sch.ops["q"]))
    fin.deps = [last_store]
    sch.ops["q"].append(fin)

    sch.finalize()

    with nc.Block() as block:
        @block.tensor
        def _(eng):
            sch.emit(eng, "pe")

        @block.vector
        def _(eng):
            sch.emit(eng, "v")

        @block.scalar
        def _(eng):
            sch.emit(eng, "s")

        @block.gpsimd
        def _(eng):
            sch.emit(eng, "g")

        @block.sync
        def _(eng):
            sch.emit(eng, "q")
    return nc


_NC_CACHE = {}


def _get_nc():
    if "nc" not in _NC_CACHE:
        _NC_CACHE["nc"] = build_nc()
    return _NC_CACHE["nc"]


def _pack_cols(vs):
    outs = []
    for v in vs:
        outs.append(np.ascontiguousarray(np.asarray(v, np.float32).reshape(-1, 128).T))
    return np.ascontiguousarray(np.concatenate(outs, axis=1))


def kernel(**inp):
    X = np.asarray(inp["L0_attn_x"], np.float32).reshape(16384, 1024)
    pwT = np.ascontiguousarray(np.asarray(inp["L0_attn_proj_w"], np.float32).T)
    f1T = np.ascontiguousarray(np.asarray(inp["L0_attn_fc1_w"], np.float32).T)
    f2T = np.asarray(inp["L0_attn_fc2_w"], np.float32).T
    f2m = np.ascontiguousarray(
        f2T.reshape(M1, 128, KC, 128).transpose(2, 1, 0, 3))
    gT = np.ascontiguousarray(np.asarray(inp["L0_ffn_w_gate"]).T.astype(np.int8))
    uT = np.ascontiguousarray(np.asarray(inp["L0_ffn_w_up"]).T.astype(np.int8))
    dT = np.asarray(inp["L0_ffn_w_down"]).T.astype(np.int8)
    dm = np.ascontiguousarray(dT.reshape(M2, 128, KC, 128).transpose(2, 1, 0, 3))
    vecf = _pack_cols([inp["L0_attn_ln1_g"], inp["L0_attn_gamma1"],
                       inp["L0_attn_ln2_g"], inp["L0_attn_gamma2"],
                       inp["L0_attn_lnh_g"], inp["L0_attn_proj_b"],
                       inp["L0_attn_fc1_b"], inp["L0_attn_fc2_b"]])
    vecb = _pack_cols([inp["L0_attn_ln1_b"], inp["L0_attn_ln2_b"],
                       inp["L0_attn_lnh_b"], inp["L0_attn_lnh_g"]])
    nc = _get_nc()
    in_maps = []
    for c in range(8):
        xs = np.ascontiguousarray(X[c * TOK:(c + 1) * TOK].T)
        in_maps.append({"x": xs, "pwT": pwT, "f1T": f1T, "f2m": f2m,
                        "gT": gT, "uT": uT, "dm": dm, "vecf": vecf, "vecb": vecb})
    res = run_bass_kernel_spmd(nc, in_maps, core_ids=list(range(8)))
    out = np.concatenate([np.asarray(r["y"], np.float32).T for r in res.results],
                         axis=0)
    return np.ascontiguousarray(out.reshape(4, 4096, 1024))


# version 28
# speedup vs baseline: 1.0953x; 1.0953x over previous
import numpy as np
import concourse.bass as bass
import concourse.mybir as mybir
from concourse.bass_utils import run_bass_kernel_spmd

F32 = mybir.dt.float32
BF16 = mybir.dt.bfloat16
I8 = mybir.dt.int8
AF = mybir.ActivationFunctionType
ALU = mybir.AluOpType

EPS = 1e-5
S127 = 1.0 / 127.0
EPS2 = EPS * 127.0 * 127.0

D = 1024
H1 = 4096
H2 = 3072
KC = 8
M1 = 32
M2 = 24
T = 256
NT = 8
TOK = T * NT
NROW = 6


class _Buf:
    __slots__ = ("name", "arena", "lo", "hi", "writer", "readers")

    def __init__(self, name, arena=None, lo=0, hi=0):
        self.name, self.arena, self.lo, self.hi = name, arena, lo, hi
        self.writer = None
        self.readers = []


class _Op:
    __slots__ = ("eng", "fn", "needs_inc", "sem_val", "deps", "waits",
                 "is_dma", "is_drain", "tag", "seq", "sem_key")

    def __init__(self, eng, fn, is_dma, tag, seq):
        self.eng, self.fn, self.is_dma, self.tag, self.seq = eng, fn, is_dma, tag, seq
        self.needs_inc = is_dma
        self.sem_val = None
        self.deps = []
        self.waits = []
        self.is_drain = False
        self.sem_key = eng


class _Sched:
    ENGS = ("pe", "v", "s", "g", "q")

    def __init__(self, nc):
        self.nc = nc
        self.sems = {e: nc.alloc_semaphore(f"sem_{e}") for e in self.ENGS}
        self.ops = {e: [] for e in self.ENGS}
        self.bufs = []
        self.last_barrier = {e: -1 for e in self.ENGS}
        self.last_dma = {}

    def _dma_sem(self, key):
        if key not in self.sems:
            self.sems[key] = self.nc.alloc_semaphore(f"sem_{key}")
        return key

    def buf(self, name, arena=None, lo=0, hi=0):
        b = _Buf(name, arena, lo, hi)
        self.bufs.append(b)
        return b

    def _partners(self, b):
        if b.arena is None:
            return ()
        return [o for o in self.bufs
                if o is not b and o.arena == b.arena and o.lo < b.hi and o.hi > b.lo]

    def op(self, eng, fn, reads=(), writes=(), tag="", inc=False):
        deps = []

        def add(d):
            if d is not None and d not in deps:
                deps.append(d)

        for b in reads:
            add(b.writer)
            for p in self._partners(b):
                add(p.writer)
        for b in writes:
            add(b.writer)
            for r in b.readers:
                add(r)
            for p in self._partners(b):
                add(p.writer)
                for r in p.readers:
                    add(r)
        o = _Op(eng, fn, eng in ("g", "q"), tag, len(self.ops[eng]))
        if o.is_dma:
            key = "dma_" + (writes[0].name if writes else "out")
            o.sem_key = self._dma_sem(key)
            prev = self.last_dma.get(key)
            if prev is not None:
                add(prev)
            self.last_dma[key] = o
        if inc:
            o.needs_inc = True
        need_drain = False
        for d in deps:
            if d is o:
                continue
            if d.is_dma:
                d.needs_inc = True
                o.deps.append(d)
            elif d.eng == eng:
                if eng == "pe":
                    continue
                if d.seq > self.last_barrier[eng]:
                    need_drain = True
            else:
                d.needs_inc = True
                o.deps.append(d)
        if need_drain:
            dr = _Op(eng, None, False, "drain", len(self.ops[eng]))
            dr.is_drain = True
            self.ops[eng].append(dr)
            self.last_barrier[eng] = dr.seq - 1
            o.seq = len(self.ops[eng])
        for b in reads:
            if o not in b.readers:
                b.readers.append(o)
        for b in writes:
            b.writer = o
            b.readers = []
        self.ops[eng].append(o)
        return o

    def finalize(self):
        cnt = {}
        for e in self.ENGS:
            for o in self.ops[e]:
                if o.needs_inc:
                    cnt[o.sem_key] = cnt.get(o.sem_key, 0) + (16 if o.is_dma else 1)
                    o.sem_val = cnt[o.sem_key]
        for e in self.ENGS:
            high = {}
            for o in self.ops[e]:
                ws = []
                for d in o.deps:
                    assert d.sem_val is not None, (e, o.tag, d.eng, d.tag)
                    if d.sem_val > high.get(d.sem_key, 0):
                        ws.append((d.sem_key, d.sem_val))
                        high[d.sem_key] = d.sem_val
                o.waits = ws

    def emit(self, eng_obj, ename):
        for o in self.ops[ename]:
            for (se, v) in o.waits:
                eng_obj.wait_ge(self.sems[se], v)
            if o.is_drain:
                eng_obj.drain()
                continue
            if o.fn is None:
                continue
            inst = o.fn(eng_obj)
            if o.needs_inc:
                inst.then_inc(self.sems[o.sem_key], 16 if o.is_dma else 1)


def build_nc():
    nc = bass.Bass("TRN2", target_bir_lowering=False)

    x_h = nc.dram_tensor("x", [D, TOK], F32, kind="ExternalInput")
    y_h = nc.dram_tensor("y", [D, TOK], F32, kind="ExternalOutput")
    pw_h = nc.dram_tensor("pwT", [D, D], F32, kind="ExternalInput").ap()
    f1_h = nc.dram_tensor("f1T", [D, H1], F32, kind="ExternalInput").ap()
    f2_h = nc.dram_tensor("f2m", [KC, 128, M1, 128], F32, kind="ExternalInput")
    g_h = nc.dram_tensor("gT", [D, H2], I8, kind="ExternalInput").ap()
    u_h = nc.dram_tensor("uT", [D, H2], I8, kind="ExternalInput").ap()
    d_h = nc.dram_tensor("dm", [KC, 128, M2, 128], I8, kind="ExternalInput")
    f2s_h = nc.dram_tensor("f2s", [KC, 128, M1, 128], BF16)
    ds_h = nc.dram_tensor("ds", [KC, 128, M2, 128], BF16)
    vf_h = nc.dram_tensor("vecf", [128, 112], F32, kind="ExternalInput").ap()
    vb_h = nc.dram_tensor("vecb", [128, 80], F32, kind="ExternalInput").ap()

    vecf = nc.alloc_sbuf_tensor("vecf_s", [128, 112], F32)
    vecb = nc.alloc_sbuf_tensor("vecb_s", [128, 80], BF16)
    cvec = nc.alloc_sbuf_tensor("cvec", [128, 24], F32)
    c1f = nc.alloc_sbuf_tensor("c1f", [128, 32], F32)
    ctmp = nc.alloc_sbuf_tensor("ctmp", [128, 32], F32)
    epsc = nc.alloc_sbuf_tensor("epsc", [128, 3], F32)
    warm = nc.alloc_sbuf_tensor("warm", [128, 1], F32)
    onec = nc.alloc_sbuf_tensor("onec", [128, 1], F32)
    avg1 = nc.alloc_sbuf_tensor("avg1", [128, 128], BF16)
    avg4 = nc.alloc_sbuf_tensor("avg4", [128, 128], BF16)
    x1r = nc.alloc_sbuf_tensor("x1r", [128, KC * TOK], BF16)
    rows = [nc.alloc_sbuf_tensor(f"rows{p}", [128, NROW * T], F32) for p in range(2)]
    oWp = 0
    oW1 = oWp + 8 * 1024
    oW2 = oW1 + 8 * 4096
    oX0 = oW2 + 3 * 4096
    oY = oX0 + 2 * 2048
    oXA = oY + 2048
    oZ = oXA + 2 * 2048
    oH = oZ + 2048
    oAO = oH + 8192
    oSC = oAO + 2048
    aA = oSC + 1024
    oWg = 0
    oWu = oWg + 8 * 3072
    oWd = oWu + 8 * 3072
    oZ2 = oWd + 4 * 3072
    oHF = oZ2 + 2 * 2048
    oSG = oHF + 6144
    oF = oSG + 2 * 512
    aB = oF + 2048
    WB = max(aA, aB)
    wb = nc.alloc_sbuf_tensor("wb", [128, WB], BF16)
    oT = 0
    oE = 2048
    oTR = oE + 512
    FB = 3072
    fb = nc.alloc_sbuf_tensor("fb", [128, FB], F32)

    P = [nc.alloc_psum_tensor(f"ps{i}", [128, 512], F32) for i in range(8)]

    sch = _Sched(nc)
    bf = sch.buf
    S_ = sch.op

    b_vecf = bf("vecf")
    b_vecb = bf("vecb")
    b_cvec = bf("cvec")
    b_c1f = [bf(f"c1f{g}") for g in range(4)]
    b_ctmp = [bf("ctmp_e"), bf("ctmp_o")]
    b_ctmp2 = [bf("ctmp2_e"), bf("ctmp2_o")]
    b_eps = bf("eps")
    b_avg1 = bf("avg1")
    b_avg4 = bf("avg4")
    b_x1 = [bf(f"x1_{i}") for i in range(NT)]
    b_rows = [[bf(f"rows{p}_{s}") for s in range(NROW)] for p in range(2)]

    W = "w"
    b_Wp = bf("Wp", W, oWp * 2, (oWp + 8 * 1024) * 2)
    b_W1 = [[bf(f"W1_{k}_{g}", W, (oW1 + k * 4096 + g * 1024) * 2,
                (oW1 + k * 4096 + (g + 1) * 1024) * 2) for g in range(4)]
            for k in range(KC)]
    b_W2 = [bf(f"W2_{j}", W, (oW2 + j * 4096) * 2, (oW2 + (j + 1) * 4096) * 2)
            for j in range(3)]
    b_X0 = [bf(f"X0_{p}", W, (oX0 + p * 2048) * 2, (oX0 + (p + 1) * 2048) * 2)
            for p in range(2)]
    b_Y = bf("Y", W, oY * 2, (oY + 2048) * 2)
    b_XA = [[bf(f"XA{p}_{c}", W, (oXA + p * 2048 + c * T) * 2,
                (oXA + p * 2048 + (c + 1) * T) * 2) for c in range(KC)]
            for p in range(2)]
    b_Z = bf("Z", W, oZ * 2, (oZ + 2048) * 2)
    b_H = [bf(f"H{gq}", W, (oH + gq * 2048) * 2, (oH + (gq + 1) * 2048) * 2)
           for gq in range(4)]
    b_AO = [bf(f"AO{c}", W, (oAO + c * T) * 2, (oAO + (c + 1) * T) * 2)
            for c in range(KC)]
    b_SC = bf("SC", W, oSC * 2, (oSC + 1024) * 2)
    b_CB = [bf(f"CB{s}", W, (oSC + s * 256) * 2, (oSC + (s + 1) * 256) * 2)
            for s in range(3)]
    b_Wg = [bf(f"Wg{k}", W, (oWg + k * 3072) * 2, (oWg + (k + 1) * 3072) * 2)
            for k in range(KC)]
    b_Wu = [bf(f"Wu{k}", W, (oWu + k * 3072) * 2, (oWu + (k + 1) * 3072) * 2)
            for k in range(KC)]
    b_Wd = [bf(f"Wd_{j}", W, (oWd + j * 3072) * 2, (oWd + (j + 1) * 3072) * 2)
            for j in range(4)]
    b_Z2 = [bf(f"Z2_{p}", W, (oZ2 + p * 2048) * 2, (oZ2 + (p + 1) * 2048) * 2)
            for p in range(2)]
    b_HF = bf("HF", W, oHF * 2, (oHF + 6144) * 2)
    b_SG = [bf(f"SG_{p}", W, (oSG + p * 512) * 2, (oSG + (p + 1) * 512) * 2)
            for p in range(2)]
    b_F = bf("F", W, oF * 2, (oF + 2048) * 2)
    Fa = "f"
    b_T = bf("Tsh", Fa, oT * 4, (oT + 2048) * 4)
    b_E = bf("Ero", Fa, oE * 4, (oE + 1024) * 4)
    b_trow = [bf(f"trow{t}", Fa, (oTR + t * 128) * 4, (oTR + (t + 1) * 128) * 4)
              for t in range(2)]
    b_trow8 = [bf(f"trow8{t}", Fa, (oE + t * 512) * 4, (oE + (t + 1) * 512) * 4)
               for t in range(2)]
    b_P = [bf(f"P{i}") for i in range(8)]
    b_f2s = [bf(f"f2s{j}") for j in range(KC)]
    b_ds = [bf(f"ds{j}") for j in range(KC)]
    b_warm = bf("warm")
    b_onec = bf("onec")

    def wsl(off, n):
        return bass.AP(wb, off, [[WB, 128], [1, n]])

    def wsl3(off, d1, n1, n2):
        return bass.AP(wb, off, [[WB, 128], [d1, n1], [1, n2]])

    def fsl(off, n):
        return bass.AP(fb, off, [[FB, 128], [1, n]])

    def fsl3(off, d1, n1, n2):
        return bass.AP(fb, off, [[FB, 128], [d1, n1], [1, n2]])

    def rsl(p, s):
        return bass.AP(rows[p], s * T, [[NROW * T, 128], [1, T]])

    def rrep(p, s, reps):
        return bass.AP(rows[p], s * T, [[NROW * T, 128], [0, reps], [1, T]])

    def wrep(off, reps):
        return bass.AP(wb, off, [[WB, 128], [0, reps], [1, T]])

    def x1sl(i):
        return bass.AP(x1r, i * T, [[KC * TOK, 128], [TOK, KC], [1, T]])

    def x1c(c, i):
        return bass.AP(x1r, c * TOK + i * T, [[KC * TOK, 128], [1, T]])

    def pap(b, n=T, off=0):
        return P[b][:, off:off + n]

    def col(t, c):
        return t[:, c:c + 1]

    S_("v", lambda e: e.memset(epsc.ap()[:, 0:1], EPS), writes=[b_eps])
    S_("v", lambda e: e.memset(epsc.ap()[:, 1:2], EPS2), writes=[b_eps])
    S_("v", lambda e: e.memset(epsc.ap()[:, 2:3], EPS * EPS), writes=[b_eps])
    S_("v", lambda e: e.memset(onec.ap(), 1.0), writes=[b_onec])
    S_("v", lambda e: e.memset(avg1.ap(), 1.0 / D), writes=[b_avg1])
    S_("v", lambda e: e.memset(avg4.ap(), 1.0 / H1), writes=[b_avg4])
    S_("q", lambda e: e.dma_start(out=vecf.ap(), in_=vf_h), writes=[b_vecf], tag="vecf")

    def x0dma(i):
        p = i % 2
        dst = wsl3(oX0 + p * 2048, T, KC, T)
        src = bass.AP(x_h, i * T, [[TOK, 128], [128 * TOK, KC], [1, T]])
        S_("g", lambda e: e.dma_start(out=dst, in_=src), writes=[b_X0[p]],
           tag=f"x0_{i}")

    x0dma(0)
    S_("g", lambda e: e.dma_start(out=vecb.ap(), in_=vb_h), writes=[b_vecb], tag="vecb")
    S_("g", lambda e: e.dma_start(out=wsl(oWp, 8 * 1024), in_=pw_h), writes=[b_Wp],
       tag="Wp")
    for gq in range(4):
        for k in range(KC):
            S_("g", lambda e, k=k, gq=gq: e.dma_start(
                out=wsl(oW1 + k * 4096 + gq * 1024, 1024),
                in_=f1_h[k * 128:(k + 1) * 128, gq * 1024:(gq + 1) * 1024]),
                writes=[b_W1[k][gq]], tag=f"W1k{k}g{gq}")
        if gq == 0:
            x0dma(1)

    def emit_staging():
        for j in range(KC):
            n = 128 * M1 * 128
            S_("g", lambda e, j=j, n=n: e.dma_start(
                out=bass.AP(f2s_h, j * n, [[0, 1], [1, n]]),
                in_=bass.AP(f2_h, j * n, [[0, 1], [1, n]])),
                writes=[b_f2s[j]], tag=f"stg2.{j}")
        for j in range(KC):
            n = 128 * M2 * 128
            S_("g", lambda e, j=j, n=n: e.dma_start(
                out=bass.AP(ds_h, j * n, [[0, 1], [1, n]]),
                in_=bass.AP(d_h, j * n, [[0, 1], [1, n]])),
                writes=[b_ds[j]], tag=f"stgd.{j}")

    def bias_const(woff, wklen, bcol0, nk, banks2, whalf, wbuf, dst_col0, tagp):
        for h in whalf:
            bank = banks2[h % 2]
            sl = h % 2
            for k in range(nk):
                S_("pe", lambda e, h=h, k=k, bank=bank: e.matmul(
                    P[bank][0:1, 0:512], col(vecb, bcol0 + k),
                    wsl(woff + k * wklen + h * 512, 512),
                    start=(k == 0), stop=(k == nk - 1)),
                    reads=[b_vecb, wbuf(k, h)], writes=[b_P[bank]],
                    tag=f"{tagp}m{h}.{k}")
            S_("v", lambda e, sl=sl, bank=bank: e.tensor_copy(
                bass.AP(fb, oE + sl * 512, [[FB, 1], [1, 512]]),
                P[bank][0:1, 0:512]),
                reads=[b_P[bank]], writes=[b_trow8[sl]], tag=f"{tagp}cp{h}")
            for j in range(4):
                cc = (h % 2) * 4 + j
                S_("pe", lambda e, sl=sl, j=j, cc=cc: e.matmul(
                    P[5][:, cc:cc + 1],
                    bass.AP(fb, oE + sl * 512 + j * 128, [[FB, 1], [1, 128]]),
                    onec.ap()[0:1, 0:1], start=True, stop=True),
                    reads=[b_trow8[sl], b_onec], writes=[b_P[5]],
                    tag=f"{tagp}tp{h}.{j}")
        S_("v", lambda e: e.tensor_copy(
            bass.AP(ctmp, dst_col0, [[32, 128], [1, 8]]), P[5][:, 0:8]),
            reads=[b_P[5]], writes=[b_ctmp[0], b_ctmp[1]], tag=f"{tagp}out")

    def startup_consts():
        bias_const(oWp, 1024, 0, KC, (0, 2), [0, 1], lambda k, h: b_Wp, 0,
                   "c2p.")
        S_("v", lambda e: e.tensor_add(cvec.ap()[:, 0:8],
                                       bass.AP(ctmp, 0, [[32, 128], [1, 8]]),
                                       vecf.ap()[:, 64:72]),
           reads=[b_ctmp[0], b_ctmp[1], b_vecf], writes=[b_cvec], tag="c2p.add")
        S_("v", lambda e: e.tensor_mul(cvec.ap()[:, 0:8], cvec.ap()[:, 0:8],
                                       vecf.ap()[:, 8:16]),
           reads=[b_cvec, b_vecf], writes=[b_cvec], tag="c3p.mul")
        for k in range(KC):
            S_("v", lambda e, k=k: e.tensor_scalar_mul(
                wsl(oWp + k * 1024, 1024), wsl(oWp + k * 1024, 1024), col(vecf, k)),
                reads=[b_Wp, b_vecf], writes=[b_Wp], tag=f"fold.p{k}")
        for gq in range(4):
            bias_const(oW1, 4096, 8, KC, (1, 3), [2 * gq, 2 * gq + 1],
                       lambda k, h: b_W1[k][h // 2], 8 * gq, "c1f.")
            S_("v", lambda e, gq=gq: e.tensor_add(
                c1f.ap()[:, gq * 8:(gq + 1) * 8],
                bass.AP(ctmp, gq * 8, [[32, 128], [1, 8]]),
                vecf.ap()[:, 72 + gq * 8:80 + gq * 8]),
                reads=[b_ctmp[0], b_ctmp[1], b_vecf], writes=[b_c1f[gq]],
                tag=f"c1f.add{gq}")
            for k in range(KC):
                S_("v", lambda e, k=k, gq=gq: e.tensor_scalar_mul(
                    wsl(oW1 + k * 4096 + gq * 1024, 1024),
                    wsl(oW1 + k * 4096 + gq * 1024, 1024),
                    col(vecf, 16 + k)),
                    reads=[b_W1[k][gq], b_vecf], writes=[b_W1[k][gq]],
                    tag=f"fold.1{k}.{gq}")

    def stats_sqv(src_chunk_ap, src_buf, tagp, sq_off, sq_buf):
        srcb = src_buf if isinstance(src_buf, list) else [src_buf] * KC
        for b0 in range(2):
            S_("v", lambda e, b0=b0: e.tensor_mul(
                wsl3(sq_off + b0 * 4 * T, T, 4, T), src_chunk_ap((b0 * 4, 4)),
                src_chunk_ap((b0 * 4, 4))),
                reads=srcb[b0 * 4:b0 * 4 + 4], writes=[sq_buf],
                tag=f"{tagp}sq{b0}")

    def stats_mms(src_chunk_ap, src_buf, banks, tagp, sq_off, sq_buf):
        bs, bq = banks
        bsb, bso = bs if isinstance(bs, tuple) else (bs, 0)
        bqb, bqo = bq if isinstance(bq, tuple) else (bq, 0)
        srcb = src_buf if isinstance(src_buf, list) else [src_buf] * KC
        for c in range(KC):
            S_("pe", lambda e, c=c: e.matmul(
                pap(bsb, T, bso), avg1.ap(), src_chunk_ap(c), start=(c == 0),
                stop=(c == KC - 1)),
                reads=[srcb[c], b_avg1], writes=[b_P[bsb]], tag=f"{tagp}s{c}")
        for cc in range(KC):
            S_("pe", lambda e, cc=cc: e.matmul(
                pap(bqb, T, bqo), avg1.ap(), wsl(sq_off + cc * T, T),
                start=(cc == 0), stop=(cc == KC - 1)),
                reads=[sq_buf, b_avg1], writes=[b_P[bqb]], tag=f"{tagp}q{cc}")

    def stats_1k(p, src_chunk_ap, src_buf, banks, tagp, sq_off=oSC, sq_buf=None):
        if sq_buf is None:
            sq_buf = b_SC
        bs, bq = banks
        bsb, bso = bs if isinstance(bs, tuple) else (bs, 0)
        bqb, bqo = bq if isinstance(bq, tuple) else (bq, 0)
        srcb = src_buf if isinstance(src_buf, list) else [src_buf] * KC
        for c in range(KC):
            S_("pe", lambda e, c=c: e.matmul(
                pap(bsb, T, bso), avg1.ap(), src_chunk_ap(c), start=(c == 0),
                stop=(c == KC - 1)),
                reads=[srcb[c], b_avg1], writes=[b_P[bsb]], tag=f"{tagp}s{c}")
        for b0 in range(2):
            S_("v", lambda e, b0=b0: e.tensor_mul(
                wsl3(sq_off, T, 4, T), src_chunk_ap((b0 * 4, 4)),
                src_chunk_ap((b0 * 4, 4))),
                reads=srcb[b0 * 4:b0 * 4 + 4], writes=[sq_buf],
                tag=f"{tagp}sq{b0}")
            for c in range(4):
                cc = b0 * 4 + c
                S_("pe", lambda e, c=c, cc=cc: e.matmul(
                    pap(bqb, T, bqo), avg1.ap(), wsl(sq_off + c * T, T),
                    start=(cc == 0), stop=(cc == KC - 1)),
                    reads=[sq_buf, b_avg1], writes=[b_P[bqb]], tag=f"{tagp}q{cc}")
        return bs, bq

    def ln_rows(p, mu_s, r_s, tmp_s, banks, eps_col=0, double=False, s_slot=None,
                tagp="", bcast=False, head_v=False):
        bs, bq = banks
        bsb, bso = bs if isinstance(bs, tuple) else (bs, 0)
        bqb, bqo = bq if isinstance(bq, tuple) else (bq, 0)
        v_s = tmp_s + 1
        if head_v:
            S_("v", lambda e: e.tensor_copy(rsl(p, mu_s), pap(bsb, T, bso)),
               reads=[b_P[bsb]], writes=[b_rows[p][mu_s]], tag=f"{tagp}mu")
            S_("v", lambda e: e.tensor_mul(rsl(p, tmp_s), rsl(p, mu_s),
                                           rsl(p, mu_s)),
               reads=[b_rows[p][mu_s]], writes=[b_rows[p][tmp_s]],
               tag=f"{tagp}musq")
        else:
            S_("s", lambda e: e.activation(out=rsl(p, mu_s), in_=pap(bsb, T, bso),
                                           func=AF.Identity),
               reads=[b_P[bsb]], writes=[b_rows[p][mu_s]], tag=f"{tagp}mu")
            S_("s", lambda e: e.activation(out=rsl(p, tmp_s), in_=rsl(p, mu_s),
                                           func=AF.Square),
               reads=[b_rows[p][mu_s]], writes=[b_rows[p][tmp_s]],
               tag=f"{tagp}musq")
        if bcast:
            eng = "v" if head_v else "s"
            if head_v:
                S_("v", lambda e: e.tensor_copy(wsl(oSC, T), rsl(p, mu_s)),
                   reads=[b_rows[p][mu_s]], writes=[b_CB[0]], tag=f"{tagp}mub")
            else:
                S_("s", lambda e: e.activation(out=wsl(oSC, T), in_=rsl(p, mu_s),
                                               func=AF.Identity),
                   reads=[b_rows[p][mu_s]], writes=[b_CB[0]], tag=f"{tagp}mub")
        S_("v", lambda e: e.scalar_tensor_tensor(
            rsl(p, v_s), rsl(p, tmp_s), -1.0, pap(bqb, T, bqo), ALU.mult, ALU.add),
           reads=[b_rows[p][tmp_s], b_P[bqb]], writes=[b_rows[p][v_s]],
           tag=f"{tagp}v")
        if double:
            S_("s", lambda e: e.activation(out=rsl(p, tmp_s), in_=rsl(p, v_s),
                                           func=AF.Ln, bias=col(epsc, 2),
                                           scale=1.0 + EPS),
               reads=[b_rows[p][v_s], b_eps], writes=[b_rows[p][tmp_s]],
               tag=f"{tagp}lns")
            if bcast:
                S_("s", lambda e: e.activation(out=wsl(oSC + 256, T),
                                               in_=rsl(p, tmp_s),
                                               func=AF.Exp, scale=-0.5),
                   reads=[b_rows[p][tmp_s]], writes=[b_CB[1]], tag=f"{tagp}exps")
            else:
                S_("s", lambda e: e.activation(out=rsl(p, s_slot), in_=rsl(p, tmp_s),
                                               func=AF.Exp, scale=-0.5),
                   reads=[b_rows[p][tmp_s]], writes=[b_rows[p][s_slot]],
                   tag=f"{tagp}exps")
        S_("s", lambda e: e.activation(out=rsl(p, tmp_s), in_=rsl(p, v_s), func=AF.Ln,
                                       bias=col(epsc, eps_col), scale=1.0),
           reads=[b_rows[p][v_s], b_eps], writes=[b_rows[p][tmp_s]], tag=f"{tagp}ln")
        if bcast:
            S_("s", lambda e: e.activation(out=wsl(oSC + 512, T), in_=rsl(p, tmp_s),
                                           func=AF.Exp, scale=-0.5),
               reads=[b_rows[p][tmp_s]], writes=[b_CB[2]], tag=f"{tagp}exp")
        else:
            S_("s", lambda e: e.activation(out=rsl(p, r_s), in_=rsl(p, tmp_s),
                                           func=AF.Exp, scale=-0.5),
               reads=[b_rows[p][tmp_s]], writes=[b_rows[p][r_s]], tag=f"{tagp}exp")

    def prewarm_ln(tag):
        S_("s", lambda e: e.activation(out=warm.ap(), in_=col(epsc, 0), func=AF.Ln,
                                       bias=col(epsc, 0), scale=1.0),
           reads=[b_eps], writes=[b_warm], tag=tag)

    def a_src(off):
        def f(c):
            if isinstance(c, tuple):
                return wsl3(off + c[0] * T, T, c[1], T)
            return wsl(off + c * T, T)
        return f

    def headA_stats(i):
        p = i % 2
        b0 = 4 if p == 0 else 6
        stats_1k(p, a_src(oX0 + p * 2048), b_X0[p], ((b0, 0), (b0, 256)),
                 f"A{i}.l0.")

    def headA_rows(i):
        p = i % 2
        b0 = 4 if p == 0 else 6
        ln_rows(p, 0, 1, 3, ((b0, 0), (b0, 256)), double=True, s_slot=2,
                tagp=f"A{i}.l0.", bcast=True, head_v=True)
        S_("v", lambda e: e.tensor_sub(wsl3(oZ, T, KC, T),
                                       wsl3(oX0 + p * 2048, T, KC, T),
                                       wrep(oSC, KC)),
           reads=[b_X0[p], b_CB[0]], writes=[b_Z], tag=f"A{i}.t")
        S_("v", lambda e: e.tensor_mul(wsl3(oY, T, KC, T), wsl3(oZ, T, KC, T),
                                       wrep(oSC + 256, KC)),
           reads=[b_Z, b_CB[1]], writes=[b_Y], tag=f"A{i}.y")
        S_("v", lambda e: e.tensor_mul(wsl3(oZ, T, KC, T), wsl3(oZ, T, KC, T),
                                       wrep(oSC + 512, KC)),
           reads=[b_Z, b_CB[2]], writes=[b_Z], tag=f"A{i}.yin")
        for c in range(KC):
            S_("v", lambda e, c=c: e.tensor_scalar(
                wsl(oZ + c * T, T), wsl(oZ + c * T, T), 1.0, col(cvec, c),
                ALU.mult, ALU.add),
                reads=[b_Z, b_cvec], writes=[b_Z], tag=f"A{i}.yc{c}")
        S_("v", lambda e: e.tensor_copy(
            bass.AP(x1r, i * T, [[KC * TOK, 128], [TOK, KC], [1, T]]),
            wsl3(oX0 + p * 2048, T, KC, T)),
           reads=[b_X0[p]], writes=[b_x1[i]], tag=f"A{i}.x1p")

    def lnh_tail(j):
        pj = j % 2
        bkj = (4, 5) if pj == 0 else (6, 7)
        off, buf = oX0 + pj * 2048, b_X0[pj]
        for c in range(24, 32):
            S_("pe", lambda e, c=c: e.matmul(
                pap(bkj[0]), avg4.ap(), wsl(oH + c * T, T), start=False,
                stop=(c == M1 - 1)),
                reads=[b_H[3], b_avg4], writes=[b_P[bkj[0]]], tag=f"A{j}.lh.s{c}")
        for b0 in range(2):
            cc0 = 24 + b0 * 4
            S_("v", lambda e, cc0=cc0, off=off, b0=b0: e.tensor_mul(
                wsl3(off + b0 * 4 * T, T, 4, T), wsl3(oH + cc0 * T, T, 4, T),
                wsl3(oH + cc0 * T, T, 4, T)),
                reads=[b_H[3]], writes=[buf], tag=f"A{j}.hsq{cc0}")
        for c in range(24, 32):
            S_("v", lambda e, c=c: e.tensor_scalar_mul(
                wsl(oH + c * T, T), wsl(oH + c * T, T), col(vecf, 32 + c)),
                reads=[b_H[3], b_vecf], writes=[b_H[3]], tag=f"A{j}.hg{c}")
    def lnh_tail2(j):
        pj = j % 2
        bkj = (4, 5) if pj == 0 else (6, 7)
        off, buf = oX0 + pj * 2048, b_X0[pj]
        for c in range(8):
            cc = 24 + c
            S_("pe", lambda e, c=c, cc=cc, off=off, bkj=bkj: e.matmul(
                pap(bkj[1]), avg4.ap(), wsl(off + c * T, T),
                start=False, stop=(cc == M1 - 1)),
                reads=[buf, b_avg4], writes=[b_P[bkj[1]]], tag=f"A{j}.lh.q{cc}")
        ln_rows(pj, 2, 4, 3, bkj, tagp=f"A{j}.lh.")
        S_("v", lambda e, pj=pj: e.tensor_mul(rsl(pj, 3), rsl(pj, 2), rsl(pj, 4)),
           reads=[b_rows[pj][2], b_rows[pj][4]], writes=[b_rows[pj][3]],
           tag=f"A{j}.mrh")
        S_("v", lambda e, pj=pj: e.tensor_scalar_mul(rsl(pj, 4), rsl(pj, 4), -1.0),
           reads=[b_rows[pj][4]], writes=[b_rows[pj][4]], tag=f"A{j}.nrh")

    def chainA_stats(i):
        p = i % 2
        b1 = 5 if p == 0 else 7
        stats_1k(p, a_src(oAO), b_AO, ((b1, 0), (b1, 256)), f"A{i}.l3.")

    def chainA_lnrows(i):
        p = i % 2
        b1 = 5 if p == 0 else 7
        ln_rows(p, 0, 1, 3, ((b1, 0), (b1, 256)), tagp=f"A{i}.l3.")

    def chainA_sheets(i):
        p = i % 2
        for hh in range(2):
            xsl = bass.AP(x1r, (hh * 4) * TOK + i * T,
                          [[KC * TOK, 128], [TOK, 4], [1, T]])
            S_("v", lambda e, hh=hh: e.tensor_sub(
                wsl3(oSC, T, 4, T), wsl3(oAO + hh * 4 * T, T, 4, T), rrep(p, 0, 4)),
                reads=b_AO[hh * 4:hh * 4 + 4] + [b_rows[p][0]], writes=[b_SC],
                tag=f"A{i}.t3{hh}")
            S_("v", lambda e, hh=hh: e.tensor_mul(
                wsl3(oSC, T, 4, T), wsl3(oSC, T, 4, T), rrep(p, 1, 4)),
                reads=[b_SC, b_rows[p][1]], writes=[b_SC], tag=f"A{i}.t3r{hh}")
            S_("v", lambda e, hh=hh, xsl=xsl: e.tensor_add(
                xsl, wsl3(oSC, T, 4, T), xsl),
                reads=[b_SC, b_x1[i]], writes=[b_x1[i]], tag=f"A{i}.x1{hh}")

    w2_cnt = [0]

    def w2dma(g):
        sb = g % 3
        m = g % KC
        dst = wsl3(oW2 + sb * 4096, 128, M1, 128)
        src = bass.AP(f2s_h, m * 128 * M1 * 128,
                      [[M1 * 128, 128], [128, M1], [1, 128]])
        S_("q", lambda e: e.dma_start(out=dst, in_=src), reads=[b_f2s[m]],
           writes=[b_W2[sb]], tag=f"W2g{g}")

    def ensure_w2(g):
        while w2_cnt[0] <= g:
            w2dma(w2_cnt[0])
            w2_cnt[0] += 1

    def fc2A(j, mps=None):
        pj = j % 2
        bkj = (4, 5) if pj == 0 else (6, 7)
        for mp in (range(KC // 2) if mps is None else mps):
            for half in range(2):
                m = 2 * mp + half
                gidx = j * KC + m
                ensure_w2(gidx)
                sb = gidx % 3
                mb = m % 4
                for k in range(M1):
                    S_("pe", lambda e, m=m, k=k, mb=mb, sb=sb: e.matmul(
                        pap(mb), wsl(oW2 + sb * 4096 + k * 128, 128),
                        wsl(oH + k * T, T),
                        start=(k == 0), stop=(k == M1 - 1)),
                        reads=[b_W2[sb], b_H[k // 8]], writes=[b_P[mb]],
                        tag=f"A{j}.f2.{m}.{k}")
                if j == 0:
                    cb = bkj[0]
                    for k in range(M1):
                        S_("pe", lambda e, k=k, sb=sb: e.matmul(
                            P[cb][:, 256:258],
                            wsl(oW2 + sb * 4096 + k * 128, 128),
                            vecb.ap()[:, 16 + k:49 + k:32],
                            start=(k == 0), stop=(k == M1 - 1)),
                            reads=[b_vecb, b_W2[sb]], writes=[b_P[cb]],
                            tag=f"c2f.{m}.{k}")
                    S_("v", lambda e, m=m: e.tensor_add(
                        col(cvec, 8 + m), P[cb][:, 256:257], col(vecf, 104 + m)),
                        reads=[b_P[cb], b_vecf], writes=[b_cvec],
                        tag=f"c2f.add{m}")
                    S_("v", lambda e, m=m: e.tensor_mul(
                        col(cvec, 8 + m), col(cvec, 8 + m), col(vecf, 24 + m)),
                        reads=[b_cvec, b_vecf], writes=[b_cvec], tag=f"c3f.mul{m}")
                    S_("v", lambda e, m=m: e.tensor_copy(
                        bass.AP(cvec, 16 + m, [[24, 128], [1, 1]]), P[cb][:, 257:258]),
                        reads=[b_P[cb]], writes=[b_cvec], tag=f"wgh.sc{m}")
                if gidx + 3 < NT * KC:
                    ensure_w2(gidx + 3)
            for half in range(2):
                m = 2 * mp + half
                S_("v", lambda e, m=m, mb=m % 4: e.scalar_tensor_tensor(
                    fsl(oE + (m % 2) * 256, T), rsl(pj, 3), col(cvec, 16 + m),
                    pap(mb), ALU.mult, ALU.subtract),
                    reads=[b_rows[pj][3], b_cvec, b_P[m % 4]], writes=[b_E],
                    tag=f"A{j}.f2v{m}")
            for half in range(2):
                m = 2 * mp + half
                S_("v", lambda e, m=m: e.tensor_mul(
                    fsl(oE + (2 + m % 2) * 256, T), fsl(oE + (m % 2) * 256, T),
                    rsl(pj, 4)),
                    reads=[b_E, b_rows[pj][4]], writes=[b_E], tag=f"A{j}.f2t{m}")
            for half in range(2):
                m = 2 * mp + half
                S_("s", lambda e, m=m: e.activation(
                    out=fsl(oE + (m % 2) * 256, T), in_=fsl(oE + (2 + m % 2) * 256, T),
                    func=AF.Identity, bias=col(cvec, 8 + m), scale=col(vecf, 24 + m)),
                    reads=[b_E, b_cvec, b_vecf], writes=[b_E], tag=f"A{j}.f2e{m}")
            for half in range(2):
                m = 2 * mp + half
                S_("v", lambda e, m=m: e.tensor_add(
                    wsl(oAO + m * T, T), fsl(oE + (m % 2) * 256, T),
                    wsl(oXA + pj * 2048 + m * T, T)),
                    reads=[b_E, b_XA[pj][m]], writes=[b_AO[m]], tag=f"A{j}.ao{m}")

    def bodyA(i):
        p = i % 2
        bk = (4, 5) if p == 0 else (6, 7)
        for m in range(KC):
            mb = m % 4
            for k in range(KC):
                S_("pe", lambda e, m=m, k=k, mb=mb: e.matmul(
                    pap(mb), wsl(oWp + k * 1024 + m * 128, 128),
                    wsl(oY + k * T, T),
                    start=(k == 0), stop=(k == KC - 1)),
                    reads=[b_Wp, b_Y], writes=[b_P[mb]], tag=f"A{i}.pj{m}.{k}")
            S_("v", lambda e, m=m, mb=mb: e.scalar_tensor_tensor(
                wsl(oXA + p * 2048 + m * T, T), pap(mb), col(vecf, 8 + m),
                wsl(oZ + m * T, T), ALU.mult, ALU.add),
                reads=[b_P[mb], b_vecf, b_Z], writes=[b_XA[p][m]], tag=f"A{i}.xa{m}")
        stats_sqv(a_src(oXA + p * 2048), b_XA[p], f"A{i}.l2.", oZ, b_Z)
        if i > 0:
            lnh_tail(i - 1)
        if i > 1:
            chainA_sheets(i - 2)
        if i > 0:
            fc2A(i - 1, [0])
            lnh_tail2(i - 1)
            fc2A(i - 1, [1, 2, 3])
            stats_sqv(a_src(oAO), b_AO, f"A{i - 1}.l3.", oY, b_Y)
        stats_mms(a_src(oXA + p * 2048), b_XA[p], bk, f"A{i}.l2.", oZ, b_Z)
        ln_rows(p, 5, 2, 3, bk, tagp=f"A{i}.l2.")
        S_("v", lambda e: e.tensor_sub(fsl3(oT, T, KC, T),
                                       wsl3(oXA + p * 2048, T, KC, T), rrep(p, 5, KC)),
           reads=b_XA[p] + [b_rows[p][5]], writes=[b_T], tag=f"A{i}.t2")
        S_("v", lambda e: e.tensor_mul(wsl3(oZ, T, KC, T), fsl3(oT, T, KC, T),
                                       rrep(p, 2, KC)),
           reads=[b_T, b_rows[p][2]], writes=[b_Z], tag=f"A{i}.z")
        if i > 0 and i + 1 < NT:
            x0dma(i + 1)
        def sqreg(g):
            if g % 2 == 0:
                return oY, b_Y
            return oX0 + p * 2048, b_X0[p]

        def lnh_sum(g):
            c0 = g * 8
            for c in range(c0, c0 + 8):
                S_("pe", lambda e, c=c: e.matmul(
                    pap(bk[0]), avg4.ap(), wsl(oH + c * T, T), start=(c == 0),
                    stop=(c == M1 - 1)),
                    reads=[b_H[c // 8], b_avg4], writes=[b_P[bk[0]]],
                    tag=f"A{i}.lh.s{c}")

        def lnh_sqv(g, hg=True):
            off, buf = sqreg(g)
            c0 = g * 8
            for b0 in range(2):
                cc0 = c0 + b0 * 4
                S_("v", lambda e, cc0=cc0, off=off, b0=b0: e.tensor_mul(
                    wsl3(off + b0 * 4 * T, T, 4, T), wsl3(oH + cc0 * T, T, 4, T),
                    wsl3(oH + cc0 * T, T, 4, T)),
                    reads=[b_H[cc0 // 8]], writes=[buf], tag=f"A{i}.hsq{cc0}")
            if hg:
                lnh_hg(g)

        def lnh_hg(g):
            for c in range(g * 8, g * 8 + 8):
                S_("v", lambda e, c=c: e.tensor_scalar_mul(
                    wsl(oH + c * T, T), wsl(oH + c * T, T), col(vecf, 32 + c)),
                    reads=[b_H[c // 8], b_vecf], writes=[b_H[c // 8]],
                    tag=f"A{i}.hg{c}")

        def lnh_sqmm(g):
            off, buf = sqreg(g)
            for c in range(8):
                cc = g * 8 + c
                S_("pe", lambda e, c=c, cc=cc, off=off: e.matmul(
                    pap(bk[1]), avg4.ap(), wsl(off + c * T, T),
                    start=(cc == 0), stop=(cc == M1 - 1)),
                    reads=[buf, b_avg4], writes=[b_P[bk[1]]],
                    tag=f"A{i}.lh.q{cc}")

        for g in range(4):
            for m in range(g * 8, g * 8 + 8):
                mb = m % 4
                for k in range(KC):
                    S_("pe", lambda e, m=m, k=k, mb=mb: e.matmul(
                        pap(mb), wsl(oW1 + k * 4096 + m * 128, 128),
                        wsl(oZ + k * T, T),
                        start=(k == 0), stop=(k == KC - 1)),
                        reads=[b_W1[k][m // 8], b_Z], writes=[b_P[mb]],
                        tag=f"A{i}.f1.{m}.{k}")
                S_("s", lambda e, m=m, mb=mb: e.activation(
                    out=wsl(oH + m * T, T), in_=pap(mb), func=AF.Gelu,
                    bias=col(c1f, m), scale=1.0),
                    reads=[b_P[mb], b_c1f[m // 8]], writes=[b_H[m // 8]],
                    tag=f"A{i}.gel{m}")
            if g == 1 and i > 0:
                p1 = (i - 1) % 2
                b1 = 5 if p1 == 0 else 7
                stats_mms(a_src(oAO), b_AO, ((b1, 0), (b1, 256)),
                          f"A{i - 1}.l3.", oY, b_Y)
            if g > 0:
                lnh_sum(g - 1)
                lnh_sqv(g - 1)
            if g > 1:
                lnh_sqmm(g - 2)
            if g == 0:
                ensure_w2(i * KC + 2)
            if g == 2 and i + 1 < NT:
                headA_stats(i + 1)
        lnh_sqmm(2)
        prewarm_ln(f"A{i}.warm")
        if i + 1 < NT:
            headA_rows(i + 1)
        if i > 0:
            chainA_lnrows(i - 1)

    def headB_sq(i):
        for b0 in range(2):
            S_("v", lambda e, b0=b0, i=i: e.tensor_mul(
                wsl3(oF + b0 * 4 * T, T, 4, T),
                bass.AP(x1r, b0 * 4 * TOK + i * T, [[KC * TOK, 128], [TOK, 4], [1, T]]),
                bass.AP(x1r, b0 * 4 * TOK + i * T, [[KC * TOK, 128], [TOK, 4], [1, T]])),
                reads=[b_x1[i]], writes=[b_F], tag=f"B{i}.z2sq{b0}")

    def headB(i):
        p = i % 2
        bs, bq = 6, 7
        for c in range(KC):
            S_("pe", lambda e, c=c: e.matmul(
                pap(bs), avg1.ap(), x1c(c, i), start=(c == 0), stop=(c == KC - 1)),
                reads=[b_x1[i], b_avg1], writes=[b_P[bs]], tag=f"B{i}.z2s{c}")
        for cc in range(KC):
            S_("pe", lambda e, cc=cc: e.matmul(
                pap(bq), avg1.ap(), wsl(oF + cc * T, T), start=(cc == 0),
                stop=(cc == KC - 1)),
                reads=[b_F, b_avg1], writes=[b_P[bq]], tag=f"B{i}.z2q{cc}")
        ln_rows(p, 0, 1, 3, (bs, bq), tagp=f"B{i}.z2.")
        S_("v", lambda e: e.tensor_sub(fsl3(oT, T, KC, T), x1sl(i), rrep(p, 0, KC)),
           reads=[b_x1[i], b_rows[p][0]], writes=[b_T], tag=f"B{i}.tz")
        S_("v", lambda e: e.tensor_mul(wsl3(oZ2 + p * 2048, T, KC, T),
                                       fsl3(oT, T, KC, T), rrep(p, 1, KC)),
           reads=[b_T, b_rows[p][1]], writes=[b_Z2[p]], tag=f"B{i}.z2n")

    wd_cnt = [0]

    def wddma(g):
        sb = g % 4
        m = g % KC
        dst = wsl3(oWd + sb * 3072, 128, M2, 128)
        src = bass.AP(ds_h, m * 128 * M2 * 128,
                      [[M2 * 128, 128], [128, M2], [1, 128]])
        S_("q", lambda e: e.dma_start(out=dst, in_=src), reads=[b_ds[m]],
           writes=[b_Wd[sb]], tag=f"Wdg{g}")

    def ensure_wd(g):
        while wd_cnt[0] <= g:
            wddma(wd_cnt[0])
            wd_cnt[0] += 1

    def bodyB(i):
        p = i % 2

        def lf_pair(mp):
            for half in range(2):
                c = 2 * mp + half
                S_("pe", lambda e, c=c: e.matmul(
                    pap(3), avg1.ap(), wsl(oF + c * T, T), start=(c == 0),
                    stop=(c == KC - 1)),
                    reads=[b_F, b_avg1], writes=[b_P[3]], tag=f"B{i}.lf.s{c}")
            for half in range(2):
                c = 2 * mp + half
                S_("pe", lambda e, c=c: e.matmul(
                    pap(4), avg1.ap(), wsl(oZ2 + p * 2048 + c * T, T),
                    start=(c == 0), stop=(c == KC - 1)),
                    reads=[b_Z2[p], b_avg1], writes=[b_P[4]], tag=f"B{i}.lf.q{c}")

        if i + 1 < NT:
            headB_sq(i + 1)
        for mp in range(M2 // 2):
            m0 = 2 * mp
            gb = mp % 3
            ub = 3 + mp % 3
            for half in range(2):
                m = m0 + half
                for k in range(KC):
                    S_("pe", lambda e, m=m, k=k, gb=gb, half=half: e.matmul(
                        pap(gb, T, half * 256),
                        wsl(oWg + k * 3072 + m * 128, 128),
                        wsl(oZ2 + p * 2048 + k * T, T), start=(k == 0),
                        stop=(k == KC - 1)),
                        reads=[b_Wg[k], b_Z2[p]], writes=[b_P[gb]],
                        tag=f"B{i}.g{m}.{k}")
            for half in range(2):
                m = m0 + half
                for k in range(KC):
                    S_("pe", lambda e, m=m, k=k, ub=ub, half=half: e.matmul(
                        pap(ub, T, half * 256),
                        wsl(oWu + k * 3072 + m * 128, 128),
                        wsl(oZ2 + p * 2048 + k * T, T), start=(k == 0),
                        stop=(k == KC - 1)),
                        reads=[b_Wu[k], b_Z2[p]], writes=[b_P[ub]],
                        tag=f"B{i}.u{m}.{k}")
            sg = mp % 2
            S_("s", lambda e, gb=gb, sg=sg: e.activation(
                out=wsl(oSG + sg * 512, 512), in_=pap(gb, 512), func=AF.Silu,
                scale=S127),
                reads=[b_P[gb]], writes=[b_SG[sg]], tag=f"B{i}.si{mp}")
            S_("v", lambda e, m0=m0, ub=ub, sg=sg: e.scalar_tensor_tensor(
                wsl(oHF + m0 * T, 512), pap(ub, 512), S127, wsl(oSG + sg * 512, 512),
                ALU.mult, ALU.mult),
                reads=[b_P[ub], b_SG[sg]], writes=[b_HF], tag=f"B{i}.hf{mp}")
        prewarm_ln(f"B{i}.warm")
        if i + 1 < NT:
            headB(i + 1)
        ensure_wd(i * KC + 3)
        for mp in range(KC // 2):
            m0 = 2 * mp
            dbk = mp % 3
            for half in range(2):
                m = m0 + half
                gidx = i * KC + m
                ensure_wd(gidx)
                sb = gidx % 4
                for k in range(M2):
                    S_("pe", lambda e, m=m, k=k, dbk=dbk, sb=sb, half=half: e.matmul(
                        pap(dbk, T, half * 256),
                        wsl(oWd + sb * 3072 + k * 128, 128), wsl(oHF + k * T, T),
                        start=(k == 0), stop=(k == M2 - 1)),
                        reads=[b_Wd[sb], b_HF], writes=[b_P[dbk]], tag=f"B{i}.d{m}.{k}")
                if gidx + 3 < NT * KC:
                    ensure_wd(gidx + 3)
            S_("s", lambda e, dbk=dbk, m0=m0: e.activation(
                out=wsl(oF + m0 * T, 512), in_=pap(dbk, 512), func=AF.Identity),
                reads=[b_P[dbk]], writes=[b_F], tag=f"B{i}.fe{mp}")
            S_("v", lambda e, m0=m0: e.tensor_mul(
                wsl(oZ2 + p * 2048 + m0 * T, 512), wsl(oF + m0 * T, 512),
                wsl(oF + m0 * T, 512)),
                reads=[b_F], writes=[b_Z2[p]], tag=f"B{i}.fsq{mp}")
            if mp > 0:
                lf_pair(mp - 1)
        lf_pair(KC // 2 - 1)
        ln_rows(p, 5, 2, 3, (3, 4), eps_col=1, tagp=f"B{i}.lf.")
        S_("v", lambda e: e.tensor_sub(fsl3(oT, T, KC, T), wsl3(oF, T, KC, T),
                                       rrep(p, 5, KC)),
           reads=[b_F, b_rows[p][5]], writes=[b_T], tag=f"B{i}.tf")
        S_("v", lambda e: e.tensor_mul(fsl3(oT, T, KC, T), fsl3(oT, T, KC, T),
                                       rrep(p, 2, KC)),
           reads=[b_T, b_rows[p][2]], writes=[b_T], tag=f"B{i}.tfr")
        for hh in range(2):
            S_("v", lambda e, hh=hh: e.tensor_add(
                fsl3(oE, T, 4, T), fsl3(oT + hh * 4 * T, T, 4, T),
                bass.AP(x1r, (hh * 4) * TOK + i * T,
                        [[KC * TOK, 128], [TOK, 4], [1, T]])),
                reads=[b_T, b_x1[i]], writes=[b_E], tag=f"B{i}.out{hh}")
            dst = bass.AP(y_h, (hh * 4 * 128) * TOK + i * T,
                          [[TOK, 128], [128 * TOK, 4], [1, T]])
            S_("q", lambda e, hh=hh, dst=dst: e.dma_start(out=dst,
                                                          in_=fsl3(oE, T, 4, T)),
               reads=[b_E], tag=f"B{i}.st{hh}")

    headA_stats(0)
    startup_consts()
    emit_staging()
    headA_rows(0)
    for i in range(NT):
        bodyA(i)
    for k in range(KC):
        S_("g", lambda e, k=k: e.dma_start(
            out=wsl(oWg + k * 3072, 3072), in_=g_h[k * 128:(k + 1) * 128, :]),
            writes=[b_Wg[k]], tag=f"Wgk{k}")
    chainA_sheets(NT - 2)
    lnh_tail(NT - 1)
    fc2A(NT - 1, [0])
    lnh_tail2(NT - 1)
    fc2A(NT - 1, [1, 2, 3])
    for k in range(KC):
        S_("g", lambda e, k=k: e.dma_start(
            out=wsl(oWu + k * 3072, 3072), in_=u_h[k * 128:(k + 1) * 128, :]),
            writes=[b_Wu[k]], tag=f"Wuk{k}")
    stats_sqv(a_src(oAO), b_AO, f"A{NT - 1}.l3.", oZ, b_Z)
    stats_mms(a_src(oAO), b_AO, ((7, 0), (7, 256)), f"A{NT - 1}.l3.", oZ, b_Z)
    chainA_lnrows(NT - 1)
    chainA_sheets(NT - 1)
    ensure_wd(3)
    headB_sq(0)
    headB(0)
    for i in range(NT):
        bodyB(i)

    last_store = [o for o in sch.ops["q"]
                  if o.tag.startswith("B") and ".st" in o.tag][-1]
    last_store.needs_inc = True
    fin = _Op("q", None, False, "final", len(sch.ops["q"]))
    fin.deps = [last_store]
    sch.ops["q"].append(fin)

    sch.finalize()

    with nc.Block() as block:
        @block.tensor
        def _(eng):
            sch.emit(eng, "pe")

        @block.vector
        def _(eng):
            sch.emit(eng, "v")

        @block.scalar
        def _(eng):
            sch.emit(eng, "s")

        @block.gpsimd
        def _(eng):
            sch.emit(eng, "g")

        @block.sync
        def _(eng):
            sch.emit(eng, "q")
    return nc


_NC_CACHE = {}


def _get_nc():
    if "nc" not in _NC_CACHE:
        _NC_CACHE["nc"] = build_nc()
    return _NC_CACHE["nc"]


def _pack_cols(vs):
    outs = []
    for v in vs:
        outs.append(np.ascontiguousarray(np.asarray(v, np.float32).reshape(-1, 128).T))
    return np.ascontiguousarray(np.concatenate(outs, axis=1))


def kernel(**inp):
    X = np.asarray(inp["L0_attn_x"], np.float32).reshape(16384, 1024)
    pwT = np.ascontiguousarray(np.asarray(inp["L0_attn_proj_w"], np.float32).T)
    f1T = np.ascontiguousarray(np.asarray(inp["L0_attn_fc1_w"], np.float32).T)
    f2T = np.asarray(inp["L0_attn_fc2_w"], np.float32).T
    f2m = np.ascontiguousarray(
        f2T.reshape(M1, 128, KC, 128).transpose(2, 1, 0, 3))
    gT = np.ascontiguousarray(np.asarray(inp["L0_ffn_w_gate"]).T.astype(np.int8))
    uT = np.ascontiguousarray(np.asarray(inp["L0_ffn_w_up"]).T.astype(np.int8))
    dT = np.asarray(inp["L0_ffn_w_down"]).T.astype(np.int8)
    dm = np.ascontiguousarray(dT.reshape(M2, 128, KC, 128).transpose(2, 1, 0, 3))
    vecf = _pack_cols([inp["L0_attn_ln1_g"], inp["L0_attn_gamma1"],
                       inp["L0_attn_ln2_g"], inp["L0_attn_gamma2"],
                       inp["L0_attn_lnh_g"], inp["L0_attn_proj_b"],
                       inp["L0_attn_fc1_b"], inp["L0_attn_fc2_b"]])
    vecb = _pack_cols([inp["L0_attn_ln1_b"], inp["L0_attn_ln2_b"],
                       inp["L0_attn_lnh_b"], inp["L0_attn_lnh_g"]])
    nc = _get_nc()
    in_maps = []
    for c in range(8):
        xs = np.ascontiguousarray(X[c * TOK:(c + 1) * TOK].T)
        in_maps.append({"x": xs, "pwT": pwT, "f1T": f1T, "f2m": f2m,
                        "gT": gT, "uT": uT, "dm": dm, "vecf": vecf, "vecb": vecb})
    res = run_bass_kernel_spmd(nc, in_maps, core_ids=list(range(8)))
    out = np.concatenate([np.asarray(r["y"], np.float32).T for r in res.results],
                         axis=0)
    return np.ascontiguousarray(out.reshape(4, 4096, 1024))


# version 29
# speedup vs baseline: 1.1210x; 1.0234x over previous
import numpy as np
import concourse.bass as bass
import concourse.mybir as mybir
from concourse.bass_utils import run_bass_kernel_spmd

F32 = mybir.dt.float32
BF16 = mybir.dt.bfloat16
I8 = mybir.dt.int8
AF = mybir.ActivationFunctionType
ALU = mybir.AluOpType

EPS = 1e-5
S127 = 1.0 / 127.0
EPS2 = EPS * 127.0 * 127.0

D = 1024
H1 = 4096
H2 = 3072
KC = 8
M1 = 32
M2 = 24
T = 256
NT = 8
TOK = T * NT
NROW = 6


class _Buf:
    __slots__ = ("name", "arena", "lo", "hi", "writer", "readers")

    def __init__(self, name, arena=None, lo=0, hi=0):
        self.name, self.arena, self.lo, self.hi = name, arena, lo, hi
        self.writer = None
        self.readers = []


class _Op:
    __slots__ = ("eng", "fn", "needs_inc", "sem_val", "deps", "waits",
                 "is_dma", "is_drain", "tag", "seq", "sem_key")

    def __init__(self, eng, fn, is_dma, tag, seq):
        self.eng, self.fn, self.is_dma, self.tag, self.seq = eng, fn, is_dma, tag, seq
        self.needs_inc = is_dma
        self.sem_val = None
        self.deps = []
        self.waits = []
        self.is_drain = False
        self.sem_key = eng


class _Sched:
    ENGS = ("pe", "v", "s", "g", "q")

    def __init__(self, nc):
        self.nc = nc
        self.sems = {e: nc.alloc_semaphore(f"sem_{e}") for e in self.ENGS}
        self.ops = {e: [] for e in self.ENGS}
        self.bufs = []
        self.last_barrier = {e: -1 for e in self.ENGS}
        self.last_dma = {}

    def _dma_sem(self, key):
        if key not in self.sems:
            self.sems[key] = self.nc.alloc_semaphore(f"sem_{key}")
        return key

    def buf(self, name, arena=None, lo=0, hi=0):
        b = _Buf(name, arena, lo, hi)
        self.bufs.append(b)
        return b

    def _partners(self, b):
        if b.arena is None:
            return ()
        return [o for o in self.bufs
                if o is not b and o.arena == b.arena and o.lo < b.hi and o.hi > b.lo]

    def op(self, eng, fn, reads=(), writes=(), tag="", inc=False):
        deps = []

        def add(d):
            if d is not None and d not in deps:
                deps.append(d)

        for b in reads:
            add(b.writer)
            for p in self._partners(b):
                add(p.writer)
        for b in writes:
            add(b.writer)
            for r in b.readers:
                add(r)
            for p in self._partners(b):
                add(p.writer)
                for r in p.readers:
                    add(r)
        o = _Op(eng, fn, eng in ("g", "q"), tag, len(self.ops[eng]))
        if o.is_dma:
            key = "dma_" + (writes[0].name if writes else "out")
            o.sem_key = self._dma_sem(key)
            prev = self.last_dma.get(key)
            if prev is not None:
                add(prev)
            self.last_dma[key] = o
        if inc:
            o.needs_inc = True
        need_drain = False
        for d in deps:
            if d is o:
                continue
            if d.is_dma:
                d.needs_inc = True
                o.deps.append(d)
            elif d.eng == eng:
                if eng == "pe":
                    continue
                if d.seq > self.last_barrier[eng]:
                    need_drain = True
            else:
                d.needs_inc = True
                o.deps.append(d)
        if need_drain:
            dr = _Op(eng, None, False, "drain", len(self.ops[eng]))
            dr.is_drain = True
            self.ops[eng].append(dr)
            self.last_barrier[eng] = dr.seq - 1
            o.seq = len(self.ops[eng])
        for b in reads:
            if o not in b.readers:
                b.readers.append(o)
        for b in writes:
            b.writer = o
            b.readers = []
        self.ops[eng].append(o)
        return o

    def finalize(self):
        cnt = {}
        for e in self.ENGS:
            for o in self.ops[e]:
                if o.needs_inc:
                    cnt[o.sem_key] = cnt.get(o.sem_key, 0) + (16 if o.is_dma else 1)
                    o.sem_val = cnt[o.sem_key]
        for e in self.ENGS:
            high = {}
            for o in self.ops[e]:
                ws = []
                for d in o.deps:
                    assert d.sem_val is not None, (e, o.tag, d.eng, d.tag)
                    if d.sem_val > high.get(d.sem_key, 0):
                        ws.append((d.sem_key, d.sem_val))
                        high[d.sem_key] = d.sem_val
                o.waits = ws

    def emit(self, eng_obj, ename):
        for o in self.ops[ename]:
            for (se, v) in o.waits:
                eng_obj.wait_ge(self.sems[se], v)
            if o.is_drain:
                eng_obj.drain()
                continue
            if o.fn is None:
                continue
            inst = o.fn(eng_obj)
            if o.needs_inc:
                inst.then_inc(self.sems[o.sem_key], 16 if o.is_dma else 1)


def build_nc():
    nc = bass.Bass("TRN2", target_bir_lowering=False)

    x_h = nc.dram_tensor("x", [D, TOK], F32, kind="ExternalInput")
    y_h = nc.dram_tensor("y", [D, TOK], F32, kind="ExternalOutput")
    pw_h = nc.dram_tensor("pwT", [D, D], F32, kind="ExternalInput").ap()
    f1_h = nc.dram_tensor("f1T", [D, H1], F32, kind="ExternalInput").ap()
    f2_h = nc.dram_tensor("f2m", [KC, 128, M1, 128], F32, kind="ExternalInput")
    g_h = nc.dram_tensor("gT", [D, H2], I8, kind="ExternalInput").ap()
    u_h = nc.dram_tensor("uT", [D, H2], I8, kind="ExternalInput").ap()
    d_h = nc.dram_tensor("dm", [KC, 128, M2, 128], I8, kind="ExternalInput")
    f2s_h = nc.dram_tensor("f2s", [KC, 128, M1, 128], BF16)
    ds_h = nc.dram_tensor("ds", [KC, 128, M2, 128], BF16)
    vf_h = nc.dram_tensor("vecf", [128, 112], F32, kind="ExternalInput").ap()
    vb_h = nc.dram_tensor("vecb", [128, 80], F32, kind="ExternalInput").ap()

    vecf = nc.alloc_sbuf_tensor("vecf_s", [128, 112], F32)
    vecb = nc.alloc_sbuf_tensor("vecb_s", [128, 80], BF16)
    cvec = nc.alloc_sbuf_tensor("cvec", [128, 24], F32)
    c1f = nc.alloc_sbuf_tensor("c1f", [128, 32], F32)
    ctmp = nc.alloc_sbuf_tensor("ctmp", [128, 32], F32)
    epsc = nc.alloc_sbuf_tensor("epsc", [128, 3], F32)
    warm = nc.alloc_sbuf_tensor("warm", [128, 1], F32)
    onec = nc.alloc_sbuf_tensor("onec", [128, 1], F32)
    avg1 = nc.alloc_sbuf_tensor("avg1", [128, 128], BF16)
    avg4 = nc.alloc_sbuf_tensor("avg4", [128, 128], BF16)
    x1r = nc.alloc_sbuf_tensor("x1r", [128, KC * TOK], BF16)
    rows = [nc.alloc_sbuf_tensor(f"rows{p}", [128, NROW * T], F32) for p in range(2)]
    oWp = 0
    oW1 = oWp + 8 * 1024
    oW2 = oW1 + 8 * 4096
    oX0 = oW2 + 3 * 4096
    oY = oX0 + 2 * 2048
    oXA = oY + 2048
    oZ = oXA + 2 * 2048
    oH = oZ + 2048
    oAO = oH + 8192
    oSC = oAO + 2048
    aA = oSC + 1024
    oWg = 0
    oWu = oWg + 8 * 3072
    oWd = oWu + 8 * 3072
    oZ2 = oWd + 4 * 3072
    oHF = oZ2 + 2 * 2048
    oSG = oHF + 6144
    oF = oSG + 2 * 512
    aB = oF + 2048
    WB = max(aA, aB)
    wb = nc.alloc_sbuf_tensor("wb", [128, WB], BF16)
    oT = 0
    oE = 2048
    oTR = oE + 512
    FB = 3072
    fb = nc.alloc_sbuf_tensor("fb", [128, FB], F32)

    P = [nc.alloc_psum_tensor(f"ps{i}", [128, 512], F32) for i in range(8)]

    sch = _Sched(nc)
    bf = sch.buf
    S_ = sch.op

    b_vecf = bf("vecf")
    b_vecb = bf("vecb")
    b_cvec = bf("cvec")
    b_c1f = [bf(f"c1f{g}") for g in range(4)]
    b_ctmp = [bf("ctmp_e"), bf("ctmp_o")]
    b_ctmp2 = [bf("ctmp2_e"), bf("ctmp2_o")]
    b_eps = bf("eps")
    b_avg1 = bf("avg1")
    b_avg4 = bf("avg4")
    b_x1 = [bf(f"x1_{i}") for i in range(NT)]
    b_rows = [[bf(f"rows{p}_{s}") for s in range(NROW)] for p in range(2)]

    W = "w"
    b_Wp = bf("Wp", W, oWp * 2, (oWp + 8 * 1024) * 2)
    b_W1 = [[bf(f"W1_{k}_{g}", W, (oW1 + k * 4096 + g * 1024) * 2,
                (oW1 + k * 4096 + (g + 1) * 1024) * 2) for g in range(4)]
            for k in range(KC)]
    b_W2 = [bf(f"W2_{j}", W, (oW2 + j * 4096) * 2, (oW2 + (j + 1) * 4096) * 2)
            for j in range(3)]
    b_X0 = [bf(f"X0_{p}", W, (oX0 + p * 2048) * 2, (oX0 + (p + 1) * 2048) * 2)
            for p in range(2)]
    b_Y = bf("Y", W, oY * 2, (oY + 2048) * 2)
    b_XA = [[bf(f"XA{p}_{c}", W, (oXA + p * 2048 + c * T) * 2,
                (oXA + p * 2048 + (c + 1) * T) * 2) for c in range(KC)]
            for p in range(2)]
    b_Z = bf("Z", W, oZ * 2, (oZ + 2048) * 2)
    b_H = [bf(f"H{gq}", W, (oH + gq * 2048) * 2, (oH + (gq + 1) * 2048) * 2)
           for gq in range(4)]
    b_AO = [bf(f"AO{c}", W, (oAO + c * T) * 2, (oAO + (c + 1) * T) * 2)
            for c in range(KC)]
    b_SC = bf("SC", W, oSC * 2, (oSC + 1024) * 2)
    b_CB = [bf(f"CB{s}", W, (oSC + s * 256) * 2, (oSC + (s + 1) * 256) * 2)
            for s in range(3)]
    b_Wg = [bf(f"Wg{k}", W, (oWg + k * 3072) * 2, (oWg + (k + 1) * 3072) * 2)
            for k in range(KC)]
    b_Wu = [bf(f"Wu{k}", W, (oWu + k * 3072) * 2, (oWu + (k + 1) * 3072) * 2)
            for k in range(KC)]
    b_Wd = [bf(f"Wd_{j}", W, (oWd + j * 3072) * 2, (oWd + (j + 1) * 3072) * 2)
            for j in range(4)]
    b_Z2 = [bf(f"Z2_{p}", W, (oZ2 + p * 2048) * 2, (oZ2 + (p + 1) * 2048) * 2)
            for p in range(2)]
    b_HF = bf("HF", W, oHF * 2, (oHF + 6144) * 2)
    b_SG = [bf(f"SG_{p}", W, (oSG + p * 512) * 2, (oSG + (p + 1) * 512) * 2)
            for p in range(2)]
    b_F = bf("F", W, oF * 2, (oF + 2048) * 2)
    Fa = "f"
    b_T = bf("Tsh", Fa, oT * 4, (oT + 2048) * 4)
    b_E = bf("Ero", Fa, oE * 4, (oE + 1024) * 4)
    b_trow = [bf(f"trow{t}", Fa, (oTR + t * 128) * 4, (oTR + (t + 1) * 128) * 4)
              for t in range(2)]
    b_trow8 = [bf(f"trow8{t}", Fa, (oE + t * 512) * 4, (oE + (t + 1) * 512) * 4)
               for t in range(2)]
    b_P = [bf(f"P{i}") for i in range(8)]
    b_f2s = [bf(f"f2s{j}") for j in range(KC)]
    b_ds = [bf(f"ds{j}") for j in range(KC)]
    b_warm = bf("warm")
    b_onec = bf("onec")

    def wsl(off, n):
        return bass.AP(wb, off, [[WB, 128], [1, n]])

    def wsl3(off, d1, n1, n2):
        return bass.AP(wb, off, [[WB, 128], [d1, n1], [1, n2]])

    def fsl(off, n):
        return bass.AP(fb, off, [[FB, 128], [1, n]])

    def fsl3(off, d1, n1, n2):
        return bass.AP(fb, off, [[FB, 128], [d1, n1], [1, n2]])

    def rsl(p, s):
        return bass.AP(rows[p], s * T, [[NROW * T, 128], [1, T]])

    def rrep(p, s, reps):
        return bass.AP(rows[p], s * T, [[NROW * T, 128], [0, reps], [1, T]])

    def wrep(off, reps):
        return bass.AP(wb, off, [[WB, 128], [0, reps], [1, T]])

    def x1sl(i):
        return bass.AP(x1r, i * T, [[KC * TOK, 128], [TOK, KC], [1, T]])

    def x1c(c, i):
        return bass.AP(x1r, c * TOK + i * T, [[KC * TOK, 128], [1, T]])

    def pap(b, n=T, off=0):
        return P[b][:, off:off + n]

    def col(t, c):
        return t[:, c:c + 1]

    S_("v", lambda e: e.memset(epsc.ap()[:, 0:1], EPS), writes=[b_eps])
    S_("v", lambda e: e.memset(epsc.ap()[:, 1:2], EPS2), writes=[b_eps])
    S_("v", lambda e: e.memset(epsc.ap()[:, 2:3], EPS * EPS), writes=[b_eps])
    S_("v", lambda e: e.memset(onec.ap(), 1.0), writes=[b_onec])
    S_("v", lambda e: e.memset(avg1.ap(), 1.0 / D), writes=[b_avg1])
    S_("v", lambda e: e.memset(avg4.ap(), 1.0 / H1), writes=[b_avg4])
    S_("q", lambda e: e.dma_start(out=vecf.ap(), in_=vf_h), writes=[b_vecf], tag="vecf")

    def x0dma(i):
        p = i % 2
        dst = wsl3(oX0 + p * 2048, T, KC, T)
        src = bass.AP(x_h, i * T, [[TOK, 128], [128 * TOK, KC], [1, T]])
        S_("g", lambda e: e.dma_start(out=dst, in_=src), writes=[b_X0[p]],
           tag=f"x0_{i}")

    x0dma(0)
    S_("g", lambda e: e.dma_start(out=vecb.ap(), in_=vb_h), writes=[b_vecb], tag="vecb")
    S_("g", lambda e: e.dma_start(out=wsl(oWp, 8 * 1024), in_=pw_h), writes=[b_Wp],
       tag="Wp")
    for gq in range(4):
        for k in range(KC):
            S_("g", lambda e, k=k, gq=gq: e.dma_start(
                out=wsl(oW1 + k * 4096 + gq * 1024, 1024),
                in_=f1_h[k * 128:(k + 1) * 128, gq * 1024:(gq + 1) * 1024]),
                writes=[b_W1[k][gq]], tag=f"W1k{k}g{gq}")
        if gq == 0:
            x0dma(1)

    def emit_staging():
        for j in range(KC):
            n = 128 * M1 * 128
            S_("g", lambda e, j=j, n=n: e.dma_start(
                out=bass.AP(f2s_h, j * n, [[0, 1], [1, n]]),
                in_=bass.AP(f2_h, j * n, [[0, 1], [1, n]])),
                writes=[b_f2s[j]], tag=f"stg2.{j}")
        for j in range(KC):
            n = 128 * M2 * 128
            S_("g", lambda e, j=j, n=n: e.dma_start(
                out=bass.AP(ds_h, j * n, [[0, 1], [1, n]]),
                in_=bass.AP(d_h, j * n, [[0, 1], [1, n]])),
                writes=[b_ds[j]], tag=f"stgd.{j}")

    def bias_const(woff, wklen, bcol0, nk, banks2, whalf, wbuf, dst_col0, tagp):
        for h in whalf:
            bank = banks2[h % 2]
            sl = h % 2
            for k in range(nk):
                S_("pe", lambda e, h=h, k=k, bank=bank: e.matmul(
                    P[bank][0:1, 0:512], col(vecb, bcol0 + k),
                    wsl(woff + k * wklen + h * 512, 512),
                    start=(k == 0), stop=(k == nk - 1)),
                    reads=[b_vecb, wbuf(k, h)], writes=[b_P[bank]],
                    tag=f"{tagp}m{h}.{k}")
            S_("v", lambda e, sl=sl, bank=bank: e.tensor_copy(
                bass.AP(fb, oE + sl * 512, [[FB, 1], [1, 512]]),
                P[bank][0:1, 0:512]),
                reads=[b_P[bank]], writes=[b_trow8[sl]], tag=f"{tagp}cp{h}")
            for j in range(4):
                cc = (h % 2) * 4 + j
                S_("pe", lambda e, sl=sl, j=j, cc=cc: e.matmul(
                    P[5][:, cc:cc + 1],
                    bass.AP(fb, oE + sl * 512 + j * 128, [[FB, 1], [1, 128]]),
                    onec.ap()[0:1, 0:1], start=True, stop=True),
                    reads=[b_trow8[sl], b_onec], writes=[b_P[5]],
                    tag=f"{tagp}tp{h}.{j}")
        S_("v", lambda e: e.tensor_copy(
            bass.AP(ctmp, dst_col0, [[32, 128], [1, 8]]), P[5][:, 0:8]),
            reads=[b_P[5]], writes=[b_ctmp[0], b_ctmp[1]], tag=f"{tagp}out")

    def startup_consts():
        bias_const(oWp, 1024, 0, KC, (0, 2), [0, 1], lambda k, h: b_Wp, 0,
                   "c2p.")
        S_("v", lambda e: e.tensor_add(cvec.ap()[:, 0:8],
                                       bass.AP(ctmp, 0, [[32, 128], [1, 8]]),
                                       vecf.ap()[:, 64:72]),
           reads=[b_ctmp[0], b_ctmp[1], b_vecf], writes=[b_cvec], tag="c2p.add")
        S_("v", lambda e: e.tensor_mul(cvec.ap()[:, 0:8], cvec.ap()[:, 0:8],
                                       vecf.ap()[:, 8:16]),
           reads=[b_cvec, b_vecf], writes=[b_cvec], tag="c3p.mul")
        for k in range(KC):
            S_("v", lambda e, k=k: e.tensor_scalar_mul(
                wsl(oWp + k * 1024, 1024), wsl(oWp + k * 1024, 1024), col(vecf, k)),
                reads=[b_Wp, b_vecf], writes=[b_Wp], tag=f"fold.p{k}")
        for gq in range(4):
            bias_const(oW1, 4096, 8, KC, (1, 3), [2 * gq, 2 * gq + 1],
                       lambda k, h: b_W1[k][h // 2], 8 * gq, "c1f.")
            S_("v", lambda e, gq=gq: e.tensor_add(
                c1f.ap()[:, gq * 8:(gq + 1) * 8],
                bass.AP(ctmp, gq * 8, [[32, 128], [1, 8]]),
                vecf.ap()[:, 72 + gq * 8:80 + gq * 8]),
                reads=[b_ctmp[0], b_ctmp[1], b_vecf], writes=[b_c1f[gq]],
                tag=f"c1f.add{gq}")
            for k in range(KC):
                S_("v", lambda e, k=k, gq=gq: e.tensor_scalar_mul(
                    wsl(oW1 + k * 4096 + gq * 1024, 1024),
                    wsl(oW1 + k * 4096 + gq * 1024, 1024),
                    col(vecf, 16 + k)),
                    reads=[b_W1[k][gq], b_vecf], writes=[b_W1[k][gq]],
                    tag=f"fold.1{k}.{gq}")

    def stats_sqv(src_chunk_ap, src_buf, tagp, sq_off, sq_buf):
        srcb = src_buf if isinstance(src_buf, list) else [src_buf] * KC
        for b0 in range(2):
            S_("v", lambda e, b0=b0: e.tensor_mul(
                wsl3(sq_off + b0 * 4 * T, T, 4, T), src_chunk_ap((b0 * 4, 4)),
                src_chunk_ap((b0 * 4, 4))),
                reads=srcb[b0 * 4:b0 * 4 + 4], writes=[sq_buf],
                tag=f"{tagp}sq{b0}")

    def stats_mms(src_chunk_ap, src_buf, banks, tagp, sq_off, sq_buf):
        bs, bq = banks
        bsb, bso = bs if isinstance(bs, tuple) else (bs, 0)
        bqb, bqo = bq if isinstance(bq, tuple) else (bq, 0)
        srcb = src_buf if isinstance(src_buf, list) else [src_buf] * KC
        for c in range(KC):
            S_("pe", lambda e, c=c: e.matmul(
                pap(bsb, T, bso), avg1.ap(), src_chunk_ap(c), start=(c == 0),
                stop=(c == KC - 1)),
                reads=[srcb[c], b_avg1], writes=[b_P[bsb]], tag=f"{tagp}s{c}")
        for cc in range(KC):
            S_("pe", lambda e, cc=cc: e.matmul(
                pap(bqb, T, bqo), avg1.ap(), wsl(sq_off + cc * T, T),
                start=(cc == 0), stop=(cc == KC - 1)),
                reads=[sq_buf, b_avg1], writes=[b_P[bqb]], tag=f"{tagp}q{cc}")

    def stats_1k(p, src_chunk_ap, src_buf, banks, tagp, sq_off=oSC, sq_buf=None):
        if sq_buf is None:
            sq_buf = b_SC
        bs, bq = banks
        bsb, bso = bs if isinstance(bs, tuple) else (bs, 0)
        bqb, bqo = bq if isinstance(bq, tuple) else (bq, 0)
        srcb = src_buf if isinstance(src_buf, list) else [src_buf] * KC
        for c in range(KC):
            S_("pe", lambda e, c=c: e.matmul(
                pap(bsb, T, bso), avg1.ap(), src_chunk_ap(c), start=(c == 0),
                stop=(c == KC - 1)),
                reads=[srcb[c], b_avg1], writes=[b_P[bsb]], tag=f"{tagp}s{c}")
        for b0 in range(2):
            S_("v", lambda e, b0=b0: e.tensor_mul(
                wsl3(sq_off, T, 4, T), src_chunk_ap((b0 * 4, 4)),
                src_chunk_ap((b0 * 4, 4))),
                reads=srcb[b0 * 4:b0 * 4 + 4], writes=[sq_buf],
                tag=f"{tagp}sq{b0}")
            for c in range(4):
                cc = b0 * 4 + c
                S_("pe", lambda e, c=c, cc=cc: e.matmul(
                    pap(bqb, T, bqo), avg1.ap(), wsl(sq_off + c * T, T),
                    start=(cc == 0), stop=(cc == KC - 1)),
                    reads=[sq_buf, b_avg1], writes=[b_P[bqb]], tag=f"{tagp}q{cc}")
        return bs, bq

    def ln_rows(p, mu_s, r_s, tmp_s, banks, eps_col=0, double=False, s_slot=None,
                tagp="", bcast=False, head_v=False):
        bs, bq = banks
        bsb, bso = bs if isinstance(bs, tuple) else (bs, 0)
        bqb, bqo = bq if isinstance(bq, tuple) else (bq, 0)
        v_s = tmp_s + 1
        if head_v:
            S_("v", lambda e: e.tensor_copy(rsl(p, mu_s), pap(bsb, T, bso)),
               reads=[b_P[bsb]], writes=[b_rows[p][mu_s]], tag=f"{tagp}mu")
            S_("v", lambda e: e.tensor_mul(rsl(p, tmp_s), rsl(p, mu_s),
                                           rsl(p, mu_s)),
               reads=[b_rows[p][mu_s]], writes=[b_rows[p][tmp_s]],
               tag=f"{tagp}musq")
        else:
            S_("s", lambda e: e.activation(out=rsl(p, mu_s), in_=pap(bsb, T, bso),
                                           func=AF.Identity),
               reads=[b_P[bsb]], writes=[b_rows[p][mu_s]], tag=f"{tagp}mu")
            S_("s", lambda e: e.activation(out=rsl(p, tmp_s), in_=rsl(p, mu_s),
                                           func=AF.Square),
               reads=[b_rows[p][mu_s]], writes=[b_rows[p][tmp_s]],
               tag=f"{tagp}musq")
        if bcast:
            eng = "v" if head_v else "s"
            if head_v:
                S_("v", lambda e: e.tensor_copy(wsl(oSC, T), rsl(p, mu_s)),
                   reads=[b_rows[p][mu_s]], writes=[b_CB[0]], tag=f"{tagp}mub")
            else:
                S_("s", lambda e: e.activation(out=wsl(oSC, T), in_=rsl(p, mu_s),
                                               func=AF.Identity),
                   reads=[b_rows[p][mu_s]], writes=[b_CB[0]], tag=f"{tagp}mub")
        S_("v", lambda e: e.scalar_tensor_tensor(
            rsl(p, v_s), rsl(p, tmp_s), -1.0, pap(bqb, T, bqo), ALU.mult, ALU.add),
           reads=[b_rows[p][tmp_s], b_P[bqb]], writes=[b_rows[p][v_s]],
           tag=f"{tagp}v")
        if double:
            S_("s", lambda e: e.activation(out=rsl(p, tmp_s), in_=rsl(p, v_s),
                                           func=AF.Ln, bias=col(epsc, 2),
                                           scale=1.0 + EPS),
               reads=[b_rows[p][v_s], b_eps], writes=[b_rows[p][tmp_s]],
               tag=f"{tagp}lns")
            if bcast:
                S_("s", lambda e: e.activation(out=wsl(oSC + 256, T),
                                               in_=rsl(p, tmp_s),
                                               func=AF.Exp, scale=-0.5),
                   reads=[b_rows[p][tmp_s]], writes=[b_CB[1]], tag=f"{tagp}exps")
            else:
                S_("s", lambda e: e.activation(out=rsl(p, s_slot), in_=rsl(p, tmp_s),
                                               func=AF.Exp, scale=-0.5),
                   reads=[b_rows[p][tmp_s]], writes=[b_rows[p][s_slot]],
                   tag=f"{tagp}exps")
        S_("s", lambda e: e.activation(out=rsl(p, tmp_s), in_=rsl(p, v_s), func=AF.Ln,
                                       bias=col(epsc, eps_col), scale=1.0),
           reads=[b_rows[p][v_s], b_eps], writes=[b_rows[p][tmp_s]], tag=f"{tagp}ln")
        if bcast:
            S_("s", lambda e: e.activation(out=wsl(oSC + 512, T), in_=rsl(p, tmp_s),
                                           func=AF.Exp, scale=-0.5),
               reads=[b_rows[p][tmp_s]], writes=[b_CB[2]], tag=f"{tagp}exp")
        else:
            S_("s", lambda e: e.activation(out=rsl(p, r_s), in_=rsl(p, tmp_s),
                                           func=AF.Exp, scale=-0.5),
               reads=[b_rows[p][tmp_s]], writes=[b_rows[p][r_s]], tag=f"{tagp}exp")

    def prewarm_ln(tag):
        S_("s", lambda e: e.activation(out=warm.ap(), in_=col(epsc, 0), func=AF.Ln,
                                       bias=col(epsc, 0), scale=1.0),
           reads=[b_eps], writes=[b_warm], tag=tag)

    def a_src(off):
        def f(c):
            if isinstance(c, tuple):
                return wsl3(off + c[0] * T, T, c[1], T)
            return wsl(off + c * T, T)
        return f

    def headA_stats(i):
        p = i % 2
        b0 = 4 if p == 0 else 6
        stats_1k(p, a_src(oX0 + p * 2048), b_X0[p], ((b0, 0), (b0, 256)),
                 f"A{i}.l0.")

    def headA_rows(i):
        p = i % 2
        b0 = 4 if p == 0 else 6
        ln_rows(p, 0, 1, 3, ((b0, 0), (b0, 256)), double=True, s_slot=2,
                tagp=f"A{i}.l0.", bcast=True, head_v=True)
        S_("v", lambda e: e.tensor_sub(wsl3(oZ, T, KC, T),
                                       wsl3(oX0 + p * 2048, T, KC, T),
                                       wrep(oSC, KC)),
           reads=[b_X0[p], b_CB[0]], writes=[b_Z], tag=f"A{i}.t")
        S_("v", lambda e: e.tensor_mul(wsl3(oY, T, KC, T), wsl3(oZ, T, KC, T),
                                       wrep(oSC + 256, KC)),
           reads=[b_Z, b_CB[1]], writes=[b_Y], tag=f"A{i}.y")
        S_("v", lambda e: e.tensor_mul(wsl3(oZ, T, KC, T), wsl3(oZ, T, KC, T),
                                       wrep(oSC + 512, KC)),
           reads=[b_Z, b_CB[2]], writes=[b_Z], tag=f"A{i}.yin")
        for c in range(KC):
            S_("v", lambda e, c=c: e.tensor_scalar(
                wsl(oZ + c * T, T), wsl(oZ + c * T, T), 1.0, col(cvec, c),
                ALU.mult, ALU.add),
                reads=[b_Z, b_cvec], writes=[b_Z], tag=f"A{i}.yc{c}")
        S_("v", lambda e: e.tensor_copy(
            bass.AP(x1r, i * T, [[KC * TOK, 128], [TOK, KC], [1, T]]),
            wsl3(oX0 + p * 2048, T, KC, T)),
           reads=[b_X0[p]], writes=[b_x1[i]], tag=f"A{i}.x1p")

    def lnh_tail(j):
        pj = j % 2
        bkj = (4, 5) if pj == 0 else (6, 7)
        off, buf = oX0 + pj * 2048, b_X0[pj]
        for c in range(24, 32):
            S_("pe", lambda e, c=c: e.matmul(
                pap(bkj[0]), avg4.ap(), wsl(oH + c * T, T), start=False,
                stop=(c == M1 - 1)),
                reads=[b_H[3], b_avg4], writes=[b_P[bkj[0]]], tag=f"A{j}.lh.s{c}")
        for b0 in range(2):
            cc0 = 24 + b0 * 4
            S_("v", lambda e, cc0=cc0, off=off, b0=b0: e.tensor_mul(
                wsl3(off + b0 * 4 * T, T, 4, T), wsl3(oH + cc0 * T, T, 4, T),
                wsl3(oH + cc0 * T, T, 4, T)),
                reads=[b_H[3]], writes=[buf], tag=f"A{j}.hsq{cc0}")
        for c in range(24, 32):
            S_("v", lambda e, c=c: e.tensor_scalar_mul(
                wsl(oH + c * T, T), wsl(oH + c * T, T), col(vecf, 32 + c)),
                reads=[b_H[3], b_vecf], writes=[b_H[3]], tag=f"A{j}.hg{c}")
        for c in range(8):
            cc = 24 + c
            S_("pe", lambda e, c=c, cc=cc, off=off, bkj=bkj: e.matmul(
                pap(bkj[1]), avg4.ap(), wsl(off + c * T, T),
                start=False, stop=(cc == M1 - 1)),
                reads=[buf, b_avg4], writes=[b_P[bkj[1]]], tag=f"A{j}.lh.q{cc}")
        ln_rows(pj, 2, 4, 3, bkj, tagp=f"A{j}.lh.")
        S_("v", lambda e, pj=pj: e.tensor_mul(rsl(pj, 3), rsl(pj, 2), rsl(pj, 4)),
           reads=[b_rows[pj][2], b_rows[pj][4]], writes=[b_rows[pj][3]],
           tag=f"A{j}.mrh")
        S_("v", lambda e, pj=pj: e.tensor_scalar_mul(rsl(pj, 4), rsl(pj, 4), -1.0),
           reads=[b_rows[pj][4]], writes=[b_rows[pj][4]], tag=f"A{j}.nrh")

    def chainA_stats(i):
        p = i % 2
        b1 = 5 if p == 0 else 7
        stats_1k(p, a_src(oAO), b_AO, ((b1, 0), (b1, 256)), f"A{i}.l3.")

    def chainA_lnrows(i):
        p = i % 2
        b1 = 5 if p == 0 else 7
        ln_rows(p, 0, 1, 3, ((b1, 0), (b1, 256)), tagp=f"A{i}.l3.")

    def chainA_sheets(i):
        p = i % 2
        for hh in range(2):
            xsl = bass.AP(x1r, (hh * 4) * TOK + i * T,
                          [[KC * TOK, 128], [TOK, 4], [1, T]])
            S_("v", lambda e, hh=hh: e.tensor_sub(
                wsl3(oSC, T, 4, T), wsl3(oAO + hh * 4 * T, T, 4, T), rrep(p, 0, 4)),
                reads=b_AO[hh * 4:hh * 4 + 4] + [b_rows[p][0]], writes=[b_SC],
                tag=f"A{i}.t3{hh}")
            S_("v", lambda e, hh=hh: e.tensor_mul(
                wsl3(oSC, T, 4, T), wsl3(oSC, T, 4, T), rrep(p, 1, 4)),
                reads=[b_SC, b_rows[p][1]], writes=[b_SC], tag=f"A{i}.t3r{hh}")
            S_("v", lambda e, hh=hh, xsl=xsl: e.tensor_add(
                xsl, wsl3(oSC, T, 4, T), xsl),
                reads=[b_SC, b_x1[i]], writes=[b_x1[i]], tag=f"A{i}.x1{hh}")

    w2_cnt = [0]

    def w2dma(g):
        sb = g % 3
        m = g % KC
        dst = wsl3(oW2 + sb * 4096, 128, M1, 128)
        src = bass.AP(f2s_h, m * 128 * M1 * 128,
                      [[M1 * 128, 128], [128, M1], [1, 128]])
        S_("q", lambda e: e.dma_start(out=dst, in_=src), reads=[b_f2s[m]],
           writes=[b_W2[sb]], tag=f"W2g{g}")

    def ensure_w2(g):
        while w2_cnt[0] <= g:
            w2dma(w2_cnt[0])
            w2_cnt[0] += 1

    def fc2A(j):
        pj = j % 2
        bkj = (4, 5) if pj == 0 else (6, 7)
        for mp in range(KC // 2):
            for half in range(2):
                m = 2 * mp + half
                gidx = j * KC + m
                ensure_w2(gidx)
                sb = gidx % 3
                mb = m % 4
                for k in range(M1):
                    S_("pe", lambda e, m=m, k=k, mb=mb, sb=sb: e.matmul(
                        pap(mb), wsl(oW2 + sb * 4096 + k * 128, 128),
                        wsl(oH + k * T, T),
                        start=(k == 0), stop=(k == M1 - 1)),
                        reads=[b_W2[sb], b_H[k // 8]], writes=[b_P[mb]],
                        tag=f"A{j}.f2.{m}.{k}")
                if j == 0:
                    cb = bkj[0]
                    for k in range(M1):
                        S_("pe", lambda e, k=k, sb=sb: e.matmul(
                            P[cb][:, 256:258],
                            wsl(oW2 + sb * 4096 + k * 128, 128),
                            vecb.ap()[:, 16 + k:49 + k:32],
                            start=(k == 0), stop=(k == M1 - 1)),
                            reads=[b_vecb, b_W2[sb]], writes=[b_P[cb]],
                            tag=f"c2f.{m}.{k}")
                    S_("v", lambda e, m=m: e.tensor_add(
                        col(cvec, 8 + m), P[cb][:, 256:257], col(vecf, 104 + m)),
                        reads=[b_P[cb], b_vecf], writes=[b_cvec],
                        tag=f"c2f.add{m}")
                    S_("v", lambda e, m=m: e.tensor_mul(
                        col(cvec, 8 + m), col(cvec, 8 + m), col(vecf, 24 + m)),
                        reads=[b_cvec, b_vecf], writes=[b_cvec], tag=f"c3f.mul{m}")
                    S_("v", lambda e, m=m: e.tensor_copy(
                        bass.AP(cvec, 16 + m, [[24, 128], [1, 1]]), P[cb][:, 257:258]),
                        reads=[b_P[cb]], writes=[b_cvec], tag=f"wgh.sc{m}")
                if gidx + 3 < NT * KC:
                    ensure_w2(gidx + 3)
            for half in range(2):
                m = 2 * mp + half
                S_("v", lambda e, m=m, mb=m % 4: e.scalar_tensor_tensor(
                    fsl(oE + (m % 2) * 256, T), rsl(pj, 3), col(cvec, 16 + m),
                    pap(mb), ALU.mult, ALU.subtract),
                    reads=[b_rows[pj][3], b_cvec, b_P[m % 4]], writes=[b_E],
                    tag=f"A{j}.f2v{m}")
            for half in range(2):
                m = 2 * mp + half
                S_("v", lambda e, m=m: e.tensor_mul(
                    fsl(oE + (2 + m % 2) * 256, T), fsl(oE + (m % 2) * 256, T),
                    rsl(pj, 4)),
                    reads=[b_E, b_rows[pj][4]], writes=[b_E], tag=f"A{j}.f2t{m}")
            for half in range(2):
                m = 2 * mp + half
                S_("s", lambda e, m=m: e.activation(
                    out=fsl(oE + (m % 2) * 256, T), in_=fsl(oE + (2 + m % 2) * 256, T),
                    func=AF.Identity, bias=col(cvec, 8 + m), scale=col(vecf, 24 + m)),
                    reads=[b_E, b_cvec, b_vecf], writes=[b_E], tag=f"A{j}.f2e{m}")
            for half in range(2):
                m = 2 * mp + half
                S_("v", lambda e, m=m: e.tensor_add(
                    wsl(oAO + m * T, T), fsl(oE + (m % 2) * 256, T),
                    wsl(oXA + pj * 2048 + m * T, T)),
                    reads=[b_E, b_XA[pj][m]], writes=[b_AO[m]], tag=f"A{j}.ao{m}")

    def bodyA(i):
        p = i % 2
        bk = (4, 5) if p == 0 else (6, 7)
        for m in range(KC):
            mb = m % 4
            for k in range(KC):
                S_("pe", lambda e, m=m, k=k, mb=mb: e.matmul(
                    pap(mb), wsl(oWp + k * 1024 + m * 128, 128),
                    wsl(oY + k * T, T),
                    start=(k == 0), stop=(k == KC - 1)),
                    reads=[b_Wp, b_Y], writes=[b_P[mb]], tag=f"A{i}.pj{m}.{k}")
            S_("v", lambda e, m=m, mb=mb: e.scalar_tensor_tensor(
                wsl(oXA + p * 2048 + m * T, T), pap(mb), col(vecf, 8 + m),
                wsl(oZ + m * T, T), ALU.mult, ALU.add),
                reads=[b_P[mb], b_vecf, b_Z], writes=[b_XA[p][m]], tag=f"A{i}.xa{m}")
        stats_sqv(a_src(oXA + p * 2048), b_XA[p], f"A{i}.l2.", oZ, b_Z)
        if i > 0:
            lnh_tail(i - 1)
        if i > 1:
            chainA_sheets(i - 2)
        if i > 0:
            fc2A(i - 1)
        stats_mms(a_src(oXA + p * 2048), b_XA[p], bk, f"A{i}.l2.", oZ, b_Z)
        ln_rows(p, 5, 2, 3, bk, tagp=f"A{i}.l2.")
        S_("v", lambda e: e.tensor_sub(fsl3(oT, T, KC, T),
                                       wsl3(oXA + p * 2048, T, KC, T), rrep(p, 5, KC)),
           reads=b_XA[p] + [b_rows[p][5]], writes=[b_T], tag=f"A{i}.t2")
        if i > 0:
            stats_sqv(a_src(oAO), b_AO, f"A{i}.l3.", oZ, b_Z)
            p1 = (i - 1) % 2
            b1 = 5 if p1 == 0 else 7
            stats_mms(a_src(oAO), b_AO, ((b1, 0), (b1, 256)), f"A{i - 1}.l3.",
                      oZ, b_Z)
        S_("v", lambda e: e.tensor_mul(wsl3(oZ, T, KC, T), fsl3(oT, T, KC, T),
                                       rrep(p, 2, KC)),
           reads=[b_T, b_rows[p][2]], writes=[b_Z], tag=f"A{i}.z")
        if i > 0 and i + 1 < NT:
            x0dma(i + 1)
        def sqreg(g):
            if g % 2 == 0:
                return oY, b_Y
            return oX0 + p * 2048, b_X0[p]

        def lnh_sum(g):
            c0 = g * 8
            for c in range(c0, c0 + 8):
                S_("pe", lambda e, c=c: e.matmul(
                    pap(bk[0]), avg4.ap(), wsl(oH + c * T, T), start=(c == 0),
                    stop=(c == M1 - 1)),
                    reads=[b_H[c // 8], b_avg4], writes=[b_P[bk[0]]],
                    tag=f"A{i}.lh.s{c}")

        def lnh_sqv(g, hg=True):
            off, buf = sqreg(g)
            c0 = g * 8
            for b0 in range(2):
                cc0 = c0 + b0 * 4
                S_("v", lambda e, cc0=cc0, off=off, b0=b0: e.tensor_mul(
                    wsl3(off + b0 * 4 * T, T, 4, T), wsl3(oH + cc0 * T, T, 4, T),
                    wsl3(oH + cc0 * T, T, 4, T)),
                    reads=[b_H[cc0 // 8]], writes=[buf], tag=f"A{i}.hsq{cc0}")
            if hg:
                lnh_hg(g)

        def lnh_hg(g):
            for c in range(g * 8, g * 8 + 8):
                S_("v", lambda e, c=c: e.tensor_scalar_mul(
                    wsl(oH + c * T, T), wsl(oH + c * T, T), col(vecf, 32 + c)),
                    reads=[b_H[c // 8], b_vecf], writes=[b_H[c // 8]],
                    tag=f"A{i}.hg{c}")

        def lnh_sqmm(g):
            off, buf = sqreg(g)
            for c in range(8):
                cc = g * 8 + c
                S_("pe", lambda e, c=c, cc=cc, off=off: e.matmul(
                    pap(bk[1]), avg4.ap(), wsl(off + c * T, T),
                    start=(cc == 0), stop=(cc == M1 - 1)),
                    reads=[buf, b_avg4], writes=[b_P[bk[1]]],
                    tag=f"A{i}.lh.q{cc}")

        for g in range(4):
            for m in range(g * 8, g * 8 + 8):
                mb = m % 4
                for k in range(KC):
                    S_("pe", lambda e, m=m, k=k, mb=mb: e.matmul(
                        pap(mb), wsl(oW1 + k * 4096 + m * 128, 128),
                        wsl(oZ + k * T, T),
                        start=(k == 0), stop=(k == KC - 1)),
                        reads=[b_W1[k][m // 8], b_Z], writes=[b_P[mb]],
                        tag=f"A{i}.f1.{m}.{k}")
                S_("s", lambda e, m=m, mb=mb: e.activation(
                    out=wsl(oH + m * T, T), in_=pap(mb), func=AF.Gelu,
                    bias=col(c1f, m), scale=1.0),
                    reads=[b_P[mb], b_c1f[m // 8]], writes=[b_H[m // 8]],
                    tag=f"A{i}.gel{m}")
            if g > 0:
                lnh_sum(g - 1)
                lnh_sqv(g - 1)
            if g > 1:
                lnh_sqmm(g - 2)
            if g == 0:
                ensure_w2(i * KC + 2)
            if g == 2 and i + 1 < NT:
                headA_stats(i + 1)
        lnh_sqmm(2)
        prewarm_ln(f"A{i}.warm")
        if i + 1 < NT:
            headA_rows(i + 1)
        if i > 0:
            chainA_lnrows(i - 1)

    def headB_sq(i):
        for b0 in range(2):
            S_("v", lambda e, b0=b0, i=i: e.tensor_mul(
                wsl3(oF + b0 * 4 * T, T, 4, T),
                bass.AP(x1r, b0 * 4 * TOK + i * T, [[KC * TOK, 128], [TOK, 4], [1, T]]),
                bass.AP(x1r, b0 * 4 * TOK + i * T, [[KC * TOK, 128], [TOK, 4], [1, T]])),
                reads=[b_x1[i]], writes=[b_F], tag=f"B{i}.z2sq{b0}")

    def headB(i):
        p = i % 2
        bs, bq = 6, 7
        for c in range(KC):
            S_("pe", lambda e, c=c: e.matmul(
                pap(bs), avg1.ap(), x1c(c, i), start=(c == 0), stop=(c == KC - 1)),
                reads=[b_x1[i], b_avg1], writes=[b_P[bs]], tag=f"B{i}.z2s{c}")
        for cc in range(KC):
            S_("pe", lambda e, cc=cc: e.matmul(
                pap(bq), avg1.ap(), wsl(oF + cc * T, T), start=(cc == 0),
                stop=(cc == KC - 1)),
                reads=[b_F, b_avg1], writes=[b_P[bq]], tag=f"B{i}.z2q{cc}")
        ln_rows(p, 0, 1, 3, (bs, bq), tagp=f"B{i}.z2.")
        S_("v", lambda e: e.tensor_sub(fsl3(oT, T, KC, T), x1sl(i), rrep(p, 0, KC)),
           reads=[b_x1[i], b_rows[p][0]], writes=[b_T], tag=f"B{i}.tz")
        S_("v", lambda e: e.tensor_mul(wsl3(oZ2 + p * 2048, T, KC, T),
                                       fsl3(oT, T, KC, T), rrep(p, 1, KC)),
           reads=[b_T, b_rows[p][1]], writes=[b_Z2[p]], tag=f"B{i}.z2n")

    wd_cnt = [0]

    def wddma(g):
        sb = g % 4
        m = g % KC
        dst = wsl3(oWd + sb * 3072, 128, M2, 128)
        src = bass.AP(ds_h, m * 128 * M2 * 128,
                      [[M2 * 128, 128], [128, M2], [1, 128]])
        S_("q", lambda e: e.dma_start(out=dst, in_=src), reads=[b_ds[m]],
           writes=[b_Wd[sb]], tag=f"Wdg{g}")

    def ensure_wd(g):
        while wd_cnt[0] <= g:
            wddma(wd_cnt[0])
            wd_cnt[0] += 1

    def bodyB(i):
        p = i % 2

        def lf_pair(mp):
            for half in range(2):
                c = 2 * mp + half
                S_("pe", lambda e, c=c: e.matmul(
                    pap(3), avg1.ap(), wsl(oF + c * T, T), start=(c == 0),
                    stop=(c == KC - 1)),
                    reads=[b_F, b_avg1], writes=[b_P[3]], tag=f"B{i}.lf.s{c}")
            for half in range(2):
                c = 2 * mp + half
                S_("pe", lambda e, c=c: e.matmul(
                    pap(4), avg1.ap(), wsl(oZ2 + p * 2048 + c * T, T),
                    start=(c == 0), stop=(c == KC - 1)),
                    reads=[b_Z2[p], b_avg1], writes=[b_P[4]], tag=f"B{i}.lf.q{c}")

        if i + 1 < NT:
            headB_sq(i + 1)
        for mp in range(M2 // 2):
            m0 = 2 * mp
            gb = mp % 3
            ub = 3 + mp % 3
            for half in range(2):
                m = m0 + half
                for k in range(KC):
                    S_("pe", lambda e, m=m, k=k, gb=gb, half=half: e.matmul(
                        pap(gb, T, half * 256),
                        wsl(oWg + k * 3072 + m * 128, 128),
                        wsl(oZ2 + p * 2048 + k * T, T), start=(k == 0),
                        stop=(k == KC - 1)),
                        reads=[b_Wg[k], b_Z2[p]], writes=[b_P[gb]],
                        tag=f"B{i}.g{m}.{k}")
            for half in range(2):
                m = m0 + half
                for k in range(KC):
                    S_("pe", lambda e, m=m, k=k, ub=ub, half=half: e.matmul(
                        pap(ub, T, half * 256),
                        wsl(oWu + k * 3072 + m * 128, 128),
                        wsl(oZ2 + p * 2048 + k * T, T), start=(k == 0),
                        stop=(k == KC - 1)),
                        reads=[b_Wu[k], b_Z2[p]], writes=[b_P[ub]],
                        tag=f"B{i}.u{m}.{k}")
            sg = mp % 2
            S_("s", lambda e, gb=gb, sg=sg: e.activation(
                out=wsl(oSG + sg * 512, 512), in_=pap(gb, 512), func=AF.Silu,
                scale=S127),
                reads=[b_P[gb]], writes=[b_SG[sg]], tag=f"B{i}.si{mp}")
            S_("v", lambda e, m0=m0, ub=ub, sg=sg: e.scalar_tensor_tensor(
                wsl(oHF + m0 * T, 512), pap(ub, 512), S127, wsl(oSG + sg * 512, 512),
                ALU.mult, ALU.mult),
                reads=[b_P[ub], b_SG[sg]], writes=[b_HF], tag=f"B{i}.hf{mp}")
        prewarm_ln(f"B{i}.warm")
        if i + 1 < NT:
            headB(i + 1)
        ensure_wd(i * KC + 3)
        for mp in range(KC // 2):
            m0 = 2 * mp
            dbk = mp % 3
            for half in range(2):
                m = m0 + half
                gidx = i * KC + m
                ensure_wd(gidx)
                sb = gidx % 4
                for k in range(M2):
                    S_("pe", lambda e, m=m, k=k, dbk=dbk, sb=sb, half=half: e.matmul(
                        pap(dbk, T, half * 256),
                        wsl(oWd + sb * 3072 + k * 128, 128), wsl(oHF + k * T, T),
                        start=(k == 0), stop=(k == M2 - 1)),
                        reads=[b_Wd[sb], b_HF], writes=[b_P[dbk]], tag=f"B{i}.d{m}.{k}")
                if gidx + 3 < NT * KC:
                    ensure_wd(gidx + 3)
            S_("s", lambda e, dbk=dbk, m0=m0: e.activation(
                out=wsl(oF + m0 * T, 512), in_=pap(dbk, 512), func=AF.Identity),
                reads=[b_P[dbk]], writes=[b_F], tag=f"B{i}.fe{mp}")
            S_("v", lambda e, m0=m0: e.tensor_mul(
                wsl(oZ2 + p * 2048 + m0 * T, 512), wsl(oF + m0 * T, 512),
                wsl(oF + m0 * T, 512)),
                reads=[b_F], writes=[b_Z2[p]], tag=f"B{i}.fsq{mp}")
            if mp > 1:
                lf_pair(mp - 2)
        lf_pair(KC // 2 - 2)
        lf_pair(KC // 2 - 1)
        ln_rows(p, 5, 2, 3, (3, 4), eps_col=1, tagp=f"B{i}.lf.")
        S_("v", lambda e: e.tensor_sub(fsl3(oT, T, KC, T), wsl3(oF, T, KC, T),
                                       rrep(p, 5, KC)),
           reads=[b_F, b_rows[p][5]], writes=[b_T], tag=f"B{i}.tf")
        S_("v", lambda e: e.tensor_mul(fsl3(oT, T, KC, T), fsl3(oT, T, KC, T),
                                       rrep(p, 2, KC)),
           reads=[b_T, b_rows[p][2]], writes=[b_T], tag=f"B{i}.tfr")
        for hh in range(2):
            S_("v", lambda e, hh=hh: e.tensor_add(
                fsl3(oE, T, 4, T), fsl3(oT + hh * 4 * T, T, 4, T),
                bass.AP(x1r, (hh * 4) * TOK + i * T,
                        [[KC * TOK, 128], [TOK, 4], [1, T]])),
                reads=[b_T, b_x1[i]], writes=[b_E], tag=f"B{i}.out{hh}")
            dst = bass.AP(y_h, (hh * 4 * 128) * TOK + i * T,
                          [[TOK, 128], [128 * TOK, 4], [1, T]])
            S_("q", lambda e, hh=hh, dst=dst: e.dma_start(out=dst,
                                                          in_=fsl3(oE, T, 4, T)),
               reads=[b_E], tag=f"B{i}.st{hh}")

    headA_stats(0)
    startup_consts()
    emit_staging()
    headA_rows(0)
    for i in range(NT):
        bodyA(i)
    for k in range(KC):
        S_("g", lambda e, k=k: e.dma_start(
            out=wsl(oWg + k * 3072, 3072), in_=g_h[k * 128:(k + 1) * 128, :]),
            writes=[b_Wg[k]], tag=f"Wgk{k}")
    chainA_sheets(NT - 2)
    lnh_tail(NT - 1)
    fc2A(NT - 1)
    for k in range(KC):
        S_("g", lambda e, k=k: e.dma_start(
            out=wsl(oWu + k * 3072, 3072), in_=u_h[k * 128:(k + 1) * 128, :]),
            writes=[b_Wu[k]], tag=f"Wuk{k}")
    stats_sqv(a_src(oAO), b_AO, f"A{NT - 1}.l3.", oZ, b_Z)
    stats_mms(a_src(oAO), b_AO, ((7, 0), (7, 256)), f"A{NT - 1}.l3.", oZ, b_Z)
    chainA_lnrows(NT - 1)
    chainA_sheets(NT - 1)
    ensure_wd(3)
    headB_sq(0)
    headB(0)
    for i in range(NT):
        bodyB(i)

    last_store = [o for o in sch.ops["q"]
                  if o.tag.startswith("B") and ".st" in o.tag][-1]
    last_store.needs_inc = True
    fin = _Op("q", None, False, "final", len(sch.ops["q"]))
    fin.deps = [last_store]
    sch.ops["q"].append(fin)

    sch.finalize()

    with nc.Block() as block:
        @block.tensor
        def _(eng):
            sch.emit(eng, "pe")

        @block.vector
        def _(eng):
            sch.emit(eng, "v")

        @block.scalar
        def _(eng):
            sch.emit(eng, "s")

        @block.gpsimd
        def _(eng):
            sch.emit(eng, "g")

        @block.sync
        def _(eng):
            sch.emit(eng, "q")
    return nc


_NC_CACHE = {}


def _get_nc():
    if "nc" not in _NC_CACHE:
        _NC_CACHE["nc"] = build_nc()
    return _NC_CACHE["nc"]


def _pack_cols(vs):
    outs = []
    for v in vs:
        outs.append(np.ascontiguousarray(np.asarray(v, np.float32).reshape(-1, 128).T))
    return np.ascontiguousarray(np.concatenate(outs, axis=1))


def kernel(**inp):
    X = np.asarray(inp["L0_attn_x"], np.float32).reshape(16384, 1024)
    pwT = np.ascontiguousarray(np.asarray(inp["L0_attn_proj_w"], np.float32).T)
    f1T = np.ascontiguousarray(np.asarray(inp["L0_attn_fc1_w"], np.float32).T)
    f2T = np.asarray(inp["L0_attn_fc2_w"], np.float32).T
    f2m = np.ascontiguousarray(
        f2T.reshape(M1, 128, KC, 128).transpose(2, 1, 0, 3))
    gT = np.ascontiguousarray(np.asarray(inp["L0_ffn_w_gate"]).T.astype(np.int8))
    uT = np.ascontiguousarray(np.asarray(inp["L0_ffn_w_up"]).T.astype(np.int8))
    dT = np.asarray(inp["L0_ffn_w_down"]).T.astype(np.int8)
    dm = np.ascontiguousarray(dT.reshape(M2, 128, KC, 128).transpose(2, 1, 0, 3))
    vecf = _pack_cols([inp["L0_attn_ln1_g"], inp["L0_attn_gamma1"],
                       inp["L0_attn_ln2_g"], inp["L0_attn_gamma2"],
                       inp["L0_attn_lnh_g"], inp["L0_attn_proj_b"],
                       inp["L0_attn_fc1_b"], inp["L0_attn_fc2_b"]])
    vecb = _pack_cols([inp["L0_attn_ln1_b"], inp["L0_attn_ln2_b"],
                       inp["L0_attn_lnh_b"], inp["L0_attn_lnh_g"]])
    nc = _get_nc()
    in_maps = []
    for c in range(8):
        xs = np.ascontiguousarray(X[c * TOK:(c + 1) * TOK].T)
        in_maps.append({"x": xs, "pwT": pwT, "f1T": f1T, "f2m": f2m,
                        "gT": gT, "uT": uT, "dm": dm, "vecf": vecf, "vecb": vecb})
    res = run_bass_kernel_spmd(nc, in_maps, core_ids=list(range(8)))
    out = np.concatenate([np.asarray(r["y"], np.float32).T for r in res.results],
                         axis=0)
    return np.ascontiguousarray(out.reshape(4, 4096, 1024))


# version 30
# speedup vs baseline: 1.1378x; 1.0150x over previous
import numpy as np
import concourse.bass as bass
import concourse.mybir as mybir
from concourse.bass_utils import run_bass_kernel_spmd

F32 = mybir.dt.float32
BF16 = mybir.dt.bfloat16
I8 = mybir.dt.int8
AF = mybir.ActivationFunctionType
ALU = mybir.AluOpType

EPS = 1e-5
S127 = 1.0 / 127.0
EPS2 = EPS * 127.0 * 127.0

D = 1024
H1 = 4096
H2 = 3072
KC = 8
M1 = 32
M2 = 24
T = 256
NT = 8
TOK = T * NT
NROW = 6


class _Buf:
    __slots__ = ("name", "arena", "lo", "hi", "writer", "readers")

    def __init__(self, name, arena=None, lo=0, hi=0):
        self.name, self.arena, self.lo, self.hi = name, arena, lo, hi
        self.writer = None
        self.readers = []


class _Op:
    __slots__ = ("eng", "fn", "needs_inc", "sem_val", "deps", "waits",
                 "is_dma", "is_drain", "tag", "seq", "sem_key")

    def __init__(self, eng, fn, is_dma, tag, seq):
        self.eng, self.fn, self.is_dma, self.tag, self.seq = eng, fn, is_dma, tag, seq
        self.needs_inc = is_dma
        self.sem_val = None
        self.deps = []
        self.waits = []
        self.is_drain = False
        self.sem_key = eng


class _Sched:
    ENGS = ("pe", "v", "s", "g", "q")

    def __init__(self, nc):
        self.nc = nc
        self.sems = {e: nc.alloc_semaphore(f"sem_{e}") for e in self.ENGS}
        self.ops = {e: [] for e in self.ENGS}
        self.bufs = []
        self.last_barrier = {e: -1 for e in self.ENGS}
        self.last_dma = {}

    def _dma_sem(self, key):
        if key not in self.sems:
            self.sems[key] = self.nc.alloc_semaphore(f"sem_{key}")
        return key

    def buf(self, name, arena=None, lo=0, hi=0):
        b = _Buf(name, arena, lo, hi)
        self.bufs.append(b)
        return b

    def _partners(self, b):
        if b.arena is None:
            return ()
        return [o for o in self.bufs
                if o is not b and o.arena == b.arena and o.lo < b.hi and o.hi > b.lo]

    def op(self, eng, fn, reads=(), writes=(), tag="", inc=False):
        deps = []

        def add(d):
            if d is not None and d not in deps:
                deps.append(d)

        for b in reads:
            add(b.writer)
            for p in self._partners(b):
                add(p.writer)
        for b in writes:
            add(b.writer)
            for r in b.readers:
                add(r)
            for p in self._partners(b):
                add(p.writer)
                for r in p.readers:
                    add(r)
        o = _Op(eng, fn, eng in ("g", "q"), tag, len(self.ops[eng]))
        if o.is_dma:
            key = "dma_" + (writes[0].name if writes else "out")
            o.sem_key = self._dma_sem(key)
            prev = self.last_dma.get(key)
            if prev is not None:
                add(prev)
            self.last_dma[key] = o
        if inc:
            o.needs_inc = True
        need_drain = False
        for d in deps:
            if d is o:
                continue
            if d.is_dma:
                d.needs_inc = True
                o.deps.append(d)
            elif d.eng == eng:
                if eng == "pe":
                    continue
                if d.seq > self.last_barrier[eng]:
                    need_drain = True
            else:
                d.needs_inc = True
                o.deps.append(d)
        if need_drain:
            dr = _Op(eng, None, False, "drain", len(self.ops[eng]))
            dr.is_drain = True
            self.ops[eng].append(dr)
            self.last_barrier[eng] = dr.seq - 1
            o.seq = len(self.ops[eng])
        for b in reads:
            if o not in b.readers:
                b.readers.append(o)
        for b in writes:
            b.writer = o
            b.readers = []
        self.ops[eng].append(o)
        return o

    def finalize(self):
        cnt = {}
        for e in self.ENGS:
            for o in self.ops[e]:
                if o.needs_inc:
                    cnt[o.sem_key] = cnt.get(o.sem_key, 0) + (16 if o.is_dma else 1)
                    o.sem_val = cnt[o.sem_key]
        for e in self.ENGS:
            high = {}
            for o in self.ops[e]:
                ws = []
                for d in o.deps:
                    assert d.sem_val is not None, (e, o.tag, d.eng, d.tag)
                    if d.sem_val > high.get(d.sem_key, 0):
                        ws.append((d.sem_key, d.sem_val))
                        high[d.sem_key] = d.sem_val
                o.waits = ws

    def emit(self, eng_obj, ename):
        for o in self.ops[ename]:
            for (se, v) in o.waits:
                eng_obj.wait_ge(self.sems[se], v)
            if o.is_drain:
                eng_obj.drain()
                continue
            if o.fn is None:
                continue
            inst = o.fn(eng_obj)
            if o.needs_inc:
                inst.then_inc(self.sems[o.sem_key], 16 if o.is_dma else 1)


def build_nc():
    nc = bass.Bass("TRN2", target_bir_lowering=False)

    x_h = nc.dram_tensor("x", [D, TOK], F32, kind="ExternalInput")
    y_h = nc.dram_tensor("y", [D, TOK], F32, kind="ExternalOutput")
    pw_h = nc.dram_tensor("pwT", [D, D], F32, kind="ExternalInput").ap()
    f1_h = nc.dram_tensor("f1T", [D, H1], F32, kind="ExternalInput").ap()
    f2_h = nc.dram_tensor("f2m", [KC, 128, M1, 128], F32, kind="ExternalInput")
    g_h = nc.dram_tensor("gT", [D, H2], I8, kind="ExternalInput").ap()
    u_h = nc.dram_tensor("uT", [D, H2], I8, kind="ExternalInput").ap()
    d_h = nc.dram_tensor("dm", [KC, 128, M2, 128], I8, kind="ExternalInput")
    f2s_h = nc.dram_tensor("f2s", [KC, 128, M1, 128], BF16)
    ds_h = nc.dram_tensor("ds", [KC, 128, M2, 128], BF16)
    vf_h = nc.dram_tensor("vecf", [128, 112], F32, kind="ExternalInput").ap()
    vb_h = nc.dram_tensor("vecb", [128, 80], F32, kind="ExternalInput").ap()

    vecf = nc.alloc_sbuf_tensor("vecf_s", [128, 112], F32)
    vecb = nc.alloc_sbuf_tensor("vecb_s", [128, 80], BF16)
    cvec = nc.alloc_sbuf_tensor("cvec", [128, 24], F32)
    c1f = nc.alloc_sbuf_tensor("c1f", [128, 32], F32)
    ctmp = nc.alloc_sbuf_tensor("ctmp", [128, 32], F32)
    epsc = nc.alloc_sbuf_tensor("epsc", [128, 3], F32)
    warm = nc.alloc_sbuf_tensor("warm", [128, 1], F32)
    onec = nc.alloc_sbuf_tensor("onec", [128, 1], F32)
    avg1 = nc.alloc_sbuf_tensor("avg1", [128, 128], BF16)
    avg4 = nc.alloc_sbuf_tensor("avg4", [128, 128], BF16)
    x1r = nc.alloc_sbuf_tensor("x1r", [128, KC * TOK], BF16)
    rows = [nc.alloc_sbuf_tensor(f"rows{p}", [128, NROW * T], F32) for p in range(2)]
    oWp = 0
    oW1 = oWp + 8 * 1024
    oW2 = oW1 + 8 * 4096
    oX0 = oW2 + 3 * 4096
    oY = oX0 + 2 * 2048
    oXA = oY + 2048
    oZ = oXA + 2 * 2048
    oH = oZ + 2048
    oAO = oH + 8192
    oSC = oAO + 2048
    aA = oSC + 1024
    oWg = 0
    oWu = oWg + 8 * 3072
    oWd = oWu + 8 * 3072
    oZ2 = oWd + 4 * 3072
    oHF = oZ2 + 2 * 2048
    oSG = oHF + 6144
    oF = oSG + 2 * 512
    aB = oF + 2048
    WB = max(aA, aB)
    wb = nc.alloc_sbuf_tensor("wb", [128, WB], BF16)
    oT = 0
    oE = 2048
    oTR = oE + 512
    FB = 3072
    fb = nc.alloc_sbuf_tensor("fb", [128, FB], F32)

    P = [nc.alloc_psum_tensor(f"ps{i}", [128, 512], F32) for i in range(8)]

    sch = _Sched(nc)
    bf = sch.buf
    S_ = sch.op

    b_vecf = bf("vecf")
    b_vecb = bf("vecb")
    b_cvec = bf("cvec")
    b_c1f = [bf(f"c1f{g}") for g in range(4)]
    b_ctmp = [bf("ctmp_e"), bf("ctmp_o")]
    b_ctmp2 = [bf("ctmp2_e"), bf("ctmp2_o")]
    b_eps = bf("eps")
    b_avg1 = bf("avg1")
    b_avg4 = bf("avg4")
    b_x1 = [bf(f"x1_{i}") for i in range(NT)]
    b_rows = [[bf(f"rows{p}_{s}") for s in range(NROW)] for p in range(2)]

    W = "w"
    b_Wp = bf("Wp", W, oWp * 2, (oWp + 8 * 1024) * 2)
    b_W1 = [[bf(f"W1_{k}_{g}", W, (oW1 + k * 4096 + g * 1024) * 2,
                (oW1 + k * 4096 + (g + 1) * 1024) * 2) for g in range(4)]
            for k in range(KC)]
    b_W2 = [bf(f"W2_{j}", W, (oW2 + j * 4096) * 2, (oW2 + (j + 1) * 4096) * 2)
            for j in range(3)]
    b_X0 = [bf(f"X0_{p}", W, (oX0 + p * 2048) * 2, (oX0 + (p + 1) * 2048) * 2)
            for p in range(2)]
    b_Y = bf("Y", W, oY * 2, (oY + 2048) * 2)
    b_XA = [[bf(f"XA{p}_{c}", W, (oXA + p * 2048 + c * T) * 2,
                (oXA + p * 2048 + (c + 1) * T) * 2) for c in range(KC)]
            for p in range(2)]
    b_Z = bf("Z", W, oZ * 2, (oZ + 2048) * 2)
    b_H = [bf(f"H{gq}", W, (oH + gq * 2048) * 2, (oH + (gq + 1) * 2048) * 2)
           for gq in range(4)]
    b_AO = [bf(f"AO{c}", W, (oAO + c * T) * 2, (oAO + (c + 1) * T) * 2)
            for c in range(KC)]
    b_SC = bf("SC", W, oSC * 2, (oSC + 1024) * 2)
    b_CB = [bf(f"CB{s}", W, (oSC + s * 256) * 2, (oSC + (s + 1) * 256) * 2)
            for s in range(3)]
    b_Wg = [bf(f"Wg{k}", W, (oWg + k * 3072) * 2, (oWg + (k + 1) * 3072) * 2)
            for k in range(KC)]
    b_Wu = [bf(f"Wu{k}", W, (oWu + k * 3072) * 2, (oWu + (k + 1) * 3072) * 2)
            for k in range(KC)]
    b_Wd = [bf(f"Wd_{j}", W, (oWd + j * 3072) * 2, (oWd + (j + 1) * 3072) * 2)
            for j in range(4)]
    b_Z2 = [[bf(f"Z2_{p}_{c}", W, (oZ2 + p * 2048 + c * T) * 2,
                (oZ2 + p * 2048 + (c + 1) * T) * 2) for c in range(KC)]
            for p in range(2)]
    b_HF = [bf(f"HF{j}", W, (oHF + j * 512) * 2, (oHF + (j + 1) * 512) * 2)
            for j in range(M2 // 2)]
    b_SG = [bf(f"SG_{p}", W, (oSG + p * 512) * 2, (oSG + (p + 1) * 512) * 2)
            for p in range(2)]
    b_F = [bf(f"F{j}", W, (oF + j * 512) * 2, (oF + (j + 1) * 512) * 2)
           for j in range(KC // 2)]
    Fa = "f"
    b_T = bf("Tsh", Fa, oT * 4, (oT + 2048) * 4)
    b_E = bf("Ero", Fa, oE * 4, (oE + 1024) * 4)
    b_trow = [bf(f"trow{t}", Fa, (oTR + t * 128) * 4, (oTR + (t + 1) * 128) * 4)
              for t in range(2)]
    b_trow8 = [bf(f"trow8{t}", Fa, (oE + t * 512) * 4, (oE + (t + 1) * 512) * 4)
               for t in range(2)]
    b_P = [bf(f"P{i}") for i in range(8)]
    b_f2s = [bf(f"f2s{j}") for j in range(KC)]
    b_ds = [bf(f"ds{j}") for j in range(KC)]
    b_warm = bf("warm")
    b_onec = bf("onec")

    def wsl(off, n):
        return bass.AP(wb, off, [[WB, 128], [1, n]])

    def wsl3(off, d1, n1, n2):
        return bass.AP(wb, off, [[WB, 128], [d1, n1], [1, n2]])

    def fsl(off, n):
        return bass.AP(fb, off, [[FB, 128], [1, n]])

    def fsl3(off, d1, n1, n2):
        return bass.AP(fb, off, [[FB, 128], [d1, n1], [1, n2]])

    def rsl(p, s):
        return bass.AP(rows[p], s * T, [[NROW * T, 128], [1, T]])

    def rrep(p, s, reps):
        return bass.AP(rows[p], s * T, [[NROW * T, 128], [0, reps], [1, T]])

    def wrep(off, reps):
        return bass.AP(wb, off, [[WB, 128], [0, reps], [1, T]])

    def x1sl(i):
        return bass.AP(x1r, i * T, [[KC * TOK, 128], [TOK, KC], [1, T]])

    def x1c(c, i):
        return bass.AP(x1r, c * TOK + i * T, [[KC * TOK, 128], [1, T]])

    def pap(b, n=T, off=0):
        return P[b][:, off:off + n]

    def col(t, c):
        return t[:, c:c + 1]

    S_("v", lambda e: e.memset(epsc.ap()[:, 0:1], EPS), writes=[b_eps])
    S_("v", lambda e: e.memset(epsc.ap()[:, 1:2], EPS2), writes=[b_eps])
    S_("v", lambda e: e.memset(epsc.ap()[:, 2:3], EPS * EPS), writes=[b_eps])
    S_("v", lambda e: e.memset(onec.ap(), 1.0), writes=[b_onec])
    S_("v", lambda e: e.memset(avg1.ap(), 1.0 / D), writes=[b_avg1])
    S_("v", lambda e: e.memset(avg4.ap(), 1.0 / H1), writes=[b_avg4])
    S_("q", lambda e: e.dma_start(out=vecf.ap(), in_=vf_h), writes=[b_vecf], tag="vecf")

    def x0dma(i):
        p = i % 2
        dst = wsl3(oX0 + p * 2048, T, KC, T)
        src = bass.AP(x_h, i * T, [[TOK, 128], [128 * TOK, KC], [1, T]])
        S_("g", lambda e: e.dma_start(out=dst, in_=src), writes=[b_X0[p]],
           tag=f"x0_{i}")

    x0dma(0)
    S_("g", lambda e: e.dma_start(out=vecb.ap(), in_=vb_h), writes=[b_vecb], tag="vecb")
    S_("g", lambda e: e.dma_start(out=wsl(oWp, 8 * 1024), in_=pw_h), writes=[b_Wp],
       tag="Wp")
    for gq in range(4):
        for k in range(KC):
            S_("g", lambda e, k=k, gq=gq: e.dma_start(
                out=wsl(oW1 + k * 4096 + gq * 1024, 1024),
                in_=f1_h[k * 128:(k + 1) * 128, gq * 1024:(gq + 1) * 1024]),
                writes=[b_W1[k][gq]], tag=f"W1k{k}g{gq}")
        if gq == 0:
            x0dma(1)

    def emit_staging():
        for j in range(KC):
            n = 128 * M1 * 128
            S_("g", lambda e, j=j, n=n: e.dma_start(
                out=bass.AP(f2s_h, j * n, [[0, 1], [1, n]]),
                in_=bass.AP(f2_h, j * n, [[0, 1], [1, n]])),
                writes=[b_f2s[j]], tag=f"stg2.{j}")
        for j in range(KC):
            n = 128 * M2 * 128
            S_("g", lambda e, j=j, n=n: e.dma_start(
                out=bass.AP(ds_h, j * n, [[0, 1], [1, n]]),
                in_=bass.AP(d_h, j * n, [[0, 1], [1, n]])),
                writes=[b_ds[j]], tag=f"stgd.{j}")

    def bias_const(woff, wklen, bcol0, nk, banks2, whalf, wbuf, dst_col0, tagp):
        for h in whalf:
            bank = banks2[h % 2]
            sl = h % 2
            for k in range(nk):
                S_("pe", lambda e, h=h, k=k, bank=bank: e.matmul(
                    P[bank][0:1, 0:512], col(vecb, bcol0 + k),
                    wsl(woff + k * wklen + h * 512, 512),
                    start=(k == 0), stop=(k == nk - 1)),
                    reads=[b_vecb, wbuf(k, h)], writes=[b_P[bank]],
                    tag=f"{tagp}m{h}.{k}")
            S_("v", lambda e, sl=sl, bank=bank: e.tensor_copy(
                bass.AP(fb, oE + sl * 512, [[FB, 1], [1, 512]]),
                P[bank][0:1, 0:512]),
                reads=[b_P[bank]], writes=[b_trow8[sl]], tag=f"{tagp}cp{h}")
            for j in range(4):
                cc = (h % 2) * 4 + j
                S_("pe", lambda e, sl=sl, j=j, cc=cc: e.matmul(
                    P[5][:, cc:cc + 1],
                    bass.AP(fb, oE + sl * 512 + j * 128, [[FB, 1], [1, 128]]),
                    onec.ap()[0:1, 0:1], start=True, stop=True),
                    reads=[b_trow8[sl], b_onec], writes=[b_P[5]],
                    tag=f"{tagp}tp{h}.{j}")
        S_("v", lambda e: e.tensor_copy(
            bass.AP(ctmp, dst_col0, [[32, 128], [1, 8]]), P[5][:, 0:8]),
            reads=[b_P[5]], writes=[b_ctmp[0], b_ctmp[1]], tag=f"{tagp}out")

    def startup_consts():
        bias_const(oWp, 1024, 0, KC, (0, 2), [0, 1], lambda k, h: b_Wp, 0,
                   "c2p.")
        S_("v", lambda e: e.tensor_add(cvec.ap()[:, 0:8],
                                       bass.AP(ctmp, 0, [[32, 128], [1, 8]]),
                                       vecf.ap()[:, 64:72]),
           reads=[b_ctmp[0], b_ctmp[1], b_vecf], writes=[b_cvec], tag="c2p.add")
        S_("v", lambda e: e.tensor_mul(cvec.ap()[:, 0:8], cvec.ap()[:, 0:8],
                                       vecf.ap()[:, 8:16]),
           reads=[b_cvec, b_vecf], writes=[b_cvec], tag="c3p.mul")
        for k in range(KC):
            S_("v", lambda e, k=k: e.tensor_scalar_mul(
                wsl(oWp + k * 1024, 1024), wsl(oWp + k * 1024, 1024), col(vecf, k)),
                reads=[b_Wp, b_vecf], writes=[b_Wp], tag=f"fold.p{k}")
        for gq in range(4):
            bias_const(oW1, 4096, 8, KC, (1, 3), [2 * gq, 2 * gq + 1],
                       lambda k, h: b_W1[k][h // 2], 8 * gq, "c1f.")
            S_("v", lambda e, gq=gq: e.tensor_add(
                c1f.ap()[:, gq * 8:(gq + 1) * 8],
                bass.AP(ctmp, gq * 8, [[32, 128], [1, 8]]),
                vecf.ap()[:, 72 + gq * 8:80 + gq * 8]),
                reads=[b_ctmp[0], b_ctmp[1], b_vecf], writes=[b_c1f[gq]],
                tag=f"c1f.add{gq}")
            for k in range(KC):
                S_("v", lambda e, k=k, gq=gq: e.tensor_scalar_mul(
                    wsl(oW1 + k * 4096 + gq * 1024, 1024),
                    wsl(oW1 + k * 4096 + gq * 1024, 1024),
                    col(vecf, 16 + k)),
                    reads=[b_W1[k][gq], b_vecf], writes=[b_W1[k][gq]],
                    tag=f"fold.1{k}.{gq}")

    def stats_sqv(src_chunk_ap, src_buf, tagp, sq_off, sq_buf):
        srcb = src_buf if isinstance(src_buf, list) else [src_buf] * KC
        for b0 in range(2):
            S_("v", lambda e, b0=b0: e.tensor_mul(
                wsl3(sq_off + b0 * 4 * T, T, 4, T), src_chunk_ap((b0 * 4, 4)),
                src_chunk_ap((b0 * 4, 4))),
                reads=srcb[b0 * 4:b0 * 4 + 4], writes=[sq_buf],
                tag=f"{tagp}sq{b0}")

    def stats_mms(src_chunk_ap, src_buf, banks, tagp, sq_off, sq_buf):
        bs, bq = banks
        bsb, bso = bs if isinstance(bs, tuple) else (bs, 0)
        bqb, bqo = bq if isinstance(bq, tuple) else (bq, 0)
        srcb = src_buf if isinstance(src_buf, list) else [src_buf] * KC
        for c in range(KC):
            S_("pe", lambda e, c=c: e.matmul(
                pap(bsb, T, bso), avg1.ap(), src_chunk_ap(c), start=(c == 0),
                stop=(c == KC - 1)),
                reads=[srcb[c], b_avg1], writes=[b_P[bsb]], tag=f"{tagp}s{c}")
        for cc in range(KC):
            S_("pe", lambda e, cc=cc: e.matmul(
                pap(bqb, T, bqo), avg1.ap(), wsl(sq_off + cc * T, T),
                start=(cc == 0), stop=(cc == KC - 1)),
                reads=[sq_buf, b_avg1], writes=[b_P[bqb]], tag=f"{tagp}q{cc}")

    def stats_1k(p, src_chunk_ap, src_buf, banks, tagp, sq_off=oSC, sq_buf=None):
        if sq_buf is None:
            sq_buf = b_SC
        bs, bq = banks
        bsb, bso = bs if isinstance(bs, tuple) else (bs, 0)
        bqb, bqo = bq if isinstance(bq, tuple) else (bq, 0)
        srcb = src_buf if isinstance(src_buf, list) else [src_buf] * KC
        for c in range(KC):
            S_("pe", lambda e, c=c: e.matmul(
                pap(bsb, T, bso), avg1.ap(), src_chunk_ap(c), start=(c == 0),
                stop=(c == KC - 1)),
                reads=[srcb[c], b_avg1], writes=[b_P[bsb]], tag=f"{tagp}s{c}")
        for b0 in range(2):
            S_("v", lambda e, b0=b0: e.tensor_mul(
                wsl3(sq_off, T, 4, T), src_chunk_ap((b0 * 4, 4)),
                src_chunk_ap((b0 * 4, 4))),
                reads=srcb[b0 * 4:b0 * 4 + 4], writes=[sq_buf],
                tag=f"{tagp}sq{b0}")
            for c in range(4):
                cc = b0 * 4 + c
                S_("pe", lambda e, c=c, cc=cc: e.matmul(
                    pap(bqb, T, bqo), avg1.ap(), wsl(sq_off + c * T, T),
                    start=(cc == 0), stop=(cc == KC - 1)),
                    reads=[sq_buf, b_avg1], writes=[b_P[bqb]], tag=f"{tagp}q{cc}")
        return bs, bq

    def ln_rows(p, mu_s, r_s, tmp_s, banks, eps_col=0, double=False, s_slot=None,
                tagp="", bcast=False, head_v=False):
        bs, bq = banks
        bsb, bso = bs if isinstance(bs, tuple) else (bs, 0)
        bqb, bqo = bq if isinstance(bq, tuple) else (bq, 0)
        v_s = tmp_s + 1
        if head_v:
            S_("v", lambda e: e.tensor_copy(rsl(p, mu_s), pap(bsb, T, bso)),
               reads=[b_P[bsb]], writes=[b_rows[p][mu_s]], tag=f"{tagp}mu")
            S_("v", lambda e: e.tensor_mul(rsl(p, tmp_s), rsl(p, mu_s),
                                           rsl(p, mu_s)),
               reads=[b_rows[p][mu_s]], writes=[b_rows[p][tmp_s]],
               tag=f"{tagp}musq")
        else:
            S_("s", lambda e: e.activation(out=rsl(p, mu_s), in_=pap(bsb, T, bso),
                                           func=AF.Identity),
               reads=[b_P[bsb]], writes=[b_rows[p][mu_s]], tag=f"{tagp}mu")
            S_("s", lambda e: e.activation(out=rsl(p, tmp_s), in_=rsl(p, mu_s),
                                           func=AF.Square),
               reads=[b_rows[p][mu_s]], writes=[b_rows[p][tmp_s]],
               tag=f"{tagp}musq")
        if bcast:
            eng = "v" if head_v else "s"
            if head_v:
                S_("v", lambda e: e.tensor_copy(wsl(oSC, T), rsl(p, mu_s)),
                   reads=[b_rows[p][mu_s]], writes=[b_CB[0]], tag=f"{tagp}mub")
            else:
                S_("s", lambda e: e.activation(out=wsl(oSC, T), in_=rsl(p, mu_s),
                                               func=AF.Identity),
                   reads=[b_rows[p][mu_s]], writes=[b_CB[0]], tag=f"{tagp}mub")
        S_("v", lambda e: e.scalar_tensor_tensor(
            rsl(p, v_s), rsl(p, tmp_s), -1.0, pap(bqb, T, bqo), ALU.mult, ALU.add),
           reads=[b_rows[p][tmp_s], b_P[bqb]], writes=[b_rows[p][v_s]],
           tag=f"{tagp}v")
        if double:
            S_("s", lambda e: e.activation(out=rsl(p, tmp_s), in_=rsl(p, v_s),
                                           func=AF.Ln, bias=col(epsc, 2),
                                           scale=1.0 + EPS),
               reads=[b_rows[p][v_s], b_eps], writes=[b_rows[p][tmp_s]],
               tag=f"{tagp}lns")
            if bcast:
                S_("s", lambda e: e.activation(out=wsl(oSC + 256, T),
                                               in_=rsl(p, tmp_s),
                                               func=AF.Exp, scale=-0.5),
                   reads=[b_rows[p][tmp_s]], writes=[b_CB[1]], tag=f"{tagp}exps")
            else:
                S_("s", lambda e: e.activation(out=rsl(p, s_slot), in_=rsl(p, tmp_s),
                                               func=AF.Exp, scale=-0.5),
                   reads=[b_rows[p][tmp_s]], writes=[b_rows[p][s_slot]],
                   tag=f"{tagp}exps")
        S_("s", lambda e: e.activation(out=rsl(p, tmp_s), in_=rsl(p, v_s), func=AF.Ln,
                                       bias=col(epsc, eps_col), scale=1.0),
           reads=[b_rows[p][v_s], b_eps], writes=[b_rows[p][tmp_s]], tag=f"{tagp}ln")
        if bcast:
            S_("s", lambda e: e.activation(out=wsl(oSC + 512, T), in_=rsl(p, tmp_s),
                                           func=AF.Exp, scale=-0.5),
               reads=[b_rows[p][tmp_s]], writes=[b_CB[2]], tag=f"{tagp}exp")
        else:
            S_("s", lambda e: e.activation(out=rsl(p, r_s), in_=rsl(p, tmp_s),
                                           func=AF.Exp, scale=-0.5),
               reads=[b_rows[p][tmp_s]], writes=[b_rows[p][r_s]], tag=f"{tagp}exp")

    def prewarm_ln(tag):
        S_("s", lambda e: e.activation(out=warm.ap(), in_=col(epsc, 0), func=AF.Ln,
                                       bias=col(epsc, 0), scale=1.0),
           reads=[b_eps], writes=[b_warm], tag=tag)

    def a_src(off):
        def f(c):
            if isinstance(c, tuple):
                return wsl3(off + c[0] * T, T, c[1], T)
            return wsl(off + c * T, T)
        return f

    def headA_stats(i):
        p = i % 2
        b0 = 4 if p == 0 else 6
        stats_1k(p, a_src(oX0 + p * 2048), b_X0[p], ((b0, 0), (b0, 256)),
                 f"A{i}.l0.")

    def headA_rows(i):
        p = i % 2
        b0 = 4 if p == 0 else 6
        ln_rows(p, 0, 1, 3, ((b0, 0), (b0, 256)), double=True, s_slot=2,
                tagp=f"A{i}.l0.", bcast=True, head_v=True)
        S_("v", lambda e: e.tensor_sub(wsl3(oZ, T, KC, T),
                                       wsl3(oX0 + p * 2048, T, KC, T),
                                       wrep(oSC, KC)),
           reads=[b_X0[p], b_CB[0]], writes=[b_Z], tag=f"A{i}.t")
        S_("v", lambda e: e.tensor_mul(wsl3(oY, T, KC, T), wsl3(oZ, T, KC, T),
                                       wrep(oSC + 256, KC)),
           reads=[b_Z, b_CB[1]], writes=[b_Y], tag=f"A{i}.y")
        S_("v", lambda e: e.tensor_mul(wsl3(oZ, T, KC, T), wsl3(oZ, T, KC, T),
                                       wrep(oSC + 512, KC)),
           reads=[b_Z, b_CB[2]], writes=[b_Z], tag=f"A{i}.yin")
        for c in range(KC):
            S_("v", lambda e, c=c: e.tensor_scalar(
                wsl(oZ + c * T, T), wsl(oZ + c * T, T), 1.0, col(cvec, c),
                ALU.mult, ALU.add),
                reads=[b_Z, b_cvec], writes=[b_Z], tag=f"A{i}.yc{c}")
        S_("v", lambda e: e.tensor_copy(
            bass.AP(x1r, i * T, [[KC * TOK, 128], [TOK, KC], [1, T]]),
            wsl3(oX0 + p * 2048, T, KC, T)),
           reads=[b_X0[p]], writes=[b_x1[i]], tag=f"A{i}.x1p")

    def lnh_tail(j):
        pj = j % 2
        bkj = (4, 5) if pj == 0 else (6, 7)
        off, buf = oX0 + pj * 2048, b_X0[pj]
        for c in range(24, 32):
            S_("pe", lambda e, c=c: e.matmul(
                pap(bkj[0]), avg4.ap(), wsl(oH + c * T, T), start=False,
                stop=(c == M1 - 1)),
                reads=[b_H[3], b_avg4], writes=[b_P[bkj[0]]], tag=f"A{j}.lh.s{c}")
        for b0 in range(2):
            cc0 = 24 + b0 * 4
            S_("v", lambda e, cc0=cc0, off=off, b0=b0: e.tensor_mul(
                wsl3(off + b0 * 4 * T, T, 4, T), wsl3(oH + cc0 * T, T, 4, T),
                wsl3(oH + cc0 * T, T, 4, T)),
                reads=[b_H[3]], writes=[buf], tag=f"A{j}.hsq{cc0}")
        for c in range(24, 32):
            S_("v", lambda e, c=c: e.tensor_scalar_mul(
                wsl(oH + c * T, T), wsl(oH + c * T, T), col(vecf, 32 + c)),
                reads=[b_H[3], b_vecf], writes=[b_H[3]], tag=f"A{j}.hg{c}")
        for c in range(8):
            cc = 24 + c
            S_("pe", lambda e, c=c, cc=cc, off=off, bkj=bkj: e.matmul(
                pap(bkj[1]), avg4.ap(), wsl(off + c * T, T),
                start=False, stop=(cc == M1 - 1)),
                reads=[buf, b_avg4], writes=[b_P[bkj[1]]], tag=f"A{j}.lh.q{cc}")
        ln_rows(pj, 2, 4, 3, bkj, tagp=f"A{j}.lh.")
        S_("v", lambda e, pj=pj: e.tensor_mul(rsl(pj, 3), rsl(pj, 2), rsl(pj, 4)),
           reads=[b_rows[pj][2], b_rows[pj][4]], writes=[b_rows[pj][3]],
           tag=f"A{j}.mrh")
        S_("v", lambda e, pj=pj: e.tensor_scalar_mul(rsl(pj, 4), rsl(pj, 4), -1.0),
           reads=[b_rows[pj][4]], writes=[b_rows[pj][4]], tag=f"A{j}.nrh")

    def chainA_stats(i):
        p = i % 2
        b1 = 5 if p == 0 else 7
        stats_1k(p, a_src(oAO), b_AO, ((b1, 0), (b1, 256)), f"A{i}.l3.")

    def chainA_lnrows(i):
        p = i % 2
        b1 = 5 if p == 0 else 7
        ln_rows(p, 0, 1, 3, ((b1, 0), (b1, 256)), tagp=f"A{i}.l3.")

    def chainA_sheets(i):
        p = i % 2
        for hh in range(2):
            xsl = bass.AP(x1r, (hh * 4) * TOK + i * T,
                          [[KC * TOK, 128], [TOK, 4], [1, T]])
            S_("v", lambda e, hh=hh: e.tensor_sub(
                wsl3(oSC, T, 4, T), wsl3(oAO + hh * 4 * T, T, 4, T), rrep(p, 0, 4)),
                reads=b_AO[hh * 4:hh * 4 + 4] + [b_rows[p][0]], writes=[b_SC],
                tag=f"A{i}.t3{hh}")
            S_("v", lambda e, hh=hh: e.tensor_mul(
                wsl3(oSC, T, 4, T), wsl3(oSC, T, 4, T), rrep(p, 1, 4)),
                reads=[b_SC, b_rows[p][1]], writes=[b_SC], tag=f"A{i}.t3r{hh}")
            S_("v", lambda e, hh=hh, xsl=xsl: e.tensor_add(
                xsl, wsl3(oSC, T, 4, T), xsl),
                reads=[b_SC, b_x1[i]], writes=[b_x1[i]], tag=f"A{i}.x1{hh}")

    w2_cnt = [0]

    def w2dma(g):
        sb = g % 3
        m = g % KC
        dst = wsl3(oW2 + sb * 4096, 128, M1, 128)
        src = bass.AP(f2s_h, m * 128 * M1 * 128,
                      [[M1 * 128, 128], [128, M1], [1, 128]])
        S_("q", lambda e: e.dma_start(out=dst, in_=src), reads=[b_f2s[m]],
           writes=[b_W2[sb]], tag=f"W2g{g}")

    def ensure_w2(g):
        while w2_cnt[0] <= g:
            w2dma(w2_cnt[0])
            w2_cnt[0] += 1

    def fc2A(j):
        pj = j % 2
        bkj = (4, 5) if pj == 0 else (6, 7)
        for mp in range(KC // 2):
            for half in range(2):
                m = 2 * mp + half
                gidx = j * KC + m
                ensure_w2(gidx)
                sb = gidx % 3
                mb = m % 4
                for k in range(M1):
                    S_("pe", lambda e, m=m, k=k, mb=mb, sb=sb: e.matmul(
                        pap(mb), wsl(oW2 + sb * 4096 + k * 128, 128),
                        wsl(oH + k * T, T),
                        start=(k == 0), stop=(k == M1 - 1)),
                        reads=[b_W2[sb], b_H[k // 8]], writes=[b_P[mb]],
                        tag=f"A{j}.f2.{m}.{k}")
                if j == 0:
                    cb = bkj[0]
                    for k in range(M1):
                        S_("pe", lambda e, k=k, sb=sb: e.matmul(
                            P[cb][:, 256:258],
                            wsl(oW2 + sb * 4096 + k * 128, 128),
                            vecb.ap()[:, 16 + k:49 + k:32],
                            start=(k == 0), stop=(k == M1 - 1)),
                            reads=[b_vecb, b_W2[sb]], writes=[b_P[cb]],
                            tag=f"c2f.{m}.{k}")
                    S_("v", lambda e, m=m: e.tensor_add(
                        col(cvec, 8 + m), P[cb][:, 256:257], col(vecf, 104 + m)),
                        reads=[b_P[cb], b_vecf], writes=[b_cvec],
                        tag=f"c2f.add{m}")
                    S_("v", lambda e, m=m: e.tensor_mul(
                        col(cvec, 8 + m), col(cvec, 8 + m), col(vecf, 24 + m)),
                        reads=[b_cvec, b_vecf], writes=[b_cvec], tag=f"c3f.mul{m}")
                    S_("v", lambda e, m=m: e.tensor_copy(
                        bass.AP(cvec, 16 + m, [[24, 128], [1, 1]]), P[cb][:, 257:258]),
                        reads=[b_P[cb]], writes=[b_cvec], tag=f"wgh.sc{m}")
                if gidx + 3 < NT * KC:
                    ensure_w2(gidx + 3)
            for half in range(2):
                m = 2 * mp + half
                S_("v", lambda e, m=m, mb=m % 4: e.scalar_tensor_tensor(
                    fsl(oE + (m % 2) * 256, T), rsl(pj, 3), col(cvec, 16 + m),
                    pap(mb), ALU.mult, ALU.subtract),
                    reads=[b_rows[pj][3], b_cvec, b_P[m % 4]], writes=[b_E],
                    tag=f"A{j}.f2v{m}")
            for half in range(2):
                m = 2 * mp + half
                S_("v", lambda e, m=m: e.tensor_mul(
                    fsl(oE + (2 + m % 2) * 256, T), fsl(oE + (m % 2) * 256, T),
                    rsl(pj, 4)),
                    reads=[b_E, b_rows[pj][4]], writes=[b_E], tag=f"A{j}.f2t{m}")
            for half in range(2):
                m = 2 * mp + half
                S_("s", lambda e, m=m: e.activation(
                    out=fsl(oE + (m % 2) * 256, T), in_=fsl(oE + (2 + m % 2) * 256, T),
                    func=AF.Identity, bias=col(cvec, 8 + m), scale=col(vecf, 24 + m)),
                    reads=[b_E, b_cvec, b_vecf], writes=[b_E], tag=f"A{j}.f2e{m}")
            for half in range(2):
                m = 2 * mp + half
                S_("v", lambda e, m=m: e.tensor_add(
                    wsl(oAO + m * T, T), fsl(oE + (m % 2) * 256, T),
                    wsl(oXA + pj * 2048 + m * T, T)),
                    reads=[b_E, b_XA[pj][m]], writes=[b_AO[m]], tag=f"A{j}.ao{m}")

    def bodyA(i):
        p = i % 2
        bk = (4, 5) if p == 0 else (6, 7)
        for m in range(KC):
            mb = m % 4
            for k in range(KC):
                S_("pe", lambda e, m=m, k=k, mb=mb: e.matmul(
                    pap(mb), wsl(oWp + k * 1024 + m * 128, 128),
                    wsl(oY + k * T, T),
                    start=(k == 0), stop=(k == KC - 1)),
                    reads=[b_Wp, b_Y], writes=[b_P[mb]], tag=f"A{i}.pj{m}.{k}")
            S_("v", lambda e, m=m, mb=mb: e.scalar_tensor_tensor(
                wsl(oXA + p * 2048 + m * T, T), pap(mb), col(vecf, 8 + m),
                wsl(oZ + m * T, T), ALU.mult, ALU.add),
                reads=[b_P[mb], b_vecf, b_Z], writes=[b_XA[p][m]], tag=f"A{i}.xa{m}")
        stats_sqv(a_src(oXA + p * 2048), b_XA[p], f"A{i}.l2.", oZ, b_Z)
        if i > 0:
            lnh_tail(i - 1)
        if i > 1:
            chainA_sheets(i - 2)
        if i > 0:
            fc2A(i - 1)
        stats_mms(a_src(oXA + p * 2048), b_XA[p], bk, f"A{i}.l2.", oZ, b_Z)
        ln_rows(p, 5, 2, 3, bk, tagp=f"A{i}.l2.")
        S_("v", lambda e: e.tensor_sub(fsl3(oT, T, KC, T),
                                       wsl3(oXA + p * 2048, T, KC, T), rrep(p, 5, KC)),
           reads=b_XA[p] + [b_rows[p][5]], writes=[b_T], tag=f"A{i}.t2")
        if i > 0:
            stats_sqv(a_src(oAO), b_AO, f"A{i}.l3.", oZ, b_Z)
            p1 = (i - 1) % 2
            b1 = 5 if p1 == 0 else 7
            stats_mms(a_src(oAO), b_AO, ((b1, 0), (b1, 256)), f"A{i - 1}.l3.",
                      oZ, b_Z)
        S_("v", lambda e: e.tensor_mul(wsl3(oZ, T, KC, T), fsl3(oT, T, KC, T),
                                       rrep(p, 2, KC)),
           reads=[b_T, b_rows[p][2]], writes=[b_Z], tag=f"A{i}.z")
        if i > 0 and i + 1 < NT:
            x0dma(i + 1)
        def sqreg(g):
            if g % 2 == 0:
                return oY, b_Y
            return oX0 + p * 2048, b_X0[p]

        def lnh_sum(g):
            c0 = g * 8
            for c in range(c0, c0 + 8):
                S_("pe", lambda e, c=c: e.matmul(
                    pap(bk[0]), avg4.ap(), wsl(oH + c * T, T), start=(c == 0),
                    stop=(c == M1 - 1)),
                    reads=[b_H[c // 8], b_avg4], writes=[b_P[bk[0]]],
                    tag=f"A{i}.lh.s{c}")

        def lnh_sqv(g, hg=True):
            off, buf = sqreg(g)
            c0 = g * 8
            for b0 in range(2):
                cc0 = c0 + b0 * 4
                S_("v", lambda e, cc0=cc0, off=off, b0=b0: e.tensor_mul(
                    wsl3(off + b0 * 4 * T, T, 4, T), wsl3(oH + cc0 * T, T, 4, T),
                    wsl3(oH + cc0 * T, T, 4, T)),
                    reads=[b_H[cc0 // 8]], writes=[buf], tag=f"A{i}.hsq{cc0}")
            if hg:
                lnh_hg(g)

        def lnh_hg(g):
            for c in range(g * 8, g * 8 + 8):
                S_("v", lambda e, c=c: e.tensor_scalar_mul(
                    wsl(oH + c * T, T), wsl(oH + c * T, T), col(vecf, 32 + c)),
                    reads=[b_H[c // 8], b_vecf], writes=[b_H[c // 8]],
                    tag=f"A{i}.hg{c}")

        def lnh_sqmm(g):
            off, buf = sqreg(g)
            for c in range(8):
                cc = g * 8 + c
                S_("pe", lambda e, c=c, cc=cc, off=off: e.matmul(
                    pap(bk[1]), avg4.ap(), wsl(off + c * T, T),
                    start=(cc == 0), stop=(cc == M1 - 1)),
                    reads=[buf, b_avg4], writes=[b_P[bk[1]]],
                    tag=f"A{i}.lh.q{cc}")

        for g in range(4):
            for m in range(g * 8, g * 8 + 8):
                mb = m % 4
                for k in range(KC):
                    S_("pe", lambda e, m=m, k=k, mb=mb: e.matmul(
                        pap(mb), wsl(oW1 + k * 4096 + m * 128, 128),
                        wsl(oZ + k * T, T),
                        start=(k == 0), stop=(k == KC - 1)),
                        reads=[b_W1[k][m // 8], b_Z], writes=[b_P[mb]],
                        tag=f"A{i}.f1.{m}.{k}")
                S_("s", lambda e, m=m, mb=mb: e.activation(
                    out=wsl(oH + m * T, T), in_=pap(mb), func=AF.Gelu,
                    bias=col(c1f, m), scale=1.0),
                    reads=[b_P[mb], b_c1f[m // 8]], writes=[b_H[m // 8]],
                    tag=f"A{i}.gel{m}")
            if g > 0:
                lnh_sum(g - 1)
                lnh_sqv(g - 1)
            if g > 1:
                lnh_sqmm(g - 2)
            if g == 0:
                ensure_w2(i * KC + 2)
            if g == 2 and i + 1 < NT:
                headA_stats(i + 1)
        lnh_sqmm(2)
        prewarm_ln(f"A{i}.warm")
        if i + 1 < NT:
            headA_rows(i + 1)
        if i > 0:
            chainA_lnrows(i - 1)

    def headB_sq(i):
        for b0 in range(2):
            S_("v", lambda e, b0=b0, i=i: e.tensor_mul(
                wsl3(oF + b0 * 4 * T, T, 4, T),
                bass.AP(x1r, b0 * 4 * TOK + i * T, [[KC * TOK, 128], [TOK, 4], [1, T]]),
                bass.AP(x1r, b0 * 4 * TOK + i * T, [[KC * TOK, 128], [TOK, 4], [1, T]])),
                reads=[b_x1[i]], writes=[b_F[b0 * 2], b_F[b0 * 2 + 1]],
                tag=f"B{i}.z2sq{b0}")

    def headB(i):
        p = i % 2
        bs, bq = 6, 7
        for c in range(KC):
            S_("pe", lambda e, c=c: e.matmul(
                pap(bs), avg1.ap(), x1c(c, i), start=(c == 0), stop=(c == KC - 1)),
                reads=[b_x1[i], b_avg1], writes=[b_P[bs]], tag=f"B{i}.z2s{c}")
        for cc in range(KC):
            S_("pe", lambda e, cc=cc: e.matmul(
                pap(bq), avg1.ap(), wsl(oF + cc * T, T), start=(cc == 0),
                stop=(cc == KC - 1)),
                reads=[b_F[cc // 2], b_avg1], writes=[b_P[bq]],
                tag=f"B{i}.z2q{cc}")
        ln_rows(p, 0, 1, 3, (bs, bq), tagp=f"B{i}.z2.")
        S_("v", lambda e: e.tensor_sub(fsl3(oT, T, KC, T), x1sl(i), rrep(p, 0, KC)),
           reads=[b_x1[i], b_rows[p][0]], writes=[b_T], tag=f"B{i}.tz")
        S_("v", lambda e: e.tensor_mul(wsl3(oZ2 + p * 2048, T, KC, T),
                                       fsl3(oT, T, KC, T), rrep(p, 1, KC)),
           reads=[b_T, b_rows[p][1]], writes=b_Z2[p], tag=f"B{i}.z2n")

    wd_cnt = [0]

    def wddma(g):
        sb = g % 4
        m = g % KC
        dst = wsl3(oWd + sb * 3072, 128, M2, 128)
        src = bass.AP(ds_h, m * 128 * M2 * 128,
                      [[M2 * 128, 128], [128, M2], [1, 128]])
        S_("q", lambda e: e.dma_start(out=dst, in_=src), reads=[b_ds[m]],
           writes=[b_Wd[sb]], tag=f"Wdg{g}")

    def ensure_wd(g):
        while wd_cnt[0] <= g:
            wddma(wd_cnt[0])
            wd_cnt[0] += 1

    def bodyB(i):
        p = i % 2

        def lf_pair(mp):
            for half in range(2):
                c = 2 * mp + half
                S_("pe", lambda e, c=c: e.matmul(
                    pap(3), avg1.ap(), wsl(oF + c * T, T), start=(c == 0),
                    stop=(c == KC - 1)),
                    reads=[b_F[c // 2], b_avg1], writes=[b_P[3]],
                    tag=f"B{i}.lf.s{c}")
            for half in range(2):
                c = 2 * mp + half
                S_("pe", lambda e, c=c: e.matmul(
                    pap(4), avg1.ap(), wsl(oZ2 + p * 2048 + c * T, T),
                    start=(c == 0), stop=(c == KC - 1)),
                    reads=[b_Z2[p][c], b_avg1], writes=[b_P[4]],
                    tag=f"B{i}.lf.q{c}")

        if i + 1 < NT:
            headB_sq(i + 1)
        for mp in range(M2 // 2):
            m0 = 2 * mp
            gb = mp % 3
            ub = 3 + mp % 3
            for half in range(2):
                m = m0 + half
                for k in range(KC):
                    S_("pe", lambda e, m=m, k=k, gb=gb, half=half: e.matmul(
                        pap(gb, T, half * 256),
                        wsl(oWg + k * 3072 + m * 128, 128),
                        wsl(oZ2 + p * 2048 + k * T, T), start=(k == 0),
                        stop=(k == KC - 1)),
                        reads=[b_Wg[k], b_Z2[p][k]], writes=[b_P[gb]],
                        tag=f"B{i}.g{m}.{k}")
            for half in range(2):
                m = m0 + half
                for k in range(KC):
                    S_("pe", lambda e, m=m, k=k, ub=ub, half=half: e.matmul(
                        pap(ub, T, half * 256),
                        wsl(oWu + k * 3072 + m * 128, 128),
                        wsl(oZ2 + p * 2048 + k * T, T), start=(k == 0),
                        stop=(k == KC - 1)),
                        reads=[b_Wu[k], b_Z2[p][k]], writes=[b_P[ub]],
                        tag=f"B{i}.u{m}.{k}")
            sg = mp % 2
            S_("s", lambda e, gb=gb, sg=sg: e.activation(
                out=wsl(oSG + sg * 512, 512), in_=pap(gb, 512), func=AF.Silu,
                scale=S127),
                reads=[b_P[gb]], writes=[b_SG[sg]], tag=f"B{i}.si{mp}")
            S_("v", lambda e, m0=m0, ub=ub, sg=sg: e.scalar_tensor_tensor(
                wsl(oHF + m0 * T, 512), pap(ub, 512), S127, wsl(oSG + sg * 512, 512),
                ALU.mult, ALU.mult),
                reads=[b_P[ub], b_SG[sg]], writes=[b_HF[mp]], tag=f"B{i}.hf{mp}")
        prewarm_ln(f"B{i}.warm")
        if i + 1 < NT:
            headB(i + 1)
        ensure_wd(i * KC + 3)
        for mp in range(KC // 2):
            m0 = 2 * mp
            dbk = mp % 3
            for half in range(2):
                m = m0 + half
                gidx = i * KC + m
                ensure_wd(gidx)
                sb = gidx % 4
                for k in range(M2):
                    S_("pe", lambda e, m=m, k=k, dbk=dbk, sb=sb, half=half: e.matmul(
                        pap(dbk, T, half * 256),
                        wsl(oWd + sb * 3072 + k * 128, 128), wsl(oHF + k * T, T),
                        start=(k == 0), stop=(k == M2 - 1)),
                        reads=[b_Wd[sb], b_HF[k // 2]], writes=[b_P[dbk]],
                        tag=f"B{i}.d{m}.{k}")
                if gidx + 3 < NT * KC:
                    ensure_wd(gidx + 3)
            S_("s", lambda e, dbk=dbk, m0=m0: e.activation(
                out=wsl(oF + m0 * T, 512), in_=pap(dbk, 512), func=AF.Identity),
                reads=[b_P[dbk]], writes=[b_F[mp]], tag=f"B{i}.fe{mp}")
            S_("v", lambda e, m0=m0: e.tensor_mul(
                wsl(oZ2 + p * 2048 + m0 * T, 512), wsl(oF + m0 * T, 512),
                wsl(oF + m0 * T, 512)),
                reads=[b_F[mp]], writes=[b_Z2[p][2 * mp], b_Z2[p][2 * mp + 1]],
                tag=f"B{i}.fsq{mp}")
            if mp > 1:
                lf_pair(mp - 2)
        lf_pair(KC // 2 - 2)
        lf_pair(KC // 2 - 1)
        ln_rows(p, 5, 2, 3, (3, 4), eps_col=1, tagp=f"B{i}.lf.")
        S_("v", lambda e: e.tensor_sub(fsl3(oT, T, KC, T), wsl3(oF, T, KC, T),
                                       rrep(p, 5, KC)),
           reads=b_F + [b_rows[p][5]], writes=[b_T], tag=f"B{i}.tf")
        S_("v", lambda e: e.tensor_mul(fsl3(oT, T, KC, T), fsl3(oT, T, KC, T),
                                       rrep(p, 2, KC)),
           reads=[b_T, b_rows[p][2]], writes=[b_T], tag=f"B{i}.tfr")
        for hh in range(2):
            S_("v", lambda e, hh=hh: e.tensor_add(
                fsl3(oE, T, 4, T), fsl3(oT + hh * 4 * T, T, 4, T),
                bass.AP(x1r, (hh * 4) * TOK + i * T,
                        [[KC * TOK, 128], [TOK, 4], [1, T]])),
                reads=[b_T, b_x1[i]], writes=[b_E], tag=f"B{i}.out{hh}")
            dst = bass.AP(y_h, (hh * 4 * 128) * TOK + i * T,
                          [[TOK, 128], [128 * TOK, 4], [1, T]])
            S_("q", lambda e, hh=hh, dst=dst: e.dma_start(out=dst,
                                                          in_=fsl3(oE, T, 4, T)),
               reads=[b_E], tag=f"B{i}.st{hh}")

    headA_stats(0)
    startup_consts()
    emit_staging()
    headA_rows(0)
    for i in range(NT):
        bodyA(i)
    for k in range(KC):
        S_("g", lambda e, k=k: e.dma_start(
            out=wsl(oWg + k * 3072, 3072), in_=g_h[k * 128:(k + 1) * 128, :]),
            writes=[b_Wg[k]], tag=f"Wgk{k}")
    chainA_sheets(NT - 2)
    lnh_tail(NT - 1)
    fc2A(NT - 1)
    for k in range(KC):
        S_("g", lambda e, k=k: e.dma_start(
            out=wsl(oWu + k * 3072, 3072), in_=u_h[k * 128:(k + 1) * 128, :]),
            writes=[b_Wu[k]], tag=f"Wuk{k}")
    stats_sqv(a_src(oAO), b_AO, f"A{NT - 1}.l3.", oZ, b_Z)
    stats_mms(a_src(oAO), b_AO, ((7, 0), (7, 256)), f"A{NT - 1}.l3.", oZ, b_Z)
    chainA_lnrows(NT - 1)
    chainA_sheets(NT - 1)
    ensure_wd(3)
    headB_sq(0)
    headB(0)
    for i in range(NT):
        bodyB(i)

    last_store = [o for o in sch.ops["q"]
                  if o.tag.startswith("B") and ".st" in o.tag][-1]
    last_store.needs_inc = True
    fin = _Op("q", None, False, "final", len(sch.ops["q"]))
    fin.deps = [last_store]
    sch.ops["q"].append(fin)

    sch.finalize()

    with nc.Block() as block:
        @block.tensor
        def _(eng):
            sch.emit(eng, "pe")

        @block.vector
        def _(eng):
            sch.emit(eng, "v")

        @block.scalar
        def _(eng):
            sch.emit(eng, "s")

        @block.gpsimd
        def _(eng):
            sch.emit(eng, "g")

        @block.sync
        def _(eng):
            sch.emit(eng, "q")
    return nc


_NC_CACHE = {}


def _get_nc():
    if "nc" not in _NC_CACHE:
        _NC_CACHE["nc"] = build_nc()
    return _NC_CACHE["nc"]


def _pack_cols(vs):
    outs = []
    for v in vs:
        outs.append(np.ascontiguousarray(np.asarray(v, np.float32).reshape(-1, 128).T))
    return np.ascontiguousarray(np.concatenate(outs, axis=1))


def kernel(**inp):
    X = np.asarray(inp["L0_attn_x"], np.float32).reshape(16384, 1024)
    pwT = np.ascontiguousarray(np.asarray(inp["L0_attn_proj_w"], np.float32).T)
    f1T = np.ascontiguousarray(np.asarray(inp["L0_attn_fc1_w"], np.float32).T)
    f2T = np.asarray(inp["L0_attn_fc2_w"], np.float32).T
    f2m = np.ascontiguousarray(
        f2T.reshape(M1, 128, KC, 128).transpose(2, 1, 0, 3))
    gT = np.ascontiguousarray(np.asarray(inp["L0_ffn_w_gate"]).T.astype(np.int8))
    uT = np.ascontiguousarray(np.asarray(inp["L0_ffn_w_up"]).T.astype(np.int8))
    dT = np.asarray(inp["L0_ffn_w_down"]).T.astype(np.int8)
    dm = np.ascontiguousarray(dT.reshape(M2, 128, KC, 128).transpose(2, 1, 0, 3))
    vecf = _pack_cols([inp["L0_attn_ln1_g"], inp["L0_attn_gamma1"],
                       inp["L0_attn_ln2_g"], inp["L0_attn_gamma2"],
                       inp["L0_attn_lnh_g"], inp["L0_attn_proj_b"],
                       inp["L0_attn_fc1_b"], inp["L0_attn_fc2_b"]])
    vecb = _pack_cols([inp["L0_attn_ln1_b"], inp["L0_attn_ln2_b"],
                       inp["L0_attn_lnh_b"], inp["L0_attn_lnh_g"]])
    nc = _get_nc()
    in_maps = []
    for c in range(8):
        xs = np.ascontiguousarray(X[c * TOK:(c + 1) * TOK].T)
        in_maps.append({"x": xs, "pwT": pwT, "f1T": f1T, "f2m": f2m,
                        "gT": gT, "uT": uT, "dm": dm, "vecf": vecf, "vecb": vecb})
    res = run_bass_kernel_spmd(nc, in_maps, core_ids=list(range(8)))
    out = np.concatenate([np.asarray(r["y"], np.float32).T for r in res.results],
                         axis=0)
    return np.ascontiguousarray(out.reshape(4, 4096, 1024))
